# version 03
# baseline (speedup 1.0000x reference)
import math
import numpy as np
import ml_dtypes
import concourse.bass as bass
import concourse.mybir as mybir
from concourse.bass_utils import run_bass_kernel_spmd

F32 = mybir.dt.float32
BF16 = mybir.dt.bfloat16
I32 = mybir.dt.int32
AF = mybir.ActivationFunctionType
OP = mybir.AluOpType

ENGINES = ("sync", "gpsimd", "vector", "scalar", "tensor")
SCALER = 1.0 / math.sqrt(192.0)
EPS = 1e-5
D = 4096
TC = 2048
HALF = 1024
NCH = 32


class Rec:

    def __init__(self, nc):
        self.nc = nc
        self.steps = {e: [] for e in ENGINES}
        self.cnt = {}
        self._guards = []

    def sem(self, name):
        g = self.nc.semaphore(name)
        h = g.__enter__()
        self._guards.append(g)
        self.cnt[h] = 0
        return h

    def em(self, eng, fn):
        self.steps[eng].append(fn)

    def w(self, eng, sem, val=None):
        v = self.cnt[sem] if val is None else val
        self.em(eng, lambda e, s=sem, v=v: e.wait_ge(s, v))

    def dma(self, queue, out, in_, sem, transpose=False, slow=False):
        self.cnt[sem] += 16

        def fn(e, out=out, in_=in_, t=transpose, s=sem, sl=slow):
            kw = {}
            if sl:
                kw["allow_slow_non_contiguous"] = True
            if t:
                e.dma_start(out=out, in_=in_, transpose=True, **kw).then_inc(s, 16)
            else:
                e.dma_start(out=out, in_=in_, **kw).then_inc(s, 16)

        self.em(queue, fn)
        return self.cnt[sem]

    def op(self, eng, builder, sem=None, inc=1):
        if sem is not None:
            self.cnt[sem] += inc

        def fn(e, b=builder, s=sem, i=inc):
            ins = b(e)
            if s is not None:
                ins.then_inc(s, i)

        self.em(eng, fn)
        return self.cnt[sem] if sem is not None else None


class T:
    pass


def _alloc_group(nc, specs):
    out = {}
    guards = []
    for attr, name, shape, dtype in specs:
        g = nc.sbuf_tensor(name, shape, dtype)
        h = g.__enter__()
        guards.append(g)
        out[attr] = h
    return out, guards


def _free_group(guards):
    for g in reversed(guards):
        g.__exit__(None, None, None)


def barrier(r, t):
    t.stage += 1
    r.w("sync", t.qS)
    r.w("scalar", t.qA)
    r.w("gpsimd", t.qG)
    for e in ENGINES:
        r.op(e, lambda en: en.nop(), sem=t.bar, inc=1)
    v = 5 * t.stage
    assert r.cnt[t.bar] == v
    for e in ENGINES:
        r.w(e, t.bar, v)


def p_init(r, t):
    nc = r.nc
    r.op("gpsimd", lambda e: e.memset(t.ceps[:, :], EPS))
    r.op("gpsimd", lambda e: e.memset(t.idb[:, :], 0.0))
    r.op("gpsimd", lambda e: e.affine_select(
        out=t.idb[:, :], in_=t.idb[:, :], compare_op=OP.not_equal, fill=1.0,
        base=0, pattern=[[-1, 128]], channel_multiplier=1))
    r.op("gpsimd", lambda e: e.memset(t.idf[:, :], 0.0))
    r.op("gpsimd", lambda e: e.affine_select(
        out=t.idf[:, :], in_=t.idf[:, :], compare_op=OP.not_equal, fill=1.0,
        base=0, pattern=[[-1, 128]], channel_multiplier=1))
    r.op("gpsimd", lambda e: e.memset(t.ones1[:, :], 1.0))
    r.op("gpsimd", lambda e: e.memset(t.onesK[:, :], 1.0))
    r.op("gpsimd", lambda e: e.memset(t.bd[:, :], 0.0))
    for g in range(32):
        r.op("gpsimd", lambda e, g=g: e.memset(t.bd[:, g * 32 + g: g * 32 + g + 1], 1.0))
    r.op("gpsimd", lambda e: e.engine_nop(), sem=t.sP)
    for nm in ("bdq", "buq", "bdkv", "buk", "brq", "bout", "fb"):
        r.dma("gpsimd", getattr(t, nm)[:, :], getattr(t, nm + "_d"), t.qG)
    r.dma("gpsimd", t.buvb[:, :], t.buv_d, t.qG)
    r.dma("gpsimd", t.brk[:, :], t.brk_d, t.qG)
    r.dma("gpsimd", t.sa_f[:, 0:1024].rearrange("p (c g) -> p c g", c=32),
          t.fsc_d.rearrange("(c p) g -> p c g", p=128), t.qG)
    r.dma("gpsimd", t.sa_f[:, 1024:2048].rearrange("p (c g) -> p c g", c=32),
          t.fzp_d.rearrange("(c p) g -> p c g", p=128), t.qG)
    r.w("vector", t.qG)
    r.w("gpsimd", t.qG)
    r.op("vector", lambda e: e.tensor_copy(t.s_all[:, :], t.sa_f[:, 0:1024]))
    r.op("vector", lambda e: e.tensor_tensor(
        t.zsb[:, :], t.sa_f[:, 0:1024], t.sa_f[:, 1024:2048], OP.mult), sem=t.sV)
    r.op("vector", lambda e: e.tensor_scalar(
        t.buqs[:, :], t.buq[:, :], SCALER, None, OP.mult))
    r.op("vector", lambda e: e.tensor_scalar(
        t.brqs[:, :], t.brq[:, :], SCALER, None, OP.mult))
    r.op("vector", lambda e: e.tensor_scalar(
        t.fbh[:, :], t.fb[:, :], 0.5, None, OP.mult))
    r.w("tensor", t.sV)
    r.w("tensor", t.sP)
    for o in range(32):
        pb = t.PB2 if (o % 2 == 0) else t.PB3
        dst = pb[0:32, 0:64].bitcast(BF16)
        r.op("tensor", lambda e, o=o, dst=dst: e.transpose(
            dst, t.zsb[:, o * 32:(o + 1) * 32], t.idb[:, :]), sem=t.sT)
        r.w("vector", t.sT)
        r.op("vector", lambda e, o=o, src=dst: e.tensor_copy(
            t.zst[0:32, o * 128:(o + 1) * 128], src), sem=t.sV)
        r.w("tensor", t.sV)


def p0_half(r, t, half):
    v_done, s_done, t_done = [], [], []
    for tc in range(8):
        g = half * 8 + tc
        pb = tc % 2
        xnat = t.wstr[pb]
        u1n = t.wbf[pb]
        st = t.stats[pb]
        if tc >= 2:
            r.w("sync", t.sV, v_done[tc - 2])
            r.w("sync", t.sS, s_done[tc - 2])
            r.w("sync", t.sT, t_done[tc - 2])
        r.dma("sync", xnat[:, :], t.x0_d[g * 128:(g + 1) * 128, :], t.qS)
        r.w("vector", t.qS)
        r.op("vector", lambda e, x=xnat, st=st: e.reduce_sum(
            st[:, 0:1], x[:, :], axis=mybir.AxisListType.X), sem=t.sV)
        r.w("vector", t.sV)
        r.op("vector", lambda e, st=st: e.tensor_scalar(
            st[:, 1:2], st[:, 0:1], 1.0 / D, None, OP.mult), sem=t.sV)
        r.w("vector", t.sV)
        r.w("scalar", t.qS)
        r.op("scalar", lambda e, x=xnat, st=st: e.activation(
            t.dump[:, 0:2048], x[:, 0:2048], AF.Square,
            accum_out=st[:, 2:3]))
        r.op("scalar", lambda e, x=xnat, st=st: e.activation(
            t.dump[:, 0:2048], x[:, 2048:4096], AF.Square,
            accum_out=st[:, 3:4]), sem=t.sS)
        r.w("vector", t.sS)
        r.op("vector", lambda e, st=st: e.tensor_tensor(
            st[:, 4:5], st[:, 2:3], st[:, 3:4], OP.add), sem=t.sV)
        r.w("vector", t.sV)
        r.op("vector", lambda e, st=st: e.tensor_scalar(
            st[:, 4:5], st[:, 4:5], 1.0 / D, None, OP.mult), sem=t.sV)
        r.w("vector", t.sV)
        r.op("vector", lambda e, st=st: e.tensor_tensor(
            st[:, 5:6], st[:, 1:2], st[:, 1:2], OP.mult), sem=t.sV)
        r.w("vector", t.sV)
        r.op("vector", lambda e, st=st: e.tensor_tensor(
            st[:, 4:5], st[:, 4:5], st[:, 5:6], OP.subtract), sem=t.sV)
        r.w("scalar", t.sV)
        r.op("scalar", lambda e, st=st: e.activation(
            st[:, 6:7], st[:, 4:5], AF.Ln, bias=EPS), sem=t.sS)
        r.w("scalar", t.sS)
        r.op("scalar", lambda e, st=st: e.activation(
            st[:, 7:8], st[:, 6:7], AF.Exp, scale=-0.5), sem=t.sS)
        r.w("vector", t.sS)
        r.op("vector", lambda e, x=xnat, u=u1n, st=st: e.tensor_scalar(
            u[:, :], x[:, :], st[:, 1:2], st[:, 7:8], OP.subtract, OP.mult), sem=t.sV)
        uv = r.cnt[t.sV]
        r.w("tensor", t.qS)
        for grp in range(4):
            pbk = t.PB2 if grp % 2 == 0 else t.PB3
            for j in range(8):
                d = grp * 8 + j
                r.op("tensor", lambda e, x=xnat, d=d, pbk=pbk, j=j: e.transpose(
                    pbk[:, j * 128:(j + 1) * 128], x[:, d * 128:(d + 1) * 128],
                    t.idf[:, :]), sem=(t.sT if j == 7 else None))
            r.w("scalar", t.sT)
            r.w("scalar", t.qS)
            r.op("scalar", lambda e, pbk=pbk, pb=pb: e.activation(
                t.t1[pb][:, 0:1024], pbk[:, :], AF.Identity), sem=t.sS)
            r.w("sync", t.sS)
            r.dma("sync", t.xt0_d.rearrange("(c p) q -> p c q", p=128)
                  [:, grp * 8:(grp + 1) * 8, g * 128:(g + 1) * 128],
                  t.t1[pb][:, 0:1024].rearrange("p (c q) -> p c q", c=8), t.qS)
            r.w("tensor", t.sS)
        r.w("tensor", t.sV, uv)
        for grp in range(2):
            pbk = t.PB2 if grp == 0 else t.PB3
            dstall = pbk[:, 0:1024].bitcast(BF16)
            for j in range(16):
                d = grp * 16 + j
                dst = pbk[:, j * 64:(j + 1) * 64].bitcast(BF16)
                r.op("tensor", lambda e, u=u1n, d=d, dst=dst: e.transpose(
                    dst, u[:, d * 128:(d + 1) * 128], t.idb[:, :]),
                    sem=(t.sT if j == 15 else None))
            r.w("vector", t.sT)
            dd = grp * 16
            out = t.uTh[:, :].rearrange("p (c q) -> p c q", c=32)[
                :, dd:dd + 16, tc * 128:(tc + 1) * 128]
            r.op("vector", lambda e, out=out, src=dstall: e.tensor_copy(
                out, src.rearrange("p (c q) -> p c q", c=16)), sem=t.sV)
            r.w("tensor", t.sV)
        v_done.append(r.cnt[t.sV])
        s_done.append(r.cnt[t.sS])
        t_done.append(r.cnt[t.sT])


def usum_half(r, t, half):
    for qt in range(2):
        ps = t.psQ[qt][0:32, :]
        for i in range(32):
            r.op("tensor", lambda e, i=i, qt=qt, ps=ps: e.matmul(
                ps, t.bd[:, i * 32:(i + 1) * 32],
                t.uTh[:, i * 1024 + qt * 512: i * 1024 + (qt + 1) * 512],
                start=(i == 0), stop=(i == 31)),
                sem=(t.sT if i == 31 else None))
        r.w("scalar", t.sT)
        r.op("scalar", lambda e, qt=qt, ps=ps, half=half: e.activation(
            t.usum[0:32, half * 1024 + qt * 512: half * 1024 + (qt + 1) * 512],
            ps[0:32, :], AF.Identity, scale=-1.0), sem=t.sS)
        r.w("tensor", t.sS)


def ffn_half(r, t, half, app):
    build_w = (app == 1 and half == 0)
    v_done, s_done, t_done = [], [], []
    w_done = []
    for o in range(32):
        pb = o % 2
        wt = t.wtsb[pb]
        if o >= 2:
            r.w("sync", t.sT, t_done[o - 2])
            r.w("sync", t.sS, s_done[o - 2])
            r.w("sync", t.sV, v_done[o - 2])
            r.w("scalar", t.sT, t_done[o - 2])
            r.w("gpsimd", t.sV, v_done[o - 2])
        if build_w:
            wq = t.wbf[pb]
            r.dma("gpsimd", wq[:, :], t.fq_d[o * 128:(o + 1) * 128, :], t.qG)
            r.w("vector", t.qG)
            r.op("vector", lambda e, wq=wq, o=o: e.tensor_tensor(
                wq[:, :].rearrange("p (g m) -> p g m", g=32), wq[:, :].rearrange("p (g m) -> p g m", g=32),
                t.s_all[:, o * 32:(o + 1) * 32].unsqueeze(2).broadcast_to((128, 32, 128)),
                OP.mult), sem=t.sV)
            r.w("tensor", t.sV)
            for grp in range(2):
                pbk = t.PB2 if grp == 0 else t.PB3
                for j in range(16):
                    i = grp * 16 + j
                    dst = pbk[:, j * 64:(j + 1) * 64].bitcast(BF16)
                    r.op("tensor", lambda e, wq=wq, i=i, dst=dst: e.transpose(
                        dst, wq[:, i * 128:(i + 1) * 128], t.idb[:, :]),
                        sem=(t.sT if j == 15 else None))
                r.w("vector", t.sT)
                r.op("vector", lambda e, wt=wt, grp=grp, pbk=pbk: e.tensor_copy(
                    wt[:, grp * 2048:(grp + 1) * 2048], pbk[:, 0:1024].bitcast(BF16)),
                    sem=t.sV)
                r.w("tensor", t.sV)
            r.w("scalar", t.sV)
            r.dma("scalar", t.wtb_d[o].rearrange("i p m -> p i m"),
                  wt[:, :].rearrange("p (i m) -> p i m", i=32), t.qA)
        else:
            r.dma("scalar", wt[:, :].rearrange("p (i m) -> p i m", i=32),
                  t.wtb_d[o].rearrange("i p m -> p i m"), t.qA)
            r.w("tensor", t.qA)
        for i in range(32):
            for qt in range(2):
                ps = t.psQ[pb * 2 + qt]
                r.op("tensor", lambda e, i=i, qt=qt, ps=ps, wt=wt: e.matmul(
                    ps, wt[:, i * 128:(i + 1) * 128],
                    t.uTh[:, i * 1024 + qt * 512: i * 1024 + (qt + 1) * 512],
                    start=(i == 0), stop=False))
        for qt in range(2):
            ps = t.psQ[pb * 2 + qt]
            r.op("tensor", lambda e, qt=qt, ps=ps, o=o, half=half: e.matmul(
                ps, t.zst[0:32, o * 128:(o + 1) * 128],
                t.usum[0:32, half * 1024 + qt * 512: half * 1024 + (qt + 1) * 512],
                start=False, stop=True), sem=(t.sT if qt == 1 else None))
        t_done.append(r.cnt[t.sT])
        res_src = t.xt0_d
        r.dma("sync", t.xres[pb][:, 0:1024],
              res_src.rearrange("(c p) q -> p c q", p=128)
              [:, o:o + 1, half * 1024:(half + 1) * 1024], t.qS)
        r.w("scalar", t.sT)
        if o >= 2:
            r.w("scalar", t.qS if app == 1 else t.qOut, w_done[o - 2])
            r.w("scalar", t.sV, v_done[o - 2])
        for qt in range(2):
            ps = t.psQ[pb * 2 + qt]
            r.op("scalar", lambda e, qt=qt, ps=ps, o=o, pb=pb: e.activation(
                t.t1[pb][:, qt * 512:(qt + 1) * 512], ps, AF.Identity,
                scale=0.5, bias=t.fbh[:, o:o + 1]), sem=(t.sS if qt == 1 else None))
        r.w("vector", t.sS)
        r.w("vector", t.qS)
        r.op("vector", lambda e, pb=pb: e.tensor_tensor(
            t.t1[pb][:, 0:1024], t.t1[pb][:, 0:1024], t.xres[pb][:, 0:1024],
            OP.add), sem=t.sV)
        r.w("sync", t.sV)
        if app == 1:
            r.dma("sync", t.xt1_d.rearrange("(c p) q -> p c q", p=128)
                  [:, o:o + 1, half * 1024:(half + 1) * 1024],
                  t.t1[pb][:, 0:1024].unsqueeze(1), t.qS)
            w_done.append(r.cnt[t.qS])
            r.op("vector", lambda e, pb=pb, half=half: e.tensor_tensor(
                t.accS[:, half * 1024:(half + 1) * 1024],
                t.accS[:, half * 1024:(half + 1) * 1024],
                t.t1[pb][:, 0:1024], OP.add))
            r.w("scalar", t.sV)
            r.op("scalar", lambda e, pb=pb: e.activation(
                t.dump[:, pb * 1024:(pb + 1) * 1024], t.t1[pb][:, 0:1024],
                AF.Square), sem=t.sS)
            r.w("vector", t.sS)
            r.op("vector", lambda e, pb=pb, half=half: e.tensor_tensor(
                t.accQ[:, half * 1024:(half + 1) * 1024],
                t.accQ[:, half * 1024:(half + 1) * 1024],
                t.dump[:, pb * 1024:(pb + 1) * 1024], OP.add), sem=t.sV)
        else:
            r.w("tensor", t.sV)
            pbk = t.PB2 if pb == 0 else t.PB3
            for j in range(8):
                r.op("tensor", lambda e, pb=pb, j=j, pbk=pbk: e.transpose(
                    pbk[:, j * 128:(j + 1) * 128],
                    t.t1[pb][:, j * 128:(j + 1) * 128], t.idf[:, :]),
                    sem=(t.sT if j == 7 else None))
            t_done[-1] = r.cnt[t.sT]
            r.w("vector", t.sT)
            if o >= 2:
                r.w("vector", t.qOut, w_done[o - 2])
            r.op("vector", lambda e, pb=pb, pbk=pbk: e.tensor_copy(
                t.ystg[pb][:, 0:1024], pbk[:, :]), sem=t.sV)
            r.w("sync", t.sV)
            r.dma("sync", t.y_d.rearrange("(c p) q -> p c q", p=128)
                  [:, half * 8:(half + 1) * 8, o * 128:(o + 1) * 128],
                  t.ystg[pb][:, 0:1024].rearrange("p (c q) -> p c q", c=8),
                  t.qOut)
            w_done.append(r.cnt[t.qOut])
        v_done.append(r.cnt[t.sV])
        s_done.append(r.cnt[t.sS])


def ln_stats(r, t):
    r.op("vector", lambda e: e.tensor_copy(t.dump[:, 0:2048], t.accS[:, :]),
         sem=t.sV)
    r.w("tensor", t.sV)
    for qt in range(4):
        psm = t.psQ[qt % 2]
        r.op("tensor", lambda e, qt=qt, psm=psm: e.matmul(
            psm, t.onesK[:, :], t.dump[:, qt * 512:(qt + 1) * 512],
            start=True, stop=True), sem=t.sT)
        r.w("vector", t.sT)
        r.op("vector", lambda e, qt=qt, psm=psm: e.tensor_scalar(
            t.mb[:, qt * 512:(qt + 1) * 512], psm, 1.0 / D, None, OP.mult),
            sem=t.sV)
        r.w("tensor", t.sV)
    r.op("vector", lambda e: e.tensor_copy(t.dump[:, 0:2048], t.accQ[:, :]),
         sem=t.sV)
    r.w("tensor", t.sV)
    for qt in range(4):
        psv = t.psQ[2 + qt % 2]
        r.op("tensor", lambda e, qt=qt, psv=psv: e.matmul(
            psv, t.onesK[:, :], t.dump[:, qt * 512:(qt + 1) * 512],
            start=True, stop=True), sem=t.sT)
        r.w("vector", t.sT)
        r.op("vector", lambda e, qt=qt, psv=psv: e.tensor_scalar(
            t.rb[:, qt * 512:(qt + 1) * 512], psv, 1.0 / D, None, OP.mult))
        r.op("vector", lambda e, qt=qt: e.tensor_tensor(
            t.t1[0][:, 0:512], t.mb[:, qt * 512:(qt + 1) * 512],
            t.mb[:, qt * 512:(qt + 1) * 512], OP.mult))
        r.op("vector", lambda e, qt=qt: e.tensor_tensor(
            t.rb[:, qt * 512:(qt + 1) * 512], t.rb[:, qt * 512:(qt + 1) * 512],
            t.t1[0][:, 0:512], OP.subtract), sem=t.sV)
        r.w("scalar", t.sV)
        r.op("scalar", lambda e, qt=qt: e.activation(
            t.rb[:, qt * 512:(qt + 1) * 512], t.rb[:, qt * 512:(qt + 1) * 512],
            AF.Ln, bias=EPS))
        r.op("scalar", lambda e, qt=qt: e.activation(
            t.rb[:, qt * 512:(qt + 1) * 512], t.rb[:, qt * 512:(qt + 1) * 512],
            AF.Exp, scale=-0.5), sem=t.sS)
        r.w("tensor", t.sS)


def ln_apply_half(r, t, half, src_dram):
    for dch in range(32):
        pb = dch % 2
        if dch >= 2:
            r.w("sync", t.sV, t.lnv_done[dch - 2])
        r.dma("sync", t.xres[pb][:, 0:1024],
              src_dram.rearrange("(c p) q -> p c q", p=128)
              [:, dch:dch + 1, half * 1024:(half + 1) * 1024], t.qS)
        r.w("vector", t.qS)
        r.op("vector", lambda e, pb=pb, half=half: e.tensor_tensor(
            t.t1[pb][:, 0:1024], t.xres[pb][:, 0:1024],
            t.mb[:, half * 1024:(half + 1) * 1024], OP.subtract))
        r.op("vector", lambda e, pb=pb, dch=dch, half=half: e.tensor_tensor(
            t.uTh[:, dch * 1024:(dch + 1) * 1024], t.t1[pb][:, 0:1024],
            t.rb[:, half * 1024:(half + 1) * 1024], OP.mult), sem=t.sV)
        t.lnv_done[dch] = r.cnt[t.sV]


def proj(r, t, half, w_dram, n_in, n_out, out_kind, bias_ap=None, scale=1.0,
         dst=None, dst_dram=None, name=""):
    barrier(r, t)
    nin_ch = n_in // 128
    nout_ch = n_out // 128
    rhs_buf = t.uTh if name in ("cq", "ckv", "krope") else (
        t.cqT if name in ("qup", "qrope") else t.ckvT)
    pj_adone = [0] * nout_ch
    for oc in range(nout_ch):
        pb = oc % 2
        wst = t.wstr[0]
        wbf = t.wbf[pb]
        if oc >= 1:
            r.w("sync", t.sV, t.pj_vdone[oc - 1])
        r.dma("sync", wst[:, 0:nin_ch * 128].rearrange("p (c m) -> p c m", c=nin_ch),
              w_dram[:, oc * 128:(oc + 1) * 128].rearrange("(c p) m -> p c m", p=128),
              t.qS)
        r.w("vector", t.qS)
        if oc >= 2:
            r.w("vector", t.sT, t.pj_tdone[oc - 2])
        r.op("vector", lambda e, wst=wst, wbf=wbf, n=nin_ch * 128: e.tensor_copy(
            wbf[:, 0:n], wst[:, 0:n]), sem=t.sV)
        t.pj_vdone[oc] = r.cnt[t.sV]
        r.w("tensor", t.sV)
        if oc >= 2:
            r.w("tensor", t.sS, t.pj_sdone[oc - 2])
        for qt in range(2):
            ps = t.psQ[pb * 2 + qt]
            for ci in range(nin_ch):
                r.op("tensor", lambda e, ci=ci, qt=qt, ps=ps, wbf=wbf: e.matmul(
                    ps, wbf[:, ci * 128:(ci + 1) * 128],
                    rhs_buf[:, ci * 1024 + qt * 512: ci * 1024 + (qt + 1) * 512],
                    start=(ci == 0), stop=(ci == nin_ch - 1)),
                    sem=(t.sT if (ci == nin_ch - 1 and qt == 1) else None))
        t.pj_tdone[oc] = r.cnt[t.sT]
        r.w("scalar", t.sT)
        if out_kind == "dram" and oc >= 2:
            r.w("scalar", t.qA, pj_adone[oc - 2])
        for qt in range(2):
            ps = t.psQ[pb * 2 + qt]
            if out_kind == "sbuf":
                out_ap = dst[:, oc * 1024 + qt * 512: oc * 1024 + (qt + 1) * 512]
            else:
                out_ap = t.eb[pb][:, qt * 512:(qt + 1) * 512]
            if bias_ap is not None:
                r.op("scalar", lambda e, ps=ps, out_ap=out_ap, oc=oc: e.activation(
                    out_ap, ps, AF.Identity, scale=scale,
                    bias=bias_ap[:, oc:oc + 1]), sem=(t.sS if qt == 1 else None))
            else:
                r.op("scalar", lambda e, ps=ps, out_ap=out_ap: e.activation(
                    out_ap, ps, AF.Identity, scale=scale),
                    sem=(t.sS if qt == 1 else None))
        t.pj_sdone[oc] = r.cnt[t.sS]
        if out_kind == "dram":
            r.dma("scalar", dst_dram.rearrange("(c p) q -> p c q", p=128)
                  [:, oc:oc + 1, half * 1024:(half + 1) * 1024],
                  t.eb[pb][:, 0:1024].unsqueeze(1), t.qA)
            pj_adone[oc] = r.cnt[t.qA]


def p2_krope(r, t, half):
    barrier(r, t)
    r.dma("sync", t.wstr[0][:, 0:2048].rearrange("p (c m) -> p c m", c=32),
          t.wrk_d.rearrange("(c p) m -> p c m", p=128), t.qS)
    r.w("vector", t.qS)
    r.op("vector", lambda e: e.tensor_copy(t.wbf[0][:, 0:2048], t.wstr[0][:, 0:2048]),
         sem=t.sV)
    r.w("tensor", t.sV)
    for qt in range(2):
        ps = t.psQ[qt][0:64, :]
        for ci in range(32):
            r.op("tensor", lambda e, ci=ci, ps=ps, qt=qt: e.matmul(
                ps, t.wbf[0][:, ci * 64:(ci + 1) * 64],
                t.uTh[:, ci * 1024 + qt * 512: ci * 1024 + (qt + 1) * 512],
                start=(ci == 0), stop=(ci == 31)),
                sem=(t.sT if ci == 31 else None))
        r.w("scalar", t.sT)
        r.op("scalar", lambda e, qt=qt, ps=ps: e.activation(
            t.krs[0:64, qt * 512:(qt + 1) * 512], ps, AF.Identity,
            bias=t.brk[:, 0:1]), sem=t.sS)
        r.w("tensor", t.sS)
    r.w("scalar", t.sS)
    r.dma("scalar", t.kri_d[:, half * 1024:(half + 1) * 1024],
          t.krs[0:64, 0:1024], t.qA)


def p2_vup(r, t, half):
    barrier(r, t)
    va_done = [0] * 16
    vs_done = [0] * 16
    for dvb in range(8):
        pb = dvb % 2
        if dvb >= 1:
            r.w("sync", t.sV, t.pj_vdone[dvb - 1])
        r.dma("sync", t.wstr[0][:, 0:2048].rearrange("p (c m) -> p c m", c=4),
              t.wuv_d[:, dvb * 512:(dvb + 1) * 512].rearrange("(c p) m -> p c m", p=128),
              t.qS)
        r.w("vector", t.qS)
        if dvb >= 2:
            r.w("vector", t.sT, t.pj_tdone[dvb - 2])
        r.op("vector", lambda e, pb=pb: e.tensor_copy(
            t.wbf[pb][:, 0:2048], t.wstr[0][:, 0:2048]), sem=t.sV)
        t.pj_vdone[dvb] = r.cnt[t.sV]
        r.w("tensor", t.sV)
        if dvb >= 2:
            r.w("tensor", t.sS, t.pj_sdone[dvb - 2])
        for tk in range(8):
            ps = t.psQ[pb * 2 + tk % 2]
            if tk >= 2:
                r.w("tensor", t.sS, vs_done[tk - 2])
            for ci in range(4):
                r.op("tensor", lambda e, ci=ci, tk=tk, ps=ps, pb=pb: e.matmul(
                    ps, t.ckvT[:, ci * 1024 + tk * 128: ci * 1024 + tk * 128 + 128],
                    t.wbf[pb][:, ci * 512:(ci + 1) * 512],
                    start=(ci == 0), stop=False))
            r.op("tensor", lambda e, tk=tk, ps=ps, dvb=dvb: e.matmul(
                ps, t.ones1[:, :],
                t.buvb[:, dvb * 512:(dvb + 1) * 512],
                start=False, stop=True), sem=t.sT)
            r.w("scalar", t.sT)
            if tk >= 2:
                r.w("scalar", t.qA, va_done[tk - 2])
            r.op("scalar", lambda e, ps=ps, pb=pb, tk=tk: e.activation(
                t.eb[pb][:, (tk % 2) * 512:(tk % 2) * 512 + 512], ps, AF.Identity),
                sem=t.sS)
            vs_done[tk] = r.cnt[t.sS]
            t.pj_sdone[dvb] = r.cnt[t.sS]
            glob_tk = half * 8 + tk
            out = t.vi_d.rearrange("(h tt p) d -> p h tt d", p=128, tt=16)[
                :, dvb * 4:(dvb + 1) * 4, glob_tk, :]
            r.dma("scalar", out,
                  t.eb[pb][:, (tk % 2) * 512:(tk % 2) * 512 + 512]
                  .rearrange("p (h d) -> p h d", h=4), t.qA)
            va_done[tk] = r.cnt[t.qA]
        t.pj_tdone[dvb] = r.cnt[t.sT]


def p2_half(r, t, half):
    ln_apply_half(r, t, half, t.xt1_d)
    barrier(r, t)
    proj(r, t, half, t.wdq_d, 4096, 1536, "sbuf", bias_ap=t.bdq,
         dst=t.cqT, name="cq")
    proj(r, t, half, t.wdkv_d, 4096, 512, "sbuf", bias_ap=t.bdkv,
         dst=t.ckvT, name="ckv")
    p2_krope(r, t, half)
    barrier(r, t)
    proj(r, t, half, t.wuk_d, 512, 4096, "dram", bias_ap=t.buk,
         dst_dram=t.kti_d, name="kup")
    p2_vup(r, t, half)
    barrier(r, t)


def p2c_half(r, t, half):
    proj(r, t, half, t.wuq_d, 1536, 4096, "dram", bias_ap=t.buqs, scale=SCALER,
         dst_dram=t.qtd_d, name="qup")
    proj(r, t, half, t.wrq_d, 1536, 2048, "dram", bias_ap=t.brqs, scale=SCALER,
         dst_dram=t.qrtd_d, name="qrope")
    barrier(r, t)


def p4(r, t):
    r.op("gpsimd", lambda e: e.memset(t.krA[:, :], 0.0))
    r.op("gpsimd", lambda e: e.memset(t.krB[:, :], 0.0), sem=t.sP)
    r.w("sync", t.sP)
    r.w("sync", t.sCC, 3)
    r.dma("sync", t.krA[0:64, 0:2048], t.krg_d[0:64, :], t.qS)
    r.dma("sync", t.krA[0:64, 2048:4096], t.krg_d[64:128, :], t.qS)
    r.dma("sync", t.krB[64:128, 0:2048], t.krg_d[0:64, :], t.qS)
    r.dma("sync", t.krB[64:128, 2048:4096], t.krg_d[64:128, :], t.qS)
    r.op("gpsimd", lambda e: e.memset(t.accL[:, :], 0.0), sem=t.sP)
    head_done_v = [0] * 34
    head_done_t = [0] * 34
    ob_done = []
    t._p4_prev_exp = None
    t._p4_prev_pv = None
    t._p4_prev_acc = None
    t._p4_ovfree = 0
    for h in range(32):
        hb = h % 2
        j = h // 2
        kb, vb, qb = t.kb[hb], t.vb[hb], t.qb[hb]
        if h >= 2:
            r.w("sync", t.sT, head_done_t[h - 2])
        r.dma("sync", kb[:, 0:2048], t.ktg_d[h * 128:(h + 1) * 128, :], t.qS)
        r.dma("sync", kb[:, 2048:4096], t.ktg_d[4096 + h * 128: 4096 + (h + 1) * 128, :], t.qS)
        for hf in range(2):
            src = t.vg_d.rearrange("(r hh tt p) d -> p r hh tt d", p=128, tt=16, hh=32)[
                :, hf, h, :, :]
            r.dma("sync", vb[:, hf * 2048:(hf + 1) * 2048]
                  .rearrange("p (c d) -> p c d", c=16), src, t.qS)
        r.dma("sync", qb[:, :], t.qtd_d[h * 128:(h + 1) * 128, :], t.qS)
        if hb == 0:
            qrb = t.qrb[j % 2]
            r.dma("sync", qrb[:, :], t.qrtd_d[j * 128:(j + 1) * 128, :], t.qS)
        qrb = t.qrb[j % 2]
        krP = t.krA if hb == 0 else t.krB
        opsum = t.psQ[0] if hb == 0 else t.psQ[2]
        lpsum = t.psQ[1] if hb == 0 else t.psQ[3]
        r.w("tensor", t.qS)
        for q in range(4):
            exp_done = [0] * 16
            pv_done = [0] * 16
            acc_done = [0] * 16
            ov_free = getattr(t, "_p4_ovfree", 0)

            def emit_S(kk, q=q, kb=kb, qb=qb, krP=krP, qrb=qrb):
                S = t.PB2 if kk % 2 == 0 else t.PB3
                if kk >= 2:
                    r.w("tensor", t.sS, exp_done[kk - 2])
                elif t._p4_prev_exp:
                    r.w("tensor", t.sS, t._p4_prev_exp[kk - 2])
                for cc in range(2):
                    c = 2 * kk + cc
                    r.op("tensor", lambda e, S=S, cc=cc, c=c: e.matmul(
                        S[:, cc * 512:(cc + 1) * 512], kb[:, c * 128:(c + 1) * 128],
                        qb[:, q * 512:(q + 1) * 512], start=True, stop=False))
                    r.op("tensor", lambda e, S=S, cc=cc, c=c: e.matmul(
                        S[:, cc * 512:(cc + 1) * 512], krP[:, c * 128:(c + 1) * 128],
                        qrb[:, q * 512:(q + 1) * 512], start=False, stop=True),
                        sem=(t.sT if cc == 1 else None))
                return r.cnt[t.sT]

            def emit_EPA(kk, s_done, q=q, vb=vb, opsum=opsum, ov_free=ov_free):
                S = t.PB2 if kk % 2 == 0 else t.PB3
                pbf = t.pbuf[kk % 2]
                if kk == 0:
                    r.w("vector", t.sP)
                r.w("scalar", t.sT, s_done)
                if kk >= 2:
                    r.w("scalar", t.sT, pv_done[kk - 2])
                    r.w("scalar", t.sV, acc_done[kk - 2])
                elif t._p4_prev_pv:
                    r.w("scalar", t.sT, t._p4_prev_pv[kk - 2])
                    r.w("scalar", t.sV, t._p4_prev_acc[kk - 2])
                r.op("scalar", lambda e, S=S, pbf=pbf: e.activation(
                    pbf[:, :], S[:, :], AF.Exp), sem=t.sS)
                exp_done[kk] = r.cnt[t.sS]
                r.w("tensor", t.sS, exp_done[kk])
                if kk == 0 and ov_free:
                    r.w("tensor", t.sV, ov_free)
                for cc in range(2):
                    c = 2 * kk + cc
                    r.op("tensor", lambda e, c=c, cc=cc, pbf=pbf, kk=kk: e.matmul(
                        opsum, vb[:, c * 128:(c + 1) * 128],
                        pbf[:, cc * 512:(cc + 1) * 512],
                        start=(kk == 0 and cc == 0), stop=(kk == 15 and cc == 1)),
                        sem=(t.sT if cc == 1 else None))
                pv_done[kk] = r.cnt[t.sT]
                r.w("vector", t.sS, exp_done[kk])
                r.op("vector", lambda e, pbf=pbf: e.tensor_tensor(
                    t.accL[:, :], t.accL[:, :], pbf[:, :], OP.add), sem=t.sV)
                acc_done[kk] = r.cnt[t.sV]

            s_marks = [0] * 16
            PIPE = True
            if PIPE:
                for kk in range(16):
                    s_marks[kk] = emit_S(kk)
                    if kk >= 1:
                        emit_EPA(kk - 1, s_marks[kk - 1])
                emit_EPA(15, s_marks[15])
            else:
                for kk in range(16):
                    s_marks[kk] = emit_S(kk)
                    emit_EPA(kk, s_marks[kk])
            t._p4_prev_exp = exp_done
            t._p4_prev_pv = pv_done
            t._p4_prev_acc = acc_done
            r.op("vector", lambda e: e.tensor_copy(t.accLb[:, :], t.accL[:, :]),
                 sem=t.sV)
            r.w("gpsimd", t.sV)
            r.op("gpsimd", lambda e: e.memset(t.accL[:, :], 0.0), sem=t.sP)
            r.w("tensor", t.sV)
            r.op("tensor", lambda e, lpsum=lpsum: e.matmul(
                lpsum, t.onesK[:, :], t.accLb[:, 0:512], start=True, stop=False))
            r.op("tensor", lambda e, lpsum=lpsum: e.matmul(
                lpsum, t.onesK[:, :], t.accLb[:, 512:1024], start=False, stop=True),
                sem=t.sT)
            r.w("scalar", t.sT)
            r.op("scalar", lambda e, lpsum=lpsum: e.activation(
                t.lnt[:, :], lpsum, AF.Ln))
            r.op("scalar", lambda e: e.activation(
                t.rcp[:, :], t.lnt[:, :], AF.Exp, scale=-1.0), sem=t.sS)
            r.w("vector", t.sS)
            fi = h * 4 + q
            ob = t.ob[fi % 2]
            if fi >= 2:
                r.w("vector", t.qA, ob_done[fi - 2])
            r.op("vector", lambda e, ob=ob, opsum=opsum: e.tensor_tensor(
                ob[:, :], opsum, t.rcp[:, :], OP.mult), sem=t.sV)
            t._p4_ovfree = r.cnt[t.sV]
            r.w("scalar", t.sV)
            r.dma("scalar", t.otd_d[h * 128:(h + 1) * 128, q * 512:(q + 1) * 512],
                  ob[:, :], t.qA)
            ob_done.append(r.cnt[t.qA])
        head_done_v[h] = r.cnt[t.sV]
        head_done_t[h] = r.cnt[t.sT]
    barrier(r, t)


def p5_half(r, t, half):
    r.dma("sync", t.uTh[:, :].rearrange("p (c q) -> p c q", c=32),
          t.otd_d[:, half * 1024:(half + 1) * 1024]
          .rearrange("(c p) q -> p c q", p=128), t.qS)
    barrier(r, t)
    v_done, s_done, t_done = [], [], []
    w_done = []
    for o in range(32):
        pb = o % 2
        wst, wbf = t.wstr[pb], t.wbf[pb]
        if o >= 2:
            r.w("sync", t.sV, v_done[o - 2])
            r.w("sync", t.sT, t_done[o - 2])
        r.dma("sync", wst[:, :].rearrange("p (c m) -> p c m", c=32),
              t.wout_d[:, o * 128:(o + 1) * 128].rearrange("(c p) m -> p c m", p=128),
              t.qS)
        r.w("vector", t.qS)
        if o >= 2:
            r.w("vector", t.sT, t_done[o - 2])
        r.op("vector", lambda e, wst=wst, wbf=wbf: e.tensor_copy(wbf[:, :], wst[:, :]),
             sem=t.sV)
        r.w("tensor", t.sV)
        if o >= 2:
            r.w("tensor", t.sS, s_done[o - 2])
        for qt in range(2):
            ps = t.psQ[pb * 2 + qt]
            for ci in range(32):
                r.op("tensor", lambda e, ci=ci, qt=qt, ps=ps, wbf=wbf: e.matmul(
                    ps, wbf[:, ci * 128:(ci + 1) * 128],
                    t.uTh[:, ci * 1024 + qt * 512: ci * 1024 + (qt + 1) * 512],
                    start=(ci == 0), stop=(ci == 31)),
                    sem=(t.sT if (ci == 31 and qt == 1) else None))
        t_done.append(r.cnt[t.sT])
        r.dma("sync", t.xres[pb][:, 0:1024],
              t.xt1_d.rearrange("(c p) q -> p c q", p=128)
              [:, o:o + 1, half * 1024:(half + 1) * 1024], t.qS)
        r.w("scalar", t.sT)
        if o >= 2:
            r.w("scalar", t.qS, w_done[o - 2])
            r.w("scalar", t.sV, v_done[o - 2])
        for qt in range(2):
            ps = t.psQ[pb * 2 + qt]
            r.op("scalar", lambda e, ps=ps, qt=qt, pb=pb, o=o: e.activation(
                t.t1[pb][:, qt * 512:(qt + 1) * 512], ps, AF.Identity,
                bias=t.bout[:, o:o + 1]), sem=(t.sS if qt == 1 else None))
        r.w("vector", t.sS)
        r.w("vector", t.qS)
        r.op("vector", lambda e, pb=pb: e.tensor_tensor(
            t.t1[pb][:, 0:1024], t.t1[pb][:, 0:1024], t.xres[pb][:, 0:1024],
            OP.add), sem=t.sV)
        r.w("sync", t.sV)
        r.dma("sync", t.xt0_d.rearrange("(c p) q -> p c q", p=128)
              [:, o:o + 1, half * 1024:(half + 1) * 1024],
              t.t1[pb][:, 0:1024].unsqueeze(1), t.qS)
        w_done.append(r.cnt[t.qS])
        r.op("vector", lambda e, pb=pb, half=half: e.tensor_tensor(
            t.accS[:, half * 1024:(half + 1) * 1024],
            t.accS[:, half * 1024:(half + 1) * 1024],
            t.t1[pb][:, 0:1024], OP.add))
        r.w("scalar", t.sV)
        r.op("scalar", lambda e, pb=pb: e.activation(
            t.dump[:, pb * 1024:(pb + 1) * 1024], t.t1[pb][:, 0:1024], AF.Square),
            sem=t.sS)
        s_done.append(r.cnt[t.sS])
        r.w("vector", t.sS)
        r.op("vector", lambda e, pb=pb, half=half: e.tensor_tensor(
            t.accQ[:, half * 1024:(half + 1) * 1024],
            t.accQ[:, half * 1024:(half + 1) * 1024],
            t.dump[:, pb * 1024:(pb + 1) * 1024], OP.add), sem=t.sV)
        v_done.append(r.cnt[t.sV])


def build(stop_after="p6", dbg=()):
    nc = bass.Bass("TRN2", target_bir_lowering=False)
    t = T()
    t.stage = 0
    t.x0_d = nc.dram_tensor("x0", [TC, D], F32, kind="ExternalInput").ap()
    t.wdq_d = nc.dram_tensor("wdq", [4096, 1536], F32, kind="ExternalInput").ap()
    t.wuq_d = nc.dram_tensor("wuq", [1536, 4096], F32, kind="ExternalInput").ap()
    t.wdkv_d = nc.dram_tensor("wdkv", [4096, 512], F32, kind="ExternalInput").ap()
    t.wuk_d = nc.dram_tensor("wuk", [512, 4096], F32, kind="ExternalInput").ap()
    t.wuv_d = nc.dram_tensor("wuv", [512, 4096], F32, kind="ExternalInput").ap()
    t.wrq_d = nc.dram_tensor("wrq", [1536, 2048], F32, kind="ExternalInput").ap()
    t.wrk_d = nc.dram_tensor("wrk", [4096, 64], F32, kind="ExternalInput").ap()
    t.wout_d = nc.dram_tensor("wout", [4096, 4096], F32, kind="ExternalInput").ap()
    t.bdq_d = nc.dram_tensor("bdq", [128, 12], F32, kind="ExternalInput").ap()
    t.buq_d = nc.dram_tensor("buq", [128, 32], F32, kind="ExternalInput").ap()
    t.bdkv_d = nc.dram_tensor("bdkv", [128, 4], F32, kind="ExternalInput").ap()
    t.buk_d = nc.dram_tensor("buk", [128, 32], F32, kind="ExternalInput").ap()
    t.buv_d = nc.dram_tensor("buv", [1, 4096], F32, kind="ExternalInput").ap()
    t.brq_d = nc.dram_tensor("brq", [128, 16], F32, kind="ExternalInput").ap()
    t.brk_d = nc.dram_tensor("brk", [64, 1], F32, kind="ExternalInput").ap()
    t.bout_d = nc.dram_tensor("bout", [128, 32], F32, kind="ExternalInput").ap()
    t.fq_d = nc.dram_tensor("fq", [4096, 4096], I32, kind="ExternalInput").ap()
    t.fsc_d = nc.dram_tensor("fsc", [4096, 32], F32, kind="ExternalInput").ap()
    t.fzp_d = nc.dram_tensor("fzp", [4096, 32], F32, kind="ExternalInput").ap()
    t.fb_d = nc.dram_tensor("fb", [128, 32], F32, kind="ExternalInput").ap()
    t.y_d = nc.dram_tensor("Y", [TC, D], F32, kind="ExternalOutput").ap()
    t.xt0_d = nc.dram_tensor("XT0", [D, TC], F32).ap()
    t.xt1_d = nc.dram_tensor("XT1", [D, TC], F32).ap()
    t.wtb_d = nc.dram_tensor("WTB", [32, 32, 128, 128], BF16).ap()
    t.qtd_d = nc.dram_tensor("QTD", [D, TC], BF16).ap()
    t.qrtd_d = nc.dram_tensor("QRTD", [2048, TC], BF16).ap()
    t.otd_d = nc.dram_tensor("OTD", [D, TC], BF16).ap()
    t.kti_d = nc.dram_tensor("KTI", [D, TC], BF16).ap()
    t.vi_d = nc.dram_tensor("VI", [65536, 128], BF16).ap()
    t.kri_d = nc.dram_tensor("KRI", [64, TC], BF16).ap()
    t.ktg_d = nc.dram_tensor("KTG", [2 * D, TC], BF16).ap()
    t.vg_d = nc.dram_tensor("VG", [131072, 128], BF16).ap()
    t.krg_d = nc.dram_tensor("KRG", [128, TC], BF16).ap()

    r = Rec(nc)
    t.qS = r.sem("qS")
    t.qA = r.sem("qA")
    t.qG = r.sem("qG")
    t.qOut = r.sem("qOut")
    t.sT = r.sem("sT")
    t.sV = r.sem("sV")
    t.sS = r.sem("sS")
    t.sP = r.sem("sP")
    t.sCC = r.sem("sCC")
    t.bar = r.sem("bar")

    import contextlib
    stack = contextlib.ExitStack()

    t.PB0 = stack.enter_context(nc.psum_tensor("PB0", [128, 1024], F32))
    t.PB1 = stack.enter_context(nc.psum_tensor("PB1", [128, 1024], F32))
    t.PB2 = stack.enter_context(nc.psum_tensor("PB2", [128, 1024], F32))
    t.PB3 = stack.enter_context(nc.psum_tensor("PB3", [128, 1024], F32))
    t.psQ = [t.PB0[:, 0:512], t.PB0[:, 512:1024],
             t.PB1[:, 0:512], t.PB1[:, 512:1024]]

    def sb(name, shape, dtype):
        return stack.enter_context(nc.sbuf_tensor(name, shape, dtype))

    t.uTh = sb("uTh", [128, 32 * HALF], BF16)
    t.dump = sb("dump", [128, 2048], BF16)
    t.stats = [sb("statsA", [128, 8], F32), sb("statsB", [128, 8], F32)]
    t.idb = sb("idb", [128, 128], BF16)
    t.idf = sb("idf", [128, 128], F32)
    t.ones1 = sb("ones1", [1, 128], BF16)
    t.onesK = sb("onesK", [128, 128], BF16)
    t.bd = sb("bd", [128, 1024], BF16)
    t.s_all = sb("s_all", [128, 1024], BF16)
    t.zst = sb("zst", [32, 4096], BF16)
    t.usum = sb("usum", [32, 2048], BF16)
    t.krs = sb("krs", [64, 1024], BF16)
    for nm, shp in (("bdq", 12), ("buq", 32), ("bdkv", 4), ("buk", 32),
                    ("brq", 16), ("bout", 32), ("fb", 32)):
        setattr(t, nm, sb(nm + "_s", [128, shp], F32))
    t.brk = sb("brk_s", [64, 1], F32)
    t.buqs = sb("buqs", [128, 32], F32)
    t.brqs = sb("brqs", [128, 16], F32)
    t.fbh = sb("fbh", [128, 32], F32)
    t.buvb = sb("buvb", [1, 4096], BF16)
    t.ceps = sb("ceps", [128, 1], F32)
    nc.const_aps.register(F32, EPS, t.ceps[:, 0:1])

    KB = 1024
    ov0 = (nc.sbuf_base + 63) // 64 * 64

    def at(name, shape, dtype, off):
        return nc.alloc_sbuf_tensor_at(name, shape, dtype, offset=ov0 + off)

    t.wstr = [at("wstrA", [128, 4096], F32, 0 * KB),
              at("wstrB", [128, 4096], F32, 16 * KB)]
    t.cqT = at("cqT", [128, 12 * HALF], BF16, 16 * KB)
    t.wbf = [at("wbfA", [128, 4096], BF16, 40 * KB),
             at("wbfB", [128, 4096], BF16, 48 * KB)]
    t.wtsb = [at("wtsbA", [128, 4096], BF16, 56 * KB),
              at("wtsbB", [128, 4096], BF16, 64 * KB)]
    t.eb = [at("ebA", [128, 1024], BF16, 56 * KB),
            at("ebB", [128, 1024], BF16, 58 * KB)]
    t.ckvT = at("ckvT", [128, 4 * HALF], BF16, 60 * KB)
    t.xres = [at("xresA", [128, 1024], F32, 72 * KB),
              at("xresB", [128, 1024], F32, 76 * KB)]
    t.t1 = [at("t1A", [128, 1024], F32, 80 * KB),
            at("t1B", [128, 1024], F32, 84 * KB)]
    t.accS = at("accS", [128, 2048], F32, 88 * KB)
    t.accQ = at("accQ", [128, 2048], F32, 96 * KB)
    t.mb = at("mb", [128, 2048], F32, 88 * KB)
    t.rb = at("rb", [128, 2048], F32, 96 * KB)
    t.ystg = [at("ystgA", [128, 1024], F32, 0 * KB),
              at("ystgB", [128, 1024], F32, 4 * KB)]
    t.sa_f = at("sa_f", [128, 2048], F32, 0 * KB)
    t.zsb = at("zsb", [128, 1024], BF16, 8 * KB)
    t.kb = [at("kbA", [128, 4096], BF16, 0 * KB),
            at("kbB", [128, 4096], BF16, 8 * KB)]
    t.vb = [at("vbA", [128, 4096], BF16, 16 * KB),
            at("vbB", [128, 4096], BF16, 24 * KB)]
    t.qb = [at("qbA", [128, 2048], BF16, 32 * KB),
            at("qbB", [128, 2048], BF16, 36 * KB)]
    t.qrb = [at("qrbA", [128, 2048], BF16, 40 * KB),
             at("qrbB", [128, 2048], BF16, 44 * KB)]
    t.krA = at("krA", [128, 4096], BF16, 48 * KB)
    t.krB = at("krB", [128, 4096], BF16, 56 * KB)
    t.pbuf = [at("pbufA", [128, 1024], BF16, 64 * KB),
              at("pbufB", [128, 1024], BF16, 66 * KB)]
    t.accL = at("accL", [128, 1024], F32, 68 * KB)
    t.accLb = at("accLb", [128, 1024], BF16, 72 * KB)
    t.lnt = at("lnt", [128, 512], F32, 74 * KB)
    t.rcp = at("rcp", [128, 512], F32, 76 * KB)
    t.ob = [at("obA", [128, 512], BF16, 78 * KB),
            at("obB", [128, 512], BF16, 79 * KB)]

    t.lnv_done = [0] * 32
    t.pj_vdone = [0] * 32
    t.pj_tdone = [0] * 32
    t.pj_sdone = [0] * 32

    p_init(r, t)
    barrier(r, t)

    def alloc_ACC():
        r.op("gpsimd", lambda e: e.memset(t.accS[:, :], 0.0))
        r.op("gpsimd", lambda e: e.memset(t.accQ[:, :], 0.0), sem=t.sP)
        r.w("vector", t.sP)
        r.w("scalar", t.sP)

    alloc_ACC()

    order = ["p0a", "p1a", "p0b", "p1b", "p2s", "p2h0", "p2h1", "p3",
             "p2c1", "p4", "p5", "p6"]
    upto = order.index(stop_after) if stop_after in order else len(order) - 1

    def phase(name):
        return order.index(name) <= upto

    if phase("p0a"):
        p0_half(r, t, 0)
        usum_half(r, t, 0)
        barrier(r, t)
    if phase("p1a"):
        ffn_half(r, t, 0, 1)
        barrier(r, t)
    if phase("p0b"):
        p0_half(r, t, 1)
        usum_half(r, t, 1)
        barrier(r, t)
    if phase("p1b"):
        ffn_half(r, t, 1, 1)
        barrier(r, t)
    if phase("p2s"):
        ln_stats(r, t)
        barrier(r, t)
    if phase("p2h0"):
        p2_half(r, t, 0)
        p2c_half(r, t, 0)
    if phase("p2h1"):
        p2_half(r, t, 1)
    if phase("p3"):
        for a, b in ((t.kti_d, t.ktg_d), (t.vi_d, t.vg_d), (t.kri_d, t.krg_d)):
            r.op("gpsimd", lambda e, a=a, b=b: e.collective_compute(
                "AllGather", OP.bypass,
                replica_groups=[[0, 1], [2, 3], [4, 5], [6, 7]],
                ins=[a.opt()], outs=[b.opt()]), sem=t.sCC)
    if phase("p2c1"):
        p2c_half(r, t, 1)
    if phase("p4"):
        p4(r, t)
    if phase("p5"):
        alloc_ACC()
        barrier(r, t)
        p5_half(r, t, 0)
        p5_half(r, t, 1)
        barrier(r, t)
    if phase("p6"):
        ln_stats(r, t)
        barrier(r, t)
        for half in range(2):
            ln_apply_half(r, t, half, t.xt0_d)
            barrier(r, t)
            usum_half(r, t, half)
            barrier(r, t)
            ffn_half(r, t, half, 2)
            barrier(r, t)

    dram_dbg = {"x1": t.xt1_d, "x2": t.xt0_d, "qt": t.qtd_d, "qrt": t.qrtd_d,
                "kt": t.kti_d, "kri": t.kri_d, "ot": t.otd_d, "vi": t.vi_d,
                "ktg": t.ktg_d, "krg": t.krg_d, "vg": t.vg_d, "wtb": t.wtb_d,
                "x0t": t.xt0_d}
    sbuf_dbg = {"uTh": t.uTh, "usum": t.usum, "mb": t.mb, "rb": t.rb,
                "stats0": t.stats[0], "stats1": t.stats[1], "accS": t.accS, "accQ": t.accQ,
                "zst": t.zst, "s_all": t.s_all, "cqT": t.cqT, "ckvT": t.ckvT}
    for name in dbg:
        if name in dram_dbg:
            src = dram_dbg[name]
            dbg_out = nc.dram_tensor("DBG_" + name, list(src.shape), src.dtype,
                                     kind="ExternalOutput").ap()
            r.dma("sync", dbg_out, src, t.qOut)
        else:
            h = sbuf_dbg[name]
            ap = h[:, :]
            dbg_out = nc.dram_tensor("DBG_" + name, list(ap.shape), ap.dtype,
                                     kind="ExternalOutput").ap()
            r.dma("sync", dbg_out, ap, t.qOut)
    r.w("sync", t.qOut)
    r.w("sync", t.qS)
    r.w("scalar", t.qA)
    r.w("gpsimd", t.qG)

    with nc.Block() as block:
        @block.sync
        def _(eng):
            for fn in r.steps["sync"]:
                fn(eng)

        @block.gpsimd
        def _(eng):
            for fn in r.steps["gpsimd"]:
                fn(eng)

        @block.vector
        def _(eng):
            for fn in r.steps["vector"]:
                fn(eng)

        @block.scalar
        def _(eng):
            for fn in r.steps["scalar"]:
                fn(eng)

        @block.tensor
        def _(eng):
            for fn in r.steps["tensor"]:
                fn(eng)

    stack.close()
    return nc


def _chunked(b, n):
    return np.ascontiguousarray(b.reshape(n, 128).T.astype(np.float32))


def make_inputs(inputs):
    m = {}
    m["wdq"] = np.ascontiguousarray(inputs["L0_attn_w_dq"], np.float32)
    m["wuq"] = np.ascontiguousarray(inputs["L0_attn_w_uq"], np.float32)
    m["wdkv"] = np.ascontiguousarray(inputs["L0_attn_w_dkv"], np.float32)
    m["wuk"] = np.ascontiguousarray(inputs["L0_attn_w_uk"], np.float32)
    m["wuv"] = np.ascontiguousarray(inputs["L0_attn_w_uv"], np.float32)
    m["wrq"] = np.ascontiguousarray(inputs["L0_attn_w_rq"], np.float32)
    m["wrk"] = np.ascontiguousarray(inputs["L0_attn_w_rk"], np.float32)
    m["wout"] = np.ascontiguousarray(inputs["L0_attn_w_out"], np.float32)
    m["bdq"] = _chunked(np.asarray(inputs["L0_attn_b_dq"]), 12)
    m["buq"] = _chunked(np.asarray(inputs["L0_attn_b_uq"]), 32)
    m["bdkv"] = _chunked(np.asarray(inputs["L0_attn_b_dkv"]), 4)
    m["buk"] = _chunked(np.asarray(inputs["L0_attn_b_uk"]), 32)
    m["buv"] = np.ascontiguousarray(
        np.asarray(inputs["L0_attn_b_uv"], np.float32).reshape(1, 4096))
    m["brq"] = _chunked(np.asarray(inputs["L0_attn_b_rq"]), 16)
    m["brk"] = np.ascontiguousarray(
        np.asarray(inputs["L0_attn_b_rk"], np.float32).reshape(64, 1))
    m["bout"] = _chunked(np.asarray(inputs["L0_attn_b_out"]), 32)
    m["fq"] = np.ascontiguousarray(inputs["L0_ffn_weight_q"], np.int32)
    m["fsc"] = np.ascontiguousarray(inputs["L0_ffn_scale"], np.float32)
    m["fzp"] = np.ascontiguousarray(inputs["L0_ffn_zero_point"], np.float32)
    m["fb"] = _chunked(np.asarray(inputs["L0_ffn_bias"]), 32)
    x = np.asarray(inputs["L0_attn_sequence"], np.float32)
    in_maps = []
    for c in range(8):
        b, hf = c // 2, c % 2
        mm = dict(m)
        mm["x0"] = np.ascontiguousarray(x[b, hf * TC:(hf + 1) * TC, :])
        in_maps.append(mm)
    return in_maps


def kernel(**inputs):
    in_maps = make_inputs(inputs)
    nc = build()
    res = run_bass_kernel_spmd(nc, in_maps, core_ids=list(range(8)))
    out = np.empty((4, 4096, 4096), np.float32)
    for c in range(8):
        out[c // 2, (c % 2) * TC:(c % 2 + 1) * TC, :] = res.results[c]["Y"]
    return out


# version 04
# speedup vs baseline: 1.1334x; 1.1334x over previous
import math
import numpy as np
import ml_dtypes
import concourse.bass as bass
import concourse.mybir as mybir
from concourse.bass_utils import run_bass_kernel_spmd

F32 = mybir.dt.float32
BF16 = mybir.dt.bfloat16
I32 = mybir.dt.int32
AF = mybir.ActivationFunctionType
OP = mybir.AluOpType

ENGINES = ("sync", "gpsimd", "vector", "scalar", "tensor")
SCALER = 1.0 / math.sqrt(192.0)
EPS = 1e-5
D = 4096
TC = 2048
HALF = 1024
NCH = 32


class Rec:

    def __init__(self, nc):
        self.nc = nc
        self.steps = {e: [] for e in ENGINES}
        self.cnt = {}
        self._guards = []

    def sem(self, name):
        g = self.nc.semaphore(name)
        h = g.__enter__()
        self._guards.append(g)
        self.cnt[h] = 0
        return h

    def em(self, eng, fn):
        self.steps[eng].append(fn)

    def w(self, eng, sem, val=None):
        v = self.cnt[sem] if val is None else val
        self.em(eng, lambda e, s=sem, v=v: e.wait_ge(s, v))

    def dma(self, queue, out, in_, sem, transpose=False, slow=False):
        self.cnt[sem] += 16

        def fn(e, out=out, in_=in_, t=transpose, s=sem, sl=slow):
            kw = {}
            if sl:
                kw["allow_slow_non_contiguous"] = True
            if t:
                e.dma_start(out=out, in_=in_, transpose=True, **kw).then_inc(s, 16)
            else:
                e.dma_start(out=out, in_=in_, **kw).then_inc(s, 16)

        self.em(queue, fn)
        return self.cnt[sem]

    def op(self, eng, builder, sem=None, inc=1):
        if sem is not None:
            self.cnt[sem] += inc

        def fn(e, b=builder, s=sem, i=inc):
            ins = b(e)
            if s is not None:
                ins.then_inc(s, i)

        self.em(eng, fn)
        return self.cnt[sem] if sem is not None else None


class T:
    pass


def _alloc_group(nc, specs):
    out = {}
    guards = []
    for attr, name, shape, dtype in specs:
        g = nc.sbuf_tensor(name, shape, dtype)
        h = g.__enter__()
        guards.append(g)
        out[attr] = h
    return out, guards


def _free_group(guards):
    for g in reversed(guards):
        g.__exit__(None, None, None)


def barrier(r, t):
    t.stage += 1
    r.w("sync", t.qS)
    r.w("scalar", t.qA)
    r.w("gpsimd", t.qG)
    for e in ENGINES:
        r.op(e, lambda en: en.nop(), sem=t.bar, inc=1)
    v = 5 * t.stage
    assert r.cnt[t.bar] == v
    for e in ENGINES:
        r.w(e, t.bar, v)


def p_init(r, t):
    nc = r.nc
    r.op("gpsimd", lambda e: e.memset(t.ceps[:, :], EPS))
    r.op("gpsimd", lambda e: e.memset(t.idb[:, :], 0.0))
    r.op("gpsimd", lambda e: e.affine_select(
        out=t.idb[:, :], in_=t.idb[:, :], compare_op=OP.not_equal, fill=1.0,
        base=0, pattern=[[-1, 128]], channel_multiplier=1))
    r.op("gpsimd", lambda e: e.memset(t.idf[:, :], 0.0))
    r.op("gpsimd", lambda e: e.affine_select(
        out=t.idf[:, :], in_=t.idf[:, :], compare_op=OP.not_equal, fill=1.0,
        base=0, pattern=[[-1, 128]], channel_multiplier=1))
    r.op("gpsimd", lambda e: e.memset(t.ones1[:, :], 1.0))
    r.op("gpsimd", lambda e: e.memset(t.onesK[:, :], 1.0))
    r.op("gpsimd", lambda e: e.memset(t.bd[:, :], 0.0))
    for g in range(32):
        r.op("gpsimd", lambda e, g=g: e.memset(t.bd[:, g * 32 + g: g * 32 + g + 1], 1.0))
    r.op("gpsimd", lambda e: e.engine_nop(), sem=t.sP)
    for nm in ("bdq", "buq", "bdkv", "buk", "brq", "bout", "fb"):
        r.dma("gpsimd", getattr(t, nm)[:, :], getattr(t, nm + "_d"), t.qG)
    r.dma("gpsimd", t.buvb[:, :], t.buv_d, t.qG)
    r.dma("gpsimd", t.brk[:, :], t.brk_d, t.qG)
    r.dma("gpsimd", t.sa_f[:, 0:1024].rearrange("p (c g) -> p c g", c=32),
          t.fsc_d.rearrange("(c p) g -> p c g", p=128), t.qG)
    r.dma("gpsimd", t.sa_f[:, 1024:2048].rearrange("p (c g) -> p c g", c=32),
          t.fzp_d.rearrange("(c p) g -> p c g", p=128), t.qG)
    r.w("vector", t.qG)
    r.w("gpsimd", t.qG)
    r.op("vector", lambda e: e.tensor_copy(t.s_all[:, :], t.sa_f[:, 0:1024]))
    r.op("vector", lambda e: e.tensor_tensor(
        t.zsb[:, :], t.sa_f[:, 0:1024], t.sa_f[:, 1024:2048], OP.mult), sem=t.sV)
    r.op("vector", lambda e: e.tensor_scalar(
        t.buqs[:, :], t.buq[:, :], SCALER, None, OP.mult))
    r.op("vector", lambda e: e.tensor_scalar(
        t.brqs[:, :], t.brq[:, :], SCALER, None, OP.mult))
    r.op("vector", lambda e: e.tensor_scalar(
        t.fbh[:, :], t.fb[:, :], 0.5, None, OP.mult))
    r.w("tensor", t.sV)
    r.w("tensor", t.sP)
    for o in range(32):
        pb = t.PB2 if (o % 2 == 0) else t.PB3
        dst = pb[0:32, 0:64].bitcast(BF16)
        r.op("tensor", lambda e, o=o, dst=dst: e.transpose(
            dst, t.zsb[:, o * 32:(o + 1) * 32], t.idb[:, :]), sem=t.sT)
        r.w("vector", t.sT)
        r.op("vector", lambda e, o=o, src=dst: e.tensor_copy(
            t.zst[0:32, o * 128:(o + 1) * 128], src), sem=t.sV)
        r.w("tensor", t.sV)


def p0_half(r, t, half):
    v_done, s_done, t_done = [], [], []
    for tc in range(8):
        g = half * 8 + tc
        pb = tc % 2
        xnat = t.wstr[pb]
        u1n = t.wbf[pb]
        st = t.stats[pb]
        if tc >= 2:
            r.w("sync", t.sV, v_done[tc - 2])
            r.w("sync", t.sS, s_done[tc - 2])
            r.w("sync", t.sT, t_done[tc - 2])
        r.dma("sync", xnat[:, :], t.x0_d[g * 128:(g + 1) * 128, :], t.qS)
        r.w("vector", t.qS)
        r.op("vector", lambda e, x=xnat, st=st: e.reduce_sum(
            st[:, 0:1], x[:, :], axis=mybir.AxisListType.X), sem=t.sV)
        r.w("vector", t.sV)
        r.op("vector", lambda e, st=st: e.tensor_scalar(
            st[:, 1:2], st[:, 0:1], 1.0 / D, None, OP.mult), sem=t.sV)
        r.w("vector", t.sV)
        r.w("scalar", t.qS)
        r.op("scalar", lambda e, x=xnat, st=st: e.activation(
            t.dump[:, 0:2048], x[:, 0:2048], AF.Square,
            accum_out=st[:, 2:3]))
        r.op("scalar", lambda e, x=xnat, st=st: e.activation(
            t.dump[:, 0:2048], x[:, 2048:4096], AF.Square,
            accum_out=st[:, 3:4]), sem=t.sS)
        r.w("vector", t.sS)
        r.op("vector", lambda e, st=st: e.tensor_tensor(
            st[:, 4:5], st[:, 2:3], st[:, 3:4], OP.add), sem=t.sV)
        r.w("vector", t.sV)
        r.op("vector", lambda e, st=st: e.tensor_scalar(
            st[:, 4:5], st[:, 4:5], 1.0 / D, None, OP.mult), sem=t.sV)
        r.w("vector", t.sV)
        r.op("vector", lambda e, st=st: e.tensor_tensor(
            st[:, 5:6], st[:, 1:2], st[:, 1:2], OP.mult), sem=t.sV)
        r.w("vector", t.sV)
        r.op("vector", lambda e, st=st: e.tensor_tensor(
            st[:, 4:5], st[:, 4:5], st[:, 5:6], OP.subtract), sem=t.sV)
        r.w("scalar", t.sV)
        r.op("scalar", lambda e, st=st: e.activation(
            st[:, 6:7], st[:, 4:5], AF.Ln, bias=EPS), sem=t.sS)
        r.w("scalar", t.sS)
        r.op("scalar", lambda e, st=st: e.activation(
            st[:, 7:8], st[:, 6:7], AF.Exp, scale=-0.5), sem=t.sS)
        r.w("vector", t.sS)
        r.op("vector", lambda e, x=xnat, u=u1n, st=st: e.tensor_scalar(
            u[:, :], x[:, :], st[:, 1:2], st[:, 7:8], OP.subtract, OP.mult), sem=t.sV)
        uv = r.cnt[t.sV]
        r.w("tensor", t.qS)
        for grp in range(4):
            pbk = t.PB2 if grp % 2 == 0 else t.PB3
            for j in range(8):
                d = grp * 8 + j
                r.op("tensor", lambda e, x=xnat, d=d, pbk=pbk, j=j: e.transpose(
                    pbk[:, j * 128:(j + 1) * 128], x[:, d * 128:(d + 1) * 128],
                    t.idf[:, :]), sem=(t.sT if j == 7 else None))
            r.w("scalar", t.sT)
            r.w("scalar", t.qS)
            r.op("scalar", lambda e, pbk=pbk, pb=pb: e.activation(
                t.t1[pb][:, 0:1024], pbk[:, :], AF.Identity), sem=t.sS)
            r.w("sync", t.sS)
            r.dma("sync", t.xt0_d.rearrange("(c p) q -> p c q", p=128)
                  [:, grp * 8:(grp + 1) * 8, g * 128:(g + 1) * 128],
                  t.t1[pb][:, 0:1024].rearrange("p (c q) -> p c q", c=8), t.qS)
            r.w("tensor", t.sS)
        r.w("tensor", t.sV, uv)
        for grp in range(2):
            pbk = t.PB2 if grp == 0 else t.PB3
            dstall = pbk[:, 0:1024].bitcast(BF16)
            for j in range(16):
                d = grp * 16 + j
                dst = pbk[:, j * 64:(j + 1) * 64].bitcast(BF16)
                r.op("tensor", lambda e, u=u1n, d=d, dst=dst: e.transpose(
                    dst, u[:, d * 128:(d + 1) * 128], t.idb[:, :]),
                    sem=(t.sT if j == 15 else None))
            r.w("vector", t.sT)
            dd = grp * 16
            out = t.uTh[:, :].rearrange("p (c q) -> p c q", c=32)[
                :, dd:dd + 16, tc * 128:(tc + 1) * 128]
            r.op("vector", lambda e, out=out, src=dstall: e.tensor_copy(
                out, src.rearrange("p (c q) -> p c q", c=16)), sem=t.sV)
            r.w("tensor", t.sV)
        v_done.append(r.cnt[t.sV])
        s_done.append(r.cnt[t.sS])
        t_done.append(r.cnt[t.sT])


def usum_half(r, t, half):
    for qt in range(2):
        ps = t.psQ[qt][0:32, :]
        for i in range(32):
            r.op("tensor", lambda e, i=i, qt=qt, ps=ps: e.matmul(
                ps, t.bd[:, i * 32:(i + 1) * 32],
                t.uTh[:, i * 1024 + qt * 512: i * 1024 + (qt + 1) * 512],
                start=(i == 0), stop=(i == 31)),
                sem=(t.sT if i == 31 else None))
        r.w("scalar", t.sT)
        r.op("scalar", lambda e, qt=qt, ps=ps, half=half: e.activation(
            t.usum[0:32, half * 1024 + qt * 512: half * 1024 + (qt + 1) * 512],
            ps[0:32, :], AF.Identity, scale=-1.0), sem=t.sS)
        r.w("tensor", t.sS)


def ffn_half(r, t, half, app):
    build_w = (app == 1 and half == 0)
    v_done, s_done, t_done = [], [], []
    w_done = []
    for o in range(32):
        pb = o % 2
        wt = t.wtsb[pb]
        if o >= 2:
            r.w("sync", t.sT, t_done[o - 2])
            r.w("sync", t.sS, s_done[o - 2])
            r.w("sync", t.sV, v_done[o - 2])
            r.w("scalar", t.sT, t_done[o - 2])
            r.w("gpsimd", t.sV, v_done[o - 2])
        if build_w:
            wq = t.wbf[pb]
            r.dma("gpsimd", wq[:, :], t.fq_d[o * 128:(o + 1) * 128, :], t.qG)
            r.w("vector", t.qG)
            r.op("vector", lambda e, wq=wq, o=o: e.tensor_tensor(
                wq[:, :].rearrange("p (g m) -> p g m", g=32), wq[:, :].rearrange("p (g m) -> p g m", g=32),
                t.s_all[:, o * 32:(o + 1) * 32].unsqueeze(2).broadcast_to((128, 32, 128)),
                OP.mult), sem=t.sV)
            r.w("tensor", t.sV)
            for grp in range(2):
                pbk = t.PB2 if grp == 0 else t.PB3
                for j in range(16):
                    i = grp * 16 + j
                    dst = pbk[:, j * 64:(j + 1) * 64].bitcast(BF16)
                    r.op("tensor", lambda e, wq=wq, i=i, dst=dst: e.transpose(
                        dst, wq[:, i * 128:(i + 1) * 128], t.idb[:, :]),
                        sem=(t.sT if j == 15 else None))
                r.w("vector", t.sT)
                r.op("vector", lambda e, wt=wt, grp=grp, pbk=pbk: e.tensor_copy(
                    wt[:, grp * 2048:(grp + 1) * 2048], pbk[:, 0:1024].bitcast(BF16)),
                    sem=t.sV)
                r.w("tensor", t.sV)
            r.w("scalar", t.sV)
            r.dma("scalar", t.wtb_d[o].rearrange("i p m -> p i m"),
                  wt[:, :].rearrange("p (i m) -> p i m", i=32), t.qA)
        else:
            r.dma("scalar", wt[:, :].rearrange("p (i m) -> p i m", i=32),
                  t.wtb_d[o].rearrange("i p m -> p i m"), t.qA)
            r.w("tensor", t.qA)
        for i in range(32):
            for qt in range(2):
                ps = t.psQ[pb * 2 + qt]
                r.op("tensor", lambda e, i=i, qt=qt, ps=ps, wt=wt: e.matmul(
                    ps, wt[:, i * 128:(i + 1) * 128],
                    t.uTh[:, i * 1024 + qt * 512: i * 1024 + (qt + 1) * 512],
                    start=(i == 0), stop=False))
        for qt in range(2):
            ps = t.psQ[pb * 2 + qt]
            r.op("tensor", lambda e, qt=qt, ps=ps, o=o, half=half: e.matmul(
                ps, t.zst[0:32, o * 128:(o + 1) * 128],
                t.usum[0:32, half * 1024 + qt * 512: half * 1024 + (qt + 1) * 512],
                start=False, stop=True), sem=(t.sT if qt == 1 else None))
        t_done.append(r.cnt[t.sT])
        res_src = t.xt0_d
        r.dma("sync", t.xres[pb][:, 0:1024],
              res_src.rearrange("(c p) q -> p c q", p=128)
              [:, o:o + 1, half * 1024:(half + 1) * 1024], t.qS)
        r.w("scalar", t.sT)
        if o >= 2:
            r.w("scalar", t.qS if app == 1 else t.qOut, w_done[o - 2])
            r.w("scalar", t.sV, v_done[o - 2])
        for qt in range(2):
            ps = t.psQ[pb * 2 + qt]
            r.op("scalar", lambda e, qt=qt, ps=ps, o=o, pb=pb: e.activation(
                t.t1[pb][:, qt * 512:(qt + 1) * 512], ps, AF.Identity,
                scale=0.5), sem=(t.sS if qt == 1 else None))
        r.w("vector", t.sS)
        r.w("vector", t.qS)
        r.op("vector", lambda e, pb=pb, o=o: e.scalar_tensor_tensor(
            t.t1[pb][:, 0:1024], t.t1[pb][:, 0:1024], t.fbh[:, o:o + 1],
            t.xres[pb][:, 0:1024], OP.add, OP.add), sem=t.sV)
        r.w("sync", t.sV)
        if app == 1:
            r.dma("sync", t.xt1_d.rearrange("(c p) q -> p c q", p=128)
                  [:, o:o + 1, half * 1024:(half + 1) * 1024],
                  t.t1[pb][:, 0:1024].unsqueeze(1), t.qS)
            w_done.append(r.cnt[t.qS])
            r.op("vector", lambda e, pb=pb, half=half: e.tensor_tensor(
                t.accS[:, half * 1024:(half + 1) * 1024],
                t.accS[:, half * 1024:(half + 1) * 1024],
                t.t1[pb][:, 0:1024], OP.add))
            r.w("scalar", t.sV)
            r.op("scalar", lambda e, pb=pb: e.activation(
                t.dump[:, pb * 1024:(pb + 1) * 1024], t.t1[pb][:, 0:1024],
                AF.Square), sem=t.sS)
            r.w("vector", t.sS)
            r.op("vector", lambda e, pb=pb, half=half: e.tensor_tensor(
                t.accQ[:, half * 1024:(half + 1) * 1024],
                t.accQ[:, half * 1024:(half + 1) * 1024],
                t.dump[:, pb * 1024:(pb + 1) * 1024], OP.add), sem=t.sV)
        else:
            r.w("tensor", t.sV)
            pbk = t.PB2 if pb == 0 else t.PB3
            for j in range(8):
                r.op("tensor", lambda e, pb=pb, j=j, pbk=pbk: e.transpose(
                    pbk[:, j * 128:(j + 1) * 128],
                    t.t1[pb][:, j * 128:(j + 1) * 128], t.idf[:, :]),
                    sem=(t.sT if j == 7 else None))
            t_done[-1] = r.cnt[t.sT]
            r.w("vector", t.sT)
            if o >= 2:
                r.w("vector", t.qOut, w_done[o - 2])
            r.op("vector", lambda e, pb=pb, pbk=pbk: e.tensor_copy(
                t.ystg[pb][:, 0:1024], pbk[:, :]), sem=t.sV)
            r.w("sync", t.sV)
            r.dma("sync", t.y_d.rearrange("(c p) q -> p c q", p=128)
                  [:, half * 8:(half + 1) * 8, o * 128:(o + 1) * 128],
                  t.ystg[pb][:, 0:1024].rearrange("p (c q) -> p c q", c=8),
                  t.qOut)
            w_done.append(r.cnt[t.qOut])
        v_done.append(r.cnt[t.sV])
        s_done.append(r.cnt[t.sS])


def ln_stats(r, t):
    r.op("vector", lambda e: e.tensor_copy(t.dump[:, 0:2048], t.accS[:, :]),
         sem=t.sV)
    r.w("tensor", t.sV)
    for qt in range(4):
        psm = t.psQ[qt % 2]
        r.op("tensor", lambda e, qt=qt, psm=psm: e.matmul(
            psm, t.onesK[:, :], t.dump[:, qt * 512:(qt + 1) * 512],
            start=True, stop=True), sem=t.sT)
        r.w("vector", t.sT)
        r.op("vector", lambda e, qt=qt, psm=psm: e.tensor_scalar(
            t.mb[:, qt * 512:(qt + 1) * 512], psm, 1.0 / D, None, OP.mult),
            sem=t.sV)
        r.w("tensor", t.sV)
    r.op("vector", lambda e: e.tensor_copy(t.dump[:, 0:2048], t.accQ[:, :]),
         sem=t.sV)
    r.w("tensor", t.sV)
    for qt in range(4):
        psv = t.psQ[2 + qt % 2]
        r.op("tensor", lambda e, qt=qt, psv=psv: e.matmul(
            psv, t.onesK[:, :], t.dump[:, qt * 512:(qt + 1) * 512],
            start=True, stop=True), sem=t.sT)
        r.w("vector", t.sT)
        r.op("vector", lambda e, qt=qt, psv=psv: e.tensor_scalar(
            t.rb[:, qt * 512:(qt + 1) * 512], psv, 1.0 / D, None, OP.mult))
        r.op("vector", lambda e, qt=qt: e.tensor_tensor(
            t.t1[0][:, 0:512], t.mb[:, qt * 512:(qt + 1) * 512],
            t.mb[:, qt * 512:(qt + 1) * 512], OP.mult))
        r.op("vector", lambda e, qt=qt: e.tensor_tensor(
            t.rb[:, qt * 512:(qt + 1) * 512], t.rb[:, qt * 512:(qt + 1) * 512],
            t.t1[0][:, 0:512], OP.subtract), sem=t.sV)
        r.w("scalar", t.sV)
        r.op("scalar", lambda e, qt=qt: e.activation(
            t.rb[:, qt * 512:(qt + 1) * 512], t.rb[:, qt * 512:(qt + 1) * 512],
            AF.Ln, bias=EPS))
        r.op("scalar", lambda e, qt=qt: e.activation(
            t.rb[:, qt * 512:(qt + 1) * 512], t.rb[:, qt * 512:(qt + 1) * 512],
            AF.Exp, scale=-0.5), sem=t.sS)
        r.w("tensor", t.sS)


def ln_apply_half(r, t, half, src_dram):
    for dch in range(32):
        pb = dch % 2
        if dch >= 2:
            r.w("sync", t.sV, t.lnv_done[dch - 2])
        r.dma("sync", t.xres[pb][:, 0:1024],
              src_dram.rearrange("(c p) q -> p c q", p=128)
              [:, dch:dch + 1, half * 1024:(half + 1) * 1024], t.qS)
        r.w("vector", t.qS)
        r.op("vector", lambda e, pb=pb, half=half: e.tensor_tensor(
            t.t1[pb][:, 0:1024], t.xres[pb][:, 0:1024],
            t.mb[:, half * 1024:(half + 1) * 1024], OP.subtract))
        r.op("vector", lambda e, pb=pb, dch=dch, half=half: e.tensor_tensor(
            t.uTh[:, dch * 1024:(dch + 1) * 1024], t.t1[pb][:, 0:1024],
            t.rb[:, half * 1024:(half + 1) * 1024], OP.mult), sem=t.sV)
        t.lnv_done[dch] = r.cnt[t.sV]


def proj(r, t, half, w_dram, n_in, n_out, out_kind, bias_ap=None, scale=1.0,
         dst=None, dst_dram=None, name=""):
    barrier(r, t)
    nin_ch = n_in // 128
    nout_ch = n_out // 128
    rhs_buf = t.uTh if name in ("cq", "ckv", "krope") else (
        t.cqT if name in ("qup", "qrope") else t.ckvT)
    pj_adone = [0] * nout_ch
    for oc in range(nout_ch):
        pb = oc % 2
        wst = t.wstr[0]
        wbf = t.wbf[pb]
        if oc >= 1:
            r.w("sync", t.sV, t.pj_vdone[oc - 1])
        r.dma("sync", wst[:, 0:nin_ch * 128].rearrange("p (c m) -> p c m", c=nin_ch),
              w_dram[:, oc * 128:(oc + 1) * 128].rearrange("(c p) m -> p c m", p=128),
              t.qS)
        r.w("vector", t.qS)
        if oc >= 2:
            r.w("vector", t.sT, t.pj_tdone[oc - 2])
        r.op("vector", lambda e, wst=wst, wbf=wbf, n=nin_ch * 128: e.tensor_copy(
            wbf[:, 0:n], wst[:, 0:n]), sem=t.sV)
        t.pj_vdone[oc] = r.cnt[t.sV]
        r.w("tensor", t.sV)
        if oc >= 2:
            r.w("tensor", t.sS, t.pj_sdone[oc - 2])
        for qt in range(2):
            ps = t.psQ[pb * 2 + qt]
            for ci in range(nin_ch):
                r.op("tensor", lambda e, ci=ci, qt=qt, ps=ps, wbf=wbf: e.matmul(
                    ps, wbf[:, ci * 128:(ci + 1) * 128],
                    rhs_buf[:, ci * 1024 + qt * 512: ci * 1024 + (qt + 1) * 512],
                    start=(ci == 0), stop=(ci == nin_ch - 1)),
                    sem=(t.sT if (ci == nin_ch - 1 and qt == 1) else None))
        t.pj_tdone[oc] = r.cnt[t.sT]
        r.w("scalar", t.sT)
        if out_kind == "dram" and oc >= 2:
            r.w("scalar", t.qA, pj_adone[oc - 2])
        for qt in range(2):
            ps = t.psQ[pb * 2 + qt]
            if out_kind == "sbuf":
                out_ap = dst[:, oc * 1024 + qt * 512: oc * 1024 + (qt + 1) * 512]
            else:
                out_ap = t.eb[pb][:, qt * 512:(qt + 1) * 512]
            if bias_ap is not None:
                r.op("scalar", lambda e, ps=ps, out_ap=out_ap, oc=oc: e.activation(
                    out_ap, ps, AF.Identity, scale=scale,
                    bias=bias_ap[:, oc:oc + 1]), sem=(t.sS if qt == 1 else None))
            else:
                r.op("scalar", lambda e, ps=ps, out_ap=out_ap: e.activation(
                    out_ap, ps, AF.Identity, scale=scale),
                    sem=(t.sS if qt == 1 else None))
        t.pj_sdone[oc] = r.cnt[t.sS]
        if out_kind == "dram":
            r.dma("scalar", dst_dram.rearrange("(c p) q -> p c q", p=128)
                  [:, oc:oc + 1, half * 1024:(half + 1) * 1024],
                  t.eb[pb][:, 0:1024].unsqueeze(1), t.qA)
            pj_adone[oc] = r.cnt[t.qA]


def p2_krope(r, t, half):
    barrier(r, t)
    r.dma("sync", t.wstr[0][:, 0:2048].rearrange("p (c m) -> p c m", c=32),
          t.wrk_d.rearrange("(c p) m -> p c m", p=128), t.qS)
    r.w("vector", t.qS)
    r.op("vector", lambda e: e.tensor_copy(t.wbf[0][:, 0:2048], t.wstr[0][:, 0:2048]),
         sem=t.sV)
    r.w("tensor", t.sV)
    for qt in range(2):
        ps = t.psQ[qt][0:64, :]
        for ci in range(32):
            r.op("tensor", lambda e, ci=ci, ps=ps, qt=qt: e.matmul(
                ps, t.wbf[0][:, ci * 64:(ci + 1) * 64],
                t.uTh[:, ci * 1024 + qt * 512: ci * 1024 + (qt + 1) * 512],
                start=(ci == 0), stop=(ci == 31)),
                sem=(t.sT if ci == 31 else None))
        r.w("scalar", t.sT)
        r.op("scalar", lambda e, qt=qt, ps=ps: e.activation(
            t.krs[0:64, qt * 512:(qt + 1) * 512], ps, AF.Identity,
            bias=t.brk[:, 0:1]), sem=t.sS)
        r.w("tensor", t.sS)
    r.w("scalar", t.sS)
    r.dma("scalar", t.kri_d[:, half * 1024:(half + 1) * 1024],
          t.krs[0:64, 0:1024], t.qA)


def p2_vup(r, t, half):
    barrier(r, t)
    va_done = [0] * 16
    vs_done = [0] * 16
    for dvb in range(8):
        pb = dvb % 2
        if dvb >= 1:
            r.w("sync", t.sV, t.pj_vdone[dvb - 1])
        r.dma("sync", t.wstr[0][:, 0:2048].rearrange("p (c m) -> p c m", c=4),
              t.wuv_d[:, dvb * 512:(dvb + 1) * 512].rearrange("(c p) m -> p c m", p=128),
              t.qS)
        r.w("vector", t.qS)
        if dvb >= 2:
            r.w("vector", t.sT, t.pj_tdone[dvb - 2])
        r.op("vector", lambda e, pb=pb: e.tensor_copy(
            t.wbf[pb][:, 0:2048], t.wstr[0][:, 0:2048]), sem=t.sV)
        t.pj_vdone[dvb] = r.cnt[t.sV]
        r.w("tensor", t.sV)
        if dvb >= 2:
            r.w("tensor", t.sS, t.pj_sdone[dvb - 2])
        for tk in range(8):
            ps = t.psQ[pb * 2 + tk % 2]
            if tk >= 2:
                r.w("tensor", t.sS, vs_done[tk - 2])
            for ci in range(4):
                r.op("tensor", lambda e, ci=ci, tk=tk, ps=ps, pb=pb: e.matmul(
                    ps, t.ckvT[:, ci * 1024 + tk * 128: ci * 1024 + tk * 128 + 128],
                    t.wbf[pb][:, ci * 512:(ci + 1) * 512],
                    start=(ci == 0), stop=False))
            r.op("tensor", lambda e, tk=tk, ps=ps, dvb=dvb: e.matmul(
                ps, t.ones1[:, :],
                t.buvb[:, dvb * 512:(dvb + 1) * 512],
                start=False, stop=True), sem=t.sT)
            r.w("scalar", t.sT)
            if tk >= 2:
                r.w("scalar", t.qA, va_done[tk - 2])
            r.op("scalar", lambda e, ps=ps, pb=pb, tk=tk: e.activation(
                t.eb[pb][:, (tk % 2) * 512:(tk % 2) * 512 + 512], ps, AF.Identity),
                sem=t.sS)
            vs_done[tk] = r.cnt[t.sS]
            t.pj_sdone[dvb] = r.cnt[t.sS]
            glob_tk = half * 8 + tk
            out = t.vi_d.rearrange("(h tt p) d -> p h tt d", p=128, tt=16)[
                :, dvb * 4:(dvb + 1) * 4, glob_tk, :]
            r.dma("scalar", out,
                  t.eb[pb][:, (tk % 2) * 512:(tk % 2) * 512 + 512]
                  .rearrange("p (h d) -> p h d", h=4), t.qA)
            va_done[tk] = r.cnt[t.qA]
        t.pj_tdone[dvb] = r.cnt[t.sT]


def p2_half(r, t, half):
    ln_apply_half(r, t, half, t.xt1_d)
    barrier(r, t)
    proj(r, t, half, t.wdq_d, 4096, 1536, "sbuf", bias_ap=t.bdq,
         dst=t.cqT, name="cq")
    proj(r, t, half, t.wdkv_d, 4096, 512, "sbuf", bias_ap=t.bdkv,
         dst=t.ckvT, name="ckv")
    p2_krope(r, t, half)
    barrier(r, t)
    proj(r, t, half, t.wuk_d, 512, 4096, "dram", bias_ap=t.buk,
         dst_dram=t.kti_d, name="kup")
    p2_vup(r, t, half)
    barrier(r, t)


def p2c_half(r, t, half):
    proj(r, t, half, t.wuq_d, 1536, 4096, "dram", bias_ap=t.buqs, scale=SCALER,
         dst_dram=t.qtd_d, name="qup")
    proj(r, t, half, t.wrq_d, 1536, 2048, "dram", bias_ap=t.brqs, scale=SCALER,
         dst_dram=t.qrtd_d, name="qrope")
    barrier(r, t)


def p4(r, t):
    r.op("gpsimd", lambda e: e.memset(t.krA[:, :], 0.0))
    r.op("gpsimd", lambda e: e.memset(t.krB[:, :], 0.0), sem=t.sP)
    r.w("sync", t.sP)
    r.w("sync", t.sCC, 3)
    r.dma("sync", t.krA[0:64, 0:2048], t.krg_d[0:64, :], t.qS)
    r.dma("sync", t.krA[0:64, 2048:4096], t.krg_d[64:128, :], t.qS)
    r.dma("sync", t.krB[64:128, 0:2048], t.krg_d[0:64, :], t.qS)
    r.dma("sync", t.krB[64:128, 2048:4096], t.krg_d[64:128, :], t.qS)
    r.op("gpsimd", lambda e: e.memset(t.accL[:, :], 0.0), sem=t.sP)
    head_done_v = [0] * 34
    head_done_t = [0] * 34
    ob_done = []
    t._p4_prev_exp = None
    t._p4_prev_pv = None
    t._p4_prev_acc = None
    t._p4_ovfree = 0
    for h in range(32):
        hb = h % 2
        j = h // 2
        kb, vb, qb = t.kb[hb], t.vb[hb], t.qb[hb]
        if h >= 2:
            r.w("sync", t.sT, head_done_t[h - 2])
        r.dma("sync", kb[:, 0:2048], t.ktg_d[h * 128:(h + 1) * 128, :], t.qS)
        r.dma("sync", kb[:, 2048:4096], t.ktg_d[4096 + h * 128: 4096 + (h + 1) * 128, :], t.qS)
        for hf in range(2):
            src = t.vg_d.rearrange("(r hh tt p) d -> p r hh tt d", p=128, tt=16, hh=32)[
                :, hf, h, :, :]
            r.dma("sync", vb[:, hf * 2048:(hf + 1) * 2048]
                  .rearrange("p (c d) -> p c d", c=16), src, t.qS)
        r.dma("sync", qb[:, :], t.qtd_d[h * 128:(h + 1) * 128, :], t.qS)
        if hb == 0:
            qrb = t.qrb[j % 2]
            r.dma("sync", qrb[:, :], t.qrtd_d[j * 128:(j + 1) * 128, :], t.qS)
        qrb = t.qrb[j % 2]
        krP = t.krA if hb == 0 else t.krB
        opsum = t.psQ[0] if hb == 0 else t.psQ[2]
        lpsum = t.psQ[1] if hb == 0 else t.psQ[3]
        r.w("tensor", t.qS)
        for q in range(4):
            exp_done = [0] * 16
            pv_done = [0] * 16
            acc_done = [0] * 16
            ov_free = getattr(t, "_p4_ovfree", 0)

            def emit_S(kk, q=q, kb=kb, qb=qb, krP=krP, qrb=qrb):
                S = t.PB2 if kk % 2 == 0 else t.PB3
                if kk >= 2:
                    r.w("tensor", t.sS, exp_done[kk - 2])
                elif t._p4_prev_exp:
                    r.w("tensor", t.sS, t._p4_prev_exp[kk - 2])
                for cc in range(2):
                    c = 2 * kk + cc
                    r.op("tensor", lambda e, S=S, cc=cc, c=c: e.matmul(
                        S[:, cc * 512:(cc + 1) * 512], kb[:, c * 128:(c + 1) * 128],
                        qb[:, q * 512:(q + 1) * 512], start=True, stop=False))
                    r.op("tensor", lambda e, S=S, cc=cc, c=c: e.matmul(
                        S[:, cc * 512:(cc + 1) * 512], krP[:, c * 128:(c + 1) * 128],
                        qrb[:, q * 512:(q + 1) * 512], start=False, stop=True),
                        sem=(t.sT if cc == 1 else None))
                return r.cnt[t.sT]

            def emit_EPA(kk, s_done, q=q, vb=vb, opsum=opsum, ov_free=ov_free):
                S = t.PB2 if kk % 2 == 0 else t.PB3
                pbf = t.pbuf[kk % 2]
                if kk == 0:
                    r.w("vector", t.sP)
                r.w("scalar", t.sT, s_done)
                if kk >= 2:
                    r.w("scalar", t.sT, pv_done[kk - 2])
                    r.w("scalar", t.sV, acc_done[kk - 2])
                elif t._p4_prev_pv:
                    r.w("scalar", t.sT, t._p4_prev_pv[kk - 2])
                    r.w("scalar", t.sV, t._p4_prev_acc[kk - 2])
                r.op("scalar", lambda e, S=S, pbf=pbf: e.activation(
                    pbf[:, :], S[:, :], AF.Exp), sem=t.sS)
                exp_done[kk] = r.cnt[t.sS]
                r.w("tensor", t.sS, exp_done[kk])
                if kk == 0 and ov_free:
                    r.w("tensor", t.sV, ov_free)
                for cc in range(2):
                    c = 2 * kk + cc
                    r.op("tensor", lambda e, c=c, cc=cc, pbf=pbf, kk=kk: e.matmul(
                        opsum, vb[:, c * 128:(c + 1) * 128],
                        pbf[:, cc * 512:(cc + 1) * 512],
                        start=(kk == 0 and cc == 0), stop=(kk == 15 and cc == 1)),
                        sem=(t.sT if cc == 1 else None))
                pv_done[kk] = r.cnt[t.sT]
                r.w("vector", t.sS, exp_done[kk])
                r.op("vector", lambda e, pbf=pbf: e.tensor_tensor(
                    t.accL[:, :], t.accL[:, :], pbf[:, :], OP.add), sem=t.sV)
                acc_done[kk] = r.cnt[t.sV]

            s_marks = [0] * 16
            PIPE = True
            if PIPE:
                for kk in range(16):
                    s_marks[kk] = emit_S(kk)
                    if kk >= 1:
                        emit_EPA(kk - 1, s_marks[kk - 1])
                emit_EPA(15, s_marks[15])
            else:
                for kk in range(16):
                    s_marks[kk] = emit_S(kk)
                    emit_EPA(kk, s_marks[kk])
            t._p4_prev_exp = exp_done
            t._p4_prev_pv = pv_done
            t._p4_prev_acc = acc_done
            r.op("vector", lambda e: e.tensor_copy(t.accLb[:, :], t.accL[:, :]),
                 sem=t.sV)
            r.w("gpsimd", t.sV)
            r.op("gpsimd", lambda e: e.memset(t.accL[:, :], 0.0), sem=t.sP)
            r.w("tensor", t.sV)
            r.op("tensor", lambda e, lpsum=lpsum: e.matmul(
                lpsum, t.onesK[:, :], t.accLb[:, 0:512], start=True, stop=False))
            r.op("tensor", lambda e, lpsum=lpsum: e.matmul(
                lpsum, t.onesK[:, :], t.accLb[:, 512:1024], start=False, stop=True),
                sem=t.sT)
            r.w("scalar", t.sT)
            r.op("scalar", lambda e, lpsum=lpsum: e.activation(
                t.lnt[:, :], lpsum, AF.Ln))
            r.op("scalar", lambda e: e.activation(
                t.rcp[:, :], t.lnt[:, :], AF.Exp, scale=-1.0), sem=t.sS)
            r.w("vector", t.sS)
            fi = h * 4 + q
            ob = t.ob[fi % 2]
            if fi >= 2:
                r.w("vector", t.qA, ob_done[fi - 2])
            r.op("vector", lambda e, ob=ob, opsum=opsum: e.tensor_tensor(
                ob[:, :], opsum, t.rcp[:, :], OP.mult), sem=t.sV)
            t._p4_ovfree = r.cnt[t.sV]
            r.w("scalar", t.sV)
            r.dma("scalar", t.otd_d[h * 128:(h + 1) * 128, q * 512:(q + 1) * 512],
                  ob[:, :], t.qA)
            ob_done.append(r.cnt[t.qA])
        head_done_v[h] = r.cnt[t.sV]
        head_done_t[h] = r.cnt[t.sT]
    barrier(r, t)


def p5_half(r, t, half):
    r.w("sync", t.sT)
    r.dma("sync", t.uTh[:, :].rearrange("p (c q) -> p c q", c=32),
          t.otd_d[:, half * 1024:(half + 1) * 1024]
          .rearrange("(c p) q -> p c q", p=128), t.qS)
    barrier(r, t)
    v_done, s_done, t_done = [], [], []
    w_done = []
    for o in range(32):
        pb = o % 2
        wst, wbf = t.wstr[pb], t.wbf[pb]
        if o >= 2:
            r.w("sync", t.sV, v_done[o - 2])
            r.w("sync", t.sT, t_done[o - 2])
        r.dma("sync", wst[:, :].rearrange("p (c m) -> p c m", c=32),
              t.wout_d[:, o * 128:(o + 1) * 128].rearrange("(c p) m -> p c m", p=128),
              t.qS)
        r.w("vector", t.qS)
        if o >= 2:
            r.w("vector", t.sT, t_done[o - 2])
        r.op("vector", lambda e, wst=wst, wbf=wbf: e.tensor_copy(wbf[:, :], wst[:, :]),
             sem=t.sV)
        r.w("tensor", t.sV)
        if o >= 2:
            r.w("tensor", t.sS, s_done[o - 2])
        for qt in range(2):
            ps = t.psQ[pb * 2 + qt]
            for ci in range(32):
                r.op("tensor", lambda e, ci=ci, qt=qt, ps=ps, wbf=wbf: e.matmul(
                    ps, wbf[:, ci * 128:(ci + 1) * 128],
                    t.uTh[:, ci * 1024 + qt * 512: ci * 1024 + (qt + 1) * 512],
                    start=(ci == 0), stop=(ci == 31)),
                    sem=(t.sT if (ci == 31 and qt == 1) else None))
        t_done.append(r.cnt[t.sT])
        r.dma("sync", t.xres[pb][:, 0:1024],
              t.xt1_d.rearrange("(c p) q -> p c q", p=128)
              [:, o:o + 1, half * 1024:(half + 1) * 1024], t.qS)
        r.w("scalar", t.sT)
        if o >= 2:
            r.w("scalar", t.qS, w_done[o - 2])
            r.w("scalar", t.sV, v_done[o - 2])
        for qt in range(2):
            ps = t.psQ[pb * 2 + qt]
            r.op("scalar", lambda e, ps=ps, qt=qt, pb=pb, o=o: e.activation(
                t.t1[pb][:, qt * 512:(qt + 1) * 512], ps, AF.Identity),
                sem=(t.sS if qt == 1 else None))
        r.w("vector", t.sS)
        r.w("vector", t.qS)
        r.op("vector", lambda e, pb=pb, o=o: e.scalar_tensor_tensor(
            t.t1[pb][:, 0:1024], t.t1[pb][:, 0:1024], t.bout[:, o:o + 1],
            t.xres[pb][:, 0:1024], OP.add, OP.add), sem=t.sV)
        r.w("sync", t.sV)
        r.dma("sync", t.xt0_d.rearrange("(c p) q -> p c q", p=128)
              [:, o:o + 1, half * 1024:(half + 1) * 1024],
              t.t1[pb][:, 0:1024].unsqueeze(1), t.qS)
        w_done.append(r.cnt[t.qS])
        r.op("vector", lambda e, pb=pb, half=half: e.tensor_tensor(
            t.accS[:, half * 1024:(half + 1) * 1024],
            t.accS[:, half * 1024:(half + 1) * 1024],
            t.t1[pb][:, 0:1024], OP.add))
        r.w("scalar", t.sV)
        r.op("scalar", lambda e, pb=pb: e.activation(
            t.dump[:, pb * 1024:(pb + 1) * 1024], t.t1[pb][:, 0:1024], AF.Square),
            sem=t.sS)
        s_done.append(r.cnt[t.sS])
        r.w("vector", t.sS)
        r.op("vector", lambda e, pb=pb, half=half: e.tensor_tensor(
            t.accQ[:, half * 1024:(half + 1) * 1024],
            t.accQ[:, half * 1024:(half + 1) * 1024],
            t.dump[:, pb * 1024:(pb + 1) * 1024], OP.add), sem=t.sV)
        v_done.append(r.cnt[t.sV])


def build(stop_after="p6", dbg=()):
    nc = bass.Bass("TRN2", target_bir_lowering=False)
    t = T()
    t.stage = 0
    t.x0_d = nc.dram_tensor("x0", [TC, D], F32, kind="ExternalInput").ap()
    t.wdq_d = nc.dram_tensor("wdq", [4096, 1536], F32, kind="ExternalInput").ap()
    t.wuq_d = nc.dram_tensor("wuq", [1536, 4096], F32, kind="ExternalInput").ap()
    t.wdkv_d = nc.dram_tensor("wdkv", [4096, 512], F32, kind="ExternalInput").ap()
    t.wuk_d = nc.dram_tensor("wuk", [512, 4096], F32, kind="ExternalInput").ap()
    t.wuv_d = nc.dram_tensor("wuv", [512, 4096], F32, kind="ExternalInput").ap()
    t.wrq_d = nc.dram_tensor("wrq", [1536, 2048], F32, kind="ExternalInput").ap()
    t.wrk_d = nc.dram_tensor("wrk", [4096, 64], F32, kind="ExternalInput").ap()
    t.wout_d = nc.dram_tensor("wout", [4096, 4096], F32, kind="ExternalInput").ap()
    t.bdq_d = nc.dram_tensor("bdq", [128, 12], F32, kind="ExternalInput").ap()
    t.buq_d = nc.dram_tensor("buq", [128, 32], F32, kind="ExternalInput").ap()
    t.bdkv_d = nc.dram_tensor("bdkv", [128, 4], F32, kind="ExternalInput").ap()
    t.buk_d = nc.dram_tensor("buk", [128, 32], F32, kind="ExternalInput").ap()
    t.buv_d = nc.dram_tensor("buv", [1, 4096], F32, kind="ExternalInput").ap()
    t.brq_d = nc.dram_tensor("brq", [128, 16], F32, kind="ExternalInput").ap()
    t.brk_d = nc.dram_tensor("brk", [64, 1], F32, kind="ExternalInput").ap()
    t.bout_d = nc.dram_tensor("bout", [128, 32], F32, kind="ExternalInput").ap()
    t.fq_d = nc.dram_tensor("fq", [4096, 4096], I32, kind="ExternalInput").ap()
    t.fsc_d = nc.dram_tensor("fsc", [4096, 32], F32, kind="ExternalInput").ap()
    t.fzp_d = nc.dram_tensor("fzp", [4096, 32], F32, kind="ExternalInput").ap()
    t.fb_d = nc.dram_tensor("fb", [128, 32], F32, kind="ExternalInput").ap()
    t.y_d = nc.dram_tensor("Y", [TC, D], F32, kind="ExternalOutput").ap()
    t.xt0_d = nc.dram_tensor("XT0", [D, TC], F32).ap()
    t.xt1_d = nc.dram_tensor("XT1", [D, TC], F32).ap()
    t.wtb_d = nc.dram_tensor("WTB", [32, 32, 128, 128], BF16).ap()
    t.qtd_d = nc.dram_tensor("QTD", [D, TC], BF16).ap()
    t.qrtd_d = nc.dram_tensor("QRTD", [2048, TC], BF16).ap()
    t.otd_d = nc.dram_tensor("OTD", [D, TC], BF16).ap()
    t.kti_d = nc.dram_tensor("KTI", [D, TC], BF16).ap()
    t.vi_d = nc.dram_tensor("VI", [65536, 128], BF16).ap()
    t.kri_d = nc.dram_tensor("KRI", [64, TC], BF16).ap()
    t.ktg_d = nc.dram_tensor("KTG", [2 * D, TC], BF16).ap()
    t.vg_d = nc.dram_tensor("VG", [131072, 128], BF16).ap()
    t.krg_d = nc.dram_tensor("KRG", [128, TC], BF16).ap()

    r = Rec(nc)
    t.qS = r.sem("qS")
    t.qA = r.sem("qA")
    t.qG = r.sem("qG")
    t.qOut = r.sem("qOut")
    t.sT = r.sem("sT")
    t.sV = r.sem("sV")
    t.sS = r.sem("sS")
    t.sP = r.sem("sP")
    t.sCC = r.sem("sCC")
    t.bar = r.sem("bar")

    import contextlib
    stack = contextlib.ExitStack()

    t.PB0 = stack.enter_context(nc.psum_tensor("PB0", [128, 1024], F32))
    t.PB1 = stack.enter_context(nc.psum_tensor("PB1", [128, 1024], F32))
    t.PB2 = stack.enter_context(nc.psum_tensor("PB2", [128, 1024], F32))
    t.PB3 = stack.enter_context(nc.psum_tensor("PB3", [128, 1024], F32))
    t.psQ = [t.PB0[:, 0:512], t.PB0[:, 512:1024],
             t.PB1[:, 0:512], t.PB1[:, 512:1024]]

    def sb(name, shape, dtype):
        return stack.enter_context(nc.sbuf_tensor(name, shape, dtype))

    t.uTh = sb("uTh", [128, 32 * HALF], BF16)
    t.dump = sb("dump", [128, 2048], BF16)
    t.stats = [sb("statsA", [128, 8], F32), sb("statsB", [128, 8], F32)]
    t.idb = sb("idb", [128, 128], BF16)
    t.idf = sb("idf", [128, 128], F32)
    t.ones1 = sb("ones1", [1, 128], BF16)
    t.onesK = sb("onesK", [128, 128], BF16)
    t.bd = sb("bd", [128, 1024], BF16)
    t.s_all = sb("s_all", [128, 1024], BF16)
    t.zst = sb("zst", [32, 4096], BF16)
    t.usum = sb("usum", [32, 2048], BF16)
    t.krs = sb("krs", [64, 1024], BF16)
    for nm, shp in (("bdq", 12), ("buq", 32), ("bdkv", 4), ("buk", 32),
                    ("brq", 16), ("bout", 32), ("fb", 32)):
        setattr(t, nm, sb(nm + "_s", [128, shp], F32))
    t.brk = sb("brk_s", [64, 1], F32)
    t.buqs = sb("buqs", [128, 32], F32)
    t.brqs = sb("brqs", [128, 16], F32)
    t.fbh = sb("fbh", [128, 32], F32)
    t.buvb = sb("buvb", [1, 4096], BF16)
    t.ceps = sb("ceps", [128, 1], F32)
    nc.const_aps.register(F32, EPS, t.ceps[:, 0:1])

    KB = 1024
    ov0 = (nc.sbuf_base + 63) // 64 * 64

    def at(name, shape, dtype, off):
        return nc.alloc_sbuf_tensor_at(name, shape, dtype, offset=ov0 + off)

    t.wstr = [at("wstrA", [128, 4096], F32, 0 * KB),
              at("wstrB", [128, 4096], F32, 16 * KB)]
    t.cqT = at("cqT", [128, 12 * HALF], BF16, 16 * KB)
    t.wbf = [at("wbfA", [128, 4096], BF16, 40 * KB),
             at("wbfB", [128, 4096], BF16, 48 * KB)]
    t.wtsb = [at("wtsbA", [128, 4096], BF16, 56 * KB),
              at("wtsbB", [128, 4096], BF16, 64 * KB)]
    t.eb = [at("ebA", [128, 1024], BF16, 56 * KB),
            at("ebB", [128, 1024], BF16, 58 * KB)]
    t.ckvT = at("ckvT", [128, 4 * HALF], BF16, 60 * KB)
    t.xres = [at("xresA", [128, 1024], F32, 72 * KB),
              at("xresB", [128, 1024], F32, 76 * KB)]
    t.t1 = [at("t1A", [128, 1024], F32, 80 * KB),
            at("t1B", [128, 1024], F32, 84 * KB)]
    t.accS = at("accS", [128, 2048], F32, 88 * KB)
    t.accQ = at("accQ", [128, 2048], F32, 96 * KB)
    t.mb = at("mb", [128, 2048], F32, 88 * KB)
    t.rb = at("rb", [128, 2048], F32, 96 * KB)
    t.ystg = [at("ystgA", [128, 1024], F32, 0 * KB),
              at("ystgB", [128, 1024], F32, 4 * KB)]
    t.sa_f = at("sa_f", [128, 2048], F32, 0 * KB)
    t.zsb = at("zsb", [128, 1024], BF16, 8 * KB)
    t.kb = [at("kbA", [128, 4096], BF16, 0 * KB),
            at("kbB", [128, 4096], BF16, 8 * KB)]
    t.vb = [at("vbA", [128, 4096], BF16, 16 * KB),
            at("vbB", [128, 4096], BF16, 24 * KB)]
    t.qb = [at("qbA", [128, 2048], BF16, 32 * KB),
            at("qbB", [128, 2048], BF16, 36 * KB)]
    t.qrb = [at("qrbA", [128, 2048], BF16, 40 * KB),
             at("qrbB", [128, 2048], BF16, 44 * KB)]
    t.krA = at("krA", [128, 4096], BF16, 48 * KB)
    t.krB = at("krB", [128, 4096], BF16, 56 * KB)
    t.pbuf = [at("pbufA", [128, 1024], BF16, 64 * KB),
              at("pbufB", [128, 1024], BF16, 66 * KB)]
    t.accL = at("accL", [128, 1024], F32, 68 * KB)
    t.accLb = at("accLb", [128, 1024], BF16, 72 * KB)
    t.lnt = at("lnt", [128, 512], F32, 74 * KB)
    t.rcp = at("rcp", [128, 512], F32, 76 * KB)
    t.ob = [at("obA", [128, 512], BF16, 78 * KB),
            at("obB", [128, 512], BF16, 79 * KB)]

    t.lnv_done = [0] * 32
    t.pj_vdone = [0] * 32
    t.pj_tdone = [0] * 32
    t.pj_sdone = [0] * 32

    p_init(r, t)
    barrier(r, t)

    def alloc_ACC():
        r.op("gpsimd", lambda e: e.memset(t.accS[:, :], 0.0))
        r.op("gpsimd", lambda e: e.memset(t.accQ[:, :], 0.0), sem=t.sP)
        r.w("vector", t.sP)
        r.w("scalar", t.sP)

    alloc_ACC()

    order = ["p0a", "p1a", "p0b", "p1b", "p2s", "p2h0", "p2h1", "p3",
             "p2c1", "p4", "p5", "p6"]
    upto = order.index(stop_after) if stop_after in order else len(order) - 1

    def phase(name):
        return order.index(name) <= upto

    if phase("p0a"):
        p0_half(r, t, 0)
        usum_half(r, t, 0)
        barrier(r, t)
    if phase("p1a"):
        ffn_half(r, t, 0, 1)
        barrier(r, t)
    if phase("p0b"):
        p0_half(r, t, 1)
        usum_half(r, t, 1)
        barrier(r, t)
    if phase("p1b"):
        ffn_half(r, t, 1, 1)
        barrier(r, t)
    if phase("p2s"):
        ln_stats(r, t)
        barrier(r, t)
    if phase("p2h0"):
        p2_half(r, t, 0)
        p2c_half(r, t, 0)
    if phase("p2h1"):
        p2_half(r, t, 1)
    if phase("p3"):
        for a, b in ((t.kti_d, t.ktg_d), (t.vi_d, t.vg_d), (t.kri_d, t.krg_d)):
            r.op("gpsimd", lambda e, a=a, b=b: e.collective_compute(
                "AllGather", OP.bypass,
                replica_groups=[[0, 1], [2, 3], [4, 5], [6, 7]],
                ins=[a.opt()], outs=[b.opt()]), sem=t.sCC)
    if phase("p2c1"):
        p2c_half(r, t, 1)
    if phase("p4"):
        p4(r, t)
    if phase("p5"):
        alloc_ACC()
        barrier(r, t)
        p5_half(r, t, 0)
        p5_half(r, t, 1)
        barrier(r, t)
    if phase("p6"):
        ln_stats(r, t)
        barrier(r, t)
        for half in range(2):
            ln_apply_half(r, t, half, t.xt0_d)
            barrier(r, t)
            usum_half(r, t, half)
            barrier(r, t)
            ffn_half(r, t, half, 2)
            barrier(r, t)

    dram_dbg = {"x1": t.xt1_d, "x2": t.xt0_d, "qt": t.qtd_d, "qrt": t.qrtd_d,
                "kt": t.kti_d, "kri": t.kri_d, "ot": t.otd_d, "vi": t.vi_d,
                "ktg": t.ktg_d, "krg": t.krg_d, "vg": t.vg_d, "wtb": t.wtb_d,
                "x0t": t.xt0_d}
    sbuf_dbg = {"uTh": t.uTh, "usum": t.usum, "mb": t.mb, "rb": t.rb,
                "stats0": t.stats[0], "stats1": t.stats[1], "accS": t.accS, "accQ": t.accQ,
                "zst": t.zst, "s_all": t.s_all, "cqT": t.cqT, "ckvT": t.ckvT}
    for name in dbg:
        if name in dram_dbg:
            src = dram_dbg[name]
            dbg_out = nc.dram_tensor("DBG_" + name, list(src.shape), src.dtype,
                                     kind="ExternalOutput").ap()
            r.dma("sync", dbg_out, src, t.qOut)
        else:
            h = sbuf_dbg[name]
            ap = h[:, :]
            dbg_out = nc.dram_tensor("DBG_" + name, list(ap.shape), ap.dtype,
                                     kind="ExternalOutput").ap()
            r.dma("sync", dbg_out, ap, t.qOut)
    r.w("sync", t.qOut)
    r.w("sync", t.qS)
    r.w("scalar", t.qA)
    r.w("gpsimd", t.qG)

    with nc.Block() as block:
        @block.sync
        def _(eng):
            for fn in r.steps["sync"]:
                fn(eng)

        @block.gpsimd
        def _(eng):
            for fn in r.steps["gpsimd"]:
                fn(eng)

        @block.vector
        def _(eng):
            for fn in r.steps["vector"]:
                fn(eng)

        @block.scalar
        def _(eng):
            for fn in r.steps["scalar"]:
                fn(eng)

        @block.tensor
        def _(eng):
            for fn in r.steps["tensor"]:
                fn(eng)

    stack.close()
    return nc


def _chunked(b, n):
    return np.ascontiguousarray(b.reshape(n, 128).T.astype(np.float32))


def make_inputs(inputs):
    m = {}
    m["wdq"] = np.ascontiguousarray(inputs["L0_attn_w_dq"], np.float32)
    m["wuq"] = np.ascontiguousarray(inputs["L0_attn_w_uq"], np.float32)
    m["wdkv"] = np.ascontiguousarray(inputs["L0_attn_w_dkv"], np.float32)
    m["wuk"] = np.ascontiguousarray(inputs["L0_attn_w_uk"], np.float32)
    m["wuv"] = np.ascontiguousarray(inputs["L0_attn_w_uv"], np.float32)
    m["wrq"] = np.ascontiguousarray(inputs["L0_attn_w_rq"], np.float32)
    m["wrk"] = np.ascontiguousarray(inputs["L0_attn_w_rk"], np.float32)
    m["wout"] = np.ascontiguousarray(inputs["L0_attn_w_out"], np.float32)
    m["bdq"] = _chunked(np.asarray(inputs["L0_attn_b_dq"]), 12)
    m["buq"] = _chunked(np.asarray(inputs["L0_attn_b_uq"]), 32)
    m["bdkv"] = _chunked(np.asarray(inputs["L0_attn_b_dkv"]), 4)
    m["buk"] = _chunked(np.asarray(inputs["L0_attn_b_uk"]), 32)
    m["buv"] = np.ascontiguousarray(
        np.asarray(inputs["L0_attn_b_uv"], np.float32).reshape(1, 4096))
    m["brq"] = _chunked(np.asarray(inputs["L0_attn_b_rq"]), 16)
    m["brk"] = np.ascontiguousarray(
        np.asarray(inputs["L0_attn_b_rk"], np.float32).reshape(64, 1))
    m["bout"] = _chunked(np.asarray(inputs["L0_attn_b_out"]), 32)
    m["fq"] = np.ascontiguousarray(inputs["L0_ffn_weight_q"], np.int32)
    m["fsc"] = np.ascontiguousarray(inputs["L0_ffn_scale"], np.float32)
    m["fzp"] = np.ascontiguousarray(inputs["L0_ffn_zero_point"], np.float32)
    m["fb"] = _chunked(np.asarray(inputs["L0_ffn_bias"]), 32)
    x = np.asarray(inputs["L0_attn_sequence"], np.float32)
    in_maps = []
    for c in range(8):
        b, hf = c // 2, c % 2
        mm = dict(m)
        mm["x0"] = np.ascontiguousarray(x[b, hf * TC:(hf + 1) * TC, :])
        in_maps.append(mm)
    return in_maps


def kernel(**inputs):
    in_maps = make_inputs(inputs)
    nc = build()
    res = run_bass_kernel_spmd(nc, in_maps, core_ids=list(range(8)))
    out = np.empty((4, 4096, 4096), np.float32)
    for c in range(8):
        out[c // 2, (c % 2) * TC:(c % 2 + 1) * TC, :] = res.results[c]["Y"]
    return out


# version 05
# speedup vs baseline: 1.2379x; 1.0922x over previous
import math
import numpy as np
import ml_dtypes
import concourse.bass as bass
import concourse.mybir as mybir
from concourse.bass_utils import run_bass_kernel_spmd

F32 = mybir.dt.float32
BF16 = mybir.dt.bfloat16
I32 = mybir.dt.int32
AF = mybir.ActivationFunctionType
OP = mybir.AluOpType

ENGINES = ("sync", "gpsimd", "vector", "scalar", "tensor")
SCALER = 1.0 / math.sqrt(192.0)
EPS = 1e-5
D = 4096
TC = 2048
HALF = 1024
NCH = 32


class Rec:

    def __init__(self, nc):
        self.nc = nc
        self.steps = {e: [] for e in ENGINES}
        self.cnt = {}
        self._guards = []

    def sem(self, name):
        g = self.nc.semaphore(name)
        h = g.__enter__()
        self._guards.append(g)
        self.cnt[h] = 0
        return h

    def em(self, eng, fn):
        self.steps[eng].append(fn)

    def w(self, eng, sem, val=None):
        v = self.cnt[sem] if val is None else val
        self.em(eng, lambda e, s=sem, v=v: e.wait_ge(s, v))

    def dma(self, queue, out, in_, sem, transpose=False, slow=False):
        self.cnt[sem] += 16

        def fn(e, out=out, in_=in_, t=transpose, s=sem, sl=slow):
            kw = {}
            if sl:
                kw["allow_slow_non_contiguous"] = True
            if t:
                e.dma_start(out=out, in_=in_, transpose=True, **kw).then_inc(s, 16)
            else:
                e.dma_start(out=out, in_=in_, **kw).then_inc(s, 16)

        self.em(queue, fn)
        return self.cnt[sem]

    def op(self, eng, builder, sem=None, inc=1):
        if sem is not None:
            self.cnt[sem] += inc

        def fn(e, b=builder, s=sem, i=inc):
            ins = b(e)
            if s is not None:
                ins.then_inc(s, i)

        self.em(eng, fn)
        return self.cnt[sem] if sem is not None else None


class T:
    pass


def _alloc_group(nc, specs):
    out = {}
    guards = []
    for attr, name, shape, dtype in specs:
        g = nc.sbuf_tensor(name, shape, dtype)
        h = g.__enter__()
        guards.append(g)
        out[attr] = h
    return out, guards


def _free_group(guards):
    for g in reversed(guards):
        g.__exit__(None, None, None)


def barrier(r, t):
    t.stage += 1
    r.w("sync", t.qS)
    r.w("scalar", t.qA)
    r.w("gpsimd", t.qG)
    for e in ENGINES:
        r.op(e, lambda en: en.nop(), sem=t.bar, inc=1)
    v = 5 * t.stage
    assert r.cnt[t.bar] == v
    for e in ENGINES:
        r.w(e, t.bar, v)


def p_init(r, t):
    nc = r.nc
    r.op("gpsimd", lambda e: e.memset(t.ceps[:, :], EPS))
    r.op("gpsimd", lambda e: e.memset(t.idb[:, :], 0.0))
    r.op("gpsimd", lambda e: e.affine_select(
        out=t.idb[:, :], in_=t.idb[:, :], compare_op=OP.not_equal, fill=1.0,
        base=0, pattern=[[-1, 128]], channel_multiplier=1))
    r.op("gpsimd", lambda e: e.memset(t.idf[:, :], 0.0))
    r.op("gpsimd", lambda e: e.affine_select(
        out=t.idf[:, :], in_=t.idf[:, :], compare_op=OP.not_equal, fill=1.0,
        base=0, pattern=[[-1, 128]], channel_multiplier=1))
    r.op("gpsimd", lambda e: e.memset(t.ones1[:, :], 1.0))
    r.op("gpsimd", lambda e: e.memset(t.onesK[:, :], 1.0))
    r.op("gpsimd", lambda e: e.memset(t.bd[:, :], 0.0))
    for g in range(32):
        r.op("gpsimd", lambda e, g=g: e.memset(t.bd[:, g * 32 + g: g * 32 + g + 1], 1.0))
    r.op("gpsimd", lambda e: e.engine_nop(), sem=t.sP)
    for nm in ("bdq", "buq", "bdkv", "buk", "brq", "bout", "fb"):
        r.dma("gpsimd", getattr(t, nm)[:, :], getattr(t, nm + "_d"), t.qG)
    r.dma("gpsimd", t.buvb[:, :], t.buv_d, t.qG)
    r.dma("gpsimd", t.brk[:, :], t.brk_d, t.qG)
    r.dma("gpsimd", t.sa_f[:, 0:1024].rearrange("p (c g) -> p c g", c=32),
          t.fsc_d.rearrange("(c p) g -> p c g", p=128), t.qG)
    r.dma("gpsimd", t.sa_f[:, 1024:2048].rearrange("p (c g) -> p c g", c=32),
          t.fzp_d.rearrange("(c p) g -> p c g", p=128), t.qG)
    r.w("vector", t.qG)
    r.w("gpsimd", t.qG)
    r.op("vector", lambda e: e.tensor_copy(t.s_all[:, :], t.sa_f[:, 0:1024]))
    r.op("vector", lambda e: e.tensor_tensor(
        t.zsb[:, :], t.sa_f[:, 0:1024], t.sa_f[:, 1024:2048], OP.mult), sem=t.sV)
    r.op("vector", lambda e: e.tensor_scalar(
        t.buqs[:, :], t.buq[:, :], SCALER, None, OP.mult))
    r.op("vector", lambda e: e.tensor_scalar(
        t.brqs[:, :], t.brq[:, :], SCALER, None, OP.mult))
    r.op("vector", lambda e: e.tensor_scalar(
        t.fbh[:, :], t.fb[:, :], 0.5, None, OP.mult))
    r.w("tensor", t.sV)
    r.w("tensor", t.sP)
    for o in range(32):
        pb = t.PB2 if (o % 2 == 0) else t.PB3
        dst = pb[0:32, 0:64].bitcast(BF16)
        r.op("tensor", lambda e, o=o, dst=dst: e.transpose(
            dst, t.zsb[:, o * 32:(o + 1) * 32], t.idb[:, :]), sem=t.sT)
        r.w("vector", t.sT)
        r.op("vector", lambda e, o=o, src=dst: e.tensor_copy(
            t.zst[0:32, o * 128:(o + 1) * 128], src), sem=t.sV)
        r.w("tensor", t.sV)


def p0_half(r, t, half):
    v_done, s_done, t_done = [], [], []
    for tc in range(8):
        g = half * 8 + tc
        pb = tc % 2
        xnat = t.wstr[pb]
        u1n = t.wbf[pb]
        st = t.stats[pb]
        if tc >= 2:
            r.w("sync", t.sV, v_done[tc - 2])
            r.w("sync", t.sS, s_done[tc - 2])
            r.w("sync", t.sT, t_done[tc - 2])
        r.dma("sync", xnat[:, :], t.x0_d[g * 128:(g + 1) * 128, :], t.qS)
        r.w("vector", t.qS)
        r.op("vector", lambda e, x=xnat, st=st: e.reduce_sum(
            st[:, 0:1], x[:, :], axis=mybir.AxisListType.X), sem=t.sV)
        r.w("vector", t.sV)
        r.op("vector", lambda e, st=st: e.tensor_scalar(
            st[:, 1:2], st[:, 0:1], 1.0 / D, None, OP.mult), sem=t.sV)
        r.w("vector", t.sV)
        r.w("scalar", t.qS)
        r.op("scalar", lambda e, x=xnat, st=st: e.activation(
            t.dump[:, 0:2048], x[:, 0:2048], AF.Square,
            accum_out=st[:, 2:3]))
        r.op("scalar", lambda e, x=xnat, st=st: e.activation(
            t.dump[:, 0:2048], x[:, 2048:4096], AF.Square,
            accum_out=st[:, 3:4]), sem=t.sS)
        r.w("vector", t.sS)
        r.op("vector", lambda e, st=st: e.tensor_tensor(
            st[:, 4:5], st[:, 2:3], st[:, 3:4], OP.add), sem=t.sV)
        r.w("vector", t.sV)
        r.op("vector", lambda e, st=st: e.tensor_scalar(
            st[:, 4:5], st[:, 4:5], 1.0 / D, None, OP.mult), sem=t.sV)
        r.w("vector", t.sV)
        r.op("vector", lambda e, st=st: e.tensor_tensor(
            st[:, 5:6], st[:, 1:2], st[:, 1:2], OP.mult), sem=t.sV)
        r.w("vector", t.sV)
        r.op("vector", lambda e, st=st: e.tensor_tensor(
            st[:, 4:5], st[:, 4:5], st[:, 5:6], OP.subtract), sem=t.sV)
        r.w("scalar", t.sV)
        r.op("scalar", lambda e, st=st: e.activation(
            st[:, 6:7], st[:, 4:5], AF.Ln, bias=EPS), sem=t.sS)
        r.w("scalar", t.sS)
        r.op("scalar", lambda e, st=st: e.activation(
            st[:, 7:8], st[:, 6:7], AF.Exp, scale=-0.5), sem=t.sS)
        r.w("vector", t.sS)
        r.op("vector", lambda e, x=xnat, u=u1n, st=st: e.tensor_scalar(
            u[:, :], x[:, :], st[:, 1:2], st[:, 7:8], OP.subtract, OP.mult), sem=t.sV)
        uv = r.cnt[t.sV]
        r.w("tensor", t.qS)
        for grp in range(4):
            pbk = t.PB2 if grp % 2 == 0 else t.PB3
            for j in range(8):
                d = grp * 8 + j
                r.op("tensor", lambda e, x=xnat, d=d, pbk=pbk, j=j: e.transpose(
                    pbk[:, j * 128:(j + 1) * 128], x[:, d * 128:(d + 1) * 128],
                    t.idf[:, :]), sem=(t.sT if j == 7 else None))
            r.w("scalar", t.sT)
            r.w("scalar", t.qS)
            r.op("scalar", lambda e, pbk=pbk, pb=pb: e.activation(
                t.t1[pb][:, 0:1024], pbk[:, :], AF.Identity), sem=t.sS)
            r.w("sync", t.sS)
            r.dma("sync", t.xt0_d.rearrange("(c p) q -> p c q", p=128)
                  [:, grp * 8:(grp + 1) * 8, g * 128:(g + 1) * 128],
                  t.t1[pb][:, 0:1024].rearrange("p (c q) -> p c q", c=8), t.qS)
            r.w("tensor", t.sS)
        r.w("tensor", t.sV, uv)
        for grp in range(2):
            pbk = t.PB2 if grp == 0 else t.PB3
            dstall = pbk[:, 0:1024].bitcast(BF16)
            for j in range(16):
                d = grp * 16 + j
                dst = pbk[:, j * 64:(j + 1) * 64].bitcast(BF16)
                r.op("tensor", lambda e, u=u1n, d=d, dst=dst: e.transpose(
                    dst, u[:, d * 128:(d + 1) * 128], t.idb[:, :]),
                    sem=(t.sT if j == 15 else None))
            r.w("vector", t.sT)
            dd = grp * 16
            out = t.uTh[:, :].rearrange("p (c q) -> p c q", c=32)[
                :, dd:dd + 16, tc * 128:(tc + 1) * 128]
            r.op("vector", lambda e, out=out, src=dstall: e.tensor_copy(
                out, src.rearrange("p (c q) -> p c q", c=16)), sem=t.sV)
            r.w("tensor", t.sV)
        v_done.append(r.cnt[t.sV])
        s_done.append(r.cnt[t.sS])
        t_done.append(r.cnt[t.sT])


def usum_half(r, t, half):
    for qt in range(2):
        ps = t.psQ[qt][0:32, :]
        for i in range(32):
            r.op("tensor", lambda e, i=i, qt=qt, ps=ps: e.matmul(
                ps, t.bd[:, i * 32:(i + 1) * 32],
                t.uTh[:, i * 1024 + qt * 512: i * 1024 + (qt + 1) * 512],
                start=(i == 0), stop=(i == 31)),
                sem=(t.sT if i == 31 else None))
        r.w("scalar", t.sT)
        r.op("scalar", lambda e, qt=qt, ps=ps, half=half: e.activation(
            t.usum[0:32, half * 1024 + qt * 512: half * 1024 + (qt + 1) * 512],
            ps[0:32, :], AF.Identity, scale=-1.0), sem=t.sS)
        r.w("tensor", t.sS)


def ffn_half(r, t, half, app):
    build_w = (app == 1 and half == 0)
    v_done, s_done, t_done = [], [], []
    w_done = []
    for o in range(32):
        pb = o % 2
        wt = t.wtsb[pb]
        if o >= 2:
            r.w("sync", t.sT, t_done[o - 2])
            r.w("sync", t.sS, s_done[o - 2])
            r.w("sync", t.sV, v_done[o - 2])
            r.w("scalar", t.sT, t_done[o - 2])
            r.w("gpsimd", t.sV, v_done[o - 2])
        if build_w:
            wq = t.wbf[pb]
            r.dma("gpsimd", wq[:, :], t.fq_d[o * 128:(o + 1) * 128, :], t.qG)
            r.w("vector", t.qG)
            r.op("vector", lambda e, wq=wq, o=o: e.tensor_tensor(
                wq[:, :].rearrange("p (g m) -> p g m", g=32), wq[:, :].rearrange("p (g m) -> p g m", g=32),
                t.s_all[:, o * 32:(o + 1) * 32].unsqueeze(2).broadcast_to((128, 32, 128)),
                OP.mult), sem=t.sV)
            r.w("tensor", t.sV)
            for grp in range(2):
                pbk = t.PB2 if grp == 0 else t.PB3
                for j in range(16):
                    i = grp * 16 + j
                    dst = pbk[:, j * 64:(j + 1) * 64].bitcast(BF16)
                    r.op("tensor", lambda e, wq=wq, i=i, dst=dst: e.transpose(
                        dst, wq[:, i * 128:(i + 1) * 128], t.idb[:, :]),
                        sem=(t.sT if j == 15 else None))
                r.w("vector", t.sT)
                r.op("vector", lambda e, wt=wt, grp=grp, pbk=pbk: e.tensor_copy(
                    wt[:, grp * 2048:(grp + 1) * 2048], pbk[:, 0:1024].bitcast(BF16)),
                    sem=t.sV)
                r.w("tensor", t.sV)
            r.w("scalar", t.sV)
            r.dma("scalar", t.wtb_d[o].rearrange("i p m -> p i m"),
                  wt[:, :].rearrange("p (i m) -> p i m", i=32), t.qA)
        else:
            if o >= 2:
                r.w("gpsimd", t.sT, t_done[o - 2])
            r.dma("gpsimd", wt[:, :].rearrange("p (i m) -> p i m", i=32),
                  t.wtb_d[o].rearrange("i p m -> p i m"), t.qG)
            r.w("tensor", t.qG)
        for i in range(32):
            for qt in range(2):
                ps = t.psQ[pb * 2 + qt]
                r.op("tensor", lambda e, i=i, qt=qt, ps=ps, wt=wt: e.matmul(
                    ps, wt[:, i * 128:(i + 1) * 128],
                    t.uTh[:, i * 1024 + qt * 512: i * 1024 + (qt + 1) * 512],
                    start=(i == 0), stop=False))
        for qt in range(2):
            ps = t.psQ[pb * 2 + qt]
            r.op("tensor", lambda e, qt=qt, ps=ps, o=o, half=half: e.matmul(
                ps, t.zst[0:32, o * 128:(o + 1) * 128],
                t.usum[0:32, half * 1024 + qt * 512: half * 1024 + (qt + 1) * 512],
                start=False, stop=True), sem=(t.sT if qt == 1 else None))
        t_done.append(r.cnt[t.sT])
        res_src = t.xt0_d
        r.dma("sync", t.xres[pb][:, 0:1024],
              res_src.rearrange("(c p) q -> p c q", p=128)
              [:, o:o + 1, half * 1024:(half + 1) * 1024], t.qS)
        r.w("scalar", t.sT)
        if o >= 2:
            r.w("scalar", t.qS if app == 1 else t.qOut, w_done[o - 2])
            r.w("scalar", t.sV, v_done[o - 2])
        for qt in range(2):
            ps = t.psQ[pb * 2 + qt]
            r.op("scalar", lambda e, qt=qt, ps=ps, o=o, pb=pb: e.activation(
                t.t1[pb][:, qt * 512:(qt + 1) * 512], ps, AF.Identity,
                scale=0.5), sem=(t.sS if qt == 1 else None))
        r.w("vector", t.sS)
        r.w("vector", t.qS)
        r.op("vector", lambda e, pb=pb, o=o: e.scalar_tensor_tensor(
            t.t1[pb][:, 0:1024], t.t1[pb][:, 0:1024], t.fbh[:, o:o + 1],
            t.xres[pb][:, 0:1024], OP.add, OP.add), sem=t.sV)
        r.w("sync", t.sV)
        if app == 1:
            r.dma("sync", t.xt1_d.rearrange("(c p) q -> p c q", p=128)
                  [:, o:o + 1, half * 1024:(half + 1) * 1024],
                  t.t1[pb][:, 0:1024].unsqueeze(1), t.qS)
            w_done.append(r.cnt[t.qS])
            r.op("vector", lambda e, pb=pb, half=half: e.tensor_tensor(
                t.accS[:, half * 1024:(half + 1) * 1024],
                t.accS[:, half * 1024:(half + 1) * 1024],
                t.t1[pb][:, 0:1024], OP.add))
            r.w("scalar", t.sV)
            r.op("scalar", lambda e, pb=pb: e.activation(
                t.dump[:, pb * 1024:(pb + 1) * 1024], t.t1[pb][:, 0:1024],
                AF.Square), sem=t.sS)
            r.w("vector", t.sS)
            r.op("vector", lambda e, pb=pb, half=half: e.tensor_tensor(
                t.accQ[:, half * 1024:(half + 1) * 1024],
                t.accQ[:, half * 1024:(half + 1) * 1024],
                t.dump[:, pb * 1024:(pb + 1) * 1024], OP.add), sem=t.sV)
        else:
            r.w("tensor", t.sV)
            pbk = t.PB2 if pb == 0 else t.PB3
            for j in range(8):
                r.op("tensor", lambda e, pb=pb, j=j, pbk=pbk: e.transpose(
                    pbk[:, j * 128:(j + 1) * 128],
                    t.t1[pb][:, j * 128:(j + 1) * 128], t.idf[:, :]),
                    sem=(t.sT if j == 7 else None))
            t_done[-1] = r.cnt[t.sT]
            r.w("vector", t.sT)
            if o >= 2:
                r.w("vector", t.qOut, w_done[o - 2])
            r.op("vector", lambda e, pb=pb, pbk=pbk: e.tensor_copy(
                t.ystg[pb][:, 0:1024], pbk[:, :]), sem=t.sV)
            r.w("sync", t.sV)
            r.dma("sync", t.y_d.rearrange("(c p) q -> p c q", p=128)
                  [:, half * 8:(half + 1) * 8, o * 128:(o + 1) * 128],
                  t.ystg[pb][:, 0:1024].rearrange("p (c q) -> p c q", c=8),
                  t.qOut)
            w_done.append(r.cnt[t.qOut])
        v_done.append(r.cnt[t.sV])
        s_done.append(r.cnt[t.sS])


def ln_stats(r, t):
    r.op("vector", lambda e: e.tensor_copy(t.dump[:, 0:2048], t.accS[:, :]),
         sem=t.sV)
    r.w("tensor", t.sV)
    for qt in range(4):
        psm = t.psQ[qt % 2]
        r.op("tensor", lambda e, qt=qt, psm=psm: e.matmul(
            psm, t.onesK[:, :], t.dump[:, qt * 512:(qt + 1) * 512],
            start=True, stop=True), sem=t.sT)
        r.w("vector", t.sT)
        r.op("vector", lambda e, qt=qt, psm=psm: e.tensor_scalar(
            t.mb[:, qt * 512:(qt + 1) * 512], psm, 1.0 / D, None, OP.mult),
            sem=t.sV)
        r.w("tensor", t.sV)
    r.op("vector", lambda e: e.tensor_copy(t.dump[:, 0:2048], t.accQ[:, :]),
         sem=t.sV)
    r.w("tensor", t.sV)
    for qt in range(4):
        psv = t.psQ[2 + qt % 2]
        r.op("tensor", lambda e, qt=qt, psv=psv: e.matmul(
            psv, t.onesK[:, :], t.dump[:, qt * 512:(qt + 1) * 512],
            start=True, stop=True), sem=t.sT)
        r.w("vector", t.sT)
        r.op("vector", lambda e, qt=qt, psv=psv: e.tensor_scalar(
            t.rb[:, qt * 512:(qt + 1) * 512], psv, 1.0 / D, None, OP.mult))
        r.op("vector", lambda e, qt=qt: e.tensor_tensor(
            t.t1[0][:, 0:512], t.mb[:, qt * 512:(qt + 1) * 512],
            t.mb[:, qt * 512:(qt + 1) * 512], OP.mult))
        r.op("vector", lambda e, qt=qt: e.tensor_tensor(
            t.rb[:, qt * 512:(qt + 1) * 512], t.rb[:, qt * 512:(qt + 1) * 512],
            t.t1[0][:, 0:512], OP.subtract), sem=t.sV)
        r.w("scalar", t.sV)
        r.op("scalar", lambda e, qt=qt: e.activation(
            t.rb[:, qt * 512:(qt + 1) * 512], t.rb[:, qt * 512:(qt + 1) * 512],
            AF.Ln, bias=EPS))
        r.op("scalar", lambda e, qt=qt: e.activation(
            t.rb[:, qt * 512:(qt + 1) * 512], t.rb[:, qt * 512:(qt + 1) * 512],
            AF.Exp, scale=-0.5), sem=t.sS)
        r.w("tensor", t.sS)


def ln_apply_half(r, t, half, src_dram):
    for dch in range(32):
        pb = dch % 2
        if dch >= 2:
            r.w("sync", t.sV, t.lnv_done[dch - 2])
        r.dma("sync", t.xres[pb][:, 0:1024],
              src_dram.rearrange("(c p) q -> p c q", p=128)
              [:, dch:dch + 1, half * 1024:(half + 1) * 1024], t.qS)
        r.w("vector", t.qS)
        r.op("vector", lambda e, pb=pb, half=half: e.tensor_tensor(
            t.t1[pb][:, 0:1024], t.xres[pb][:, 0:1024],
            t.mb[:, half * 1024:(half + 1) * 1024], OP.subtract))
        r.op("vector", lambda e, pb=pb, dch=dch, half=half: e.tensor_tensor(
            t.uTh[:, dch * 1024:(dch + 1) * 1024], t.t1[pb][:, 0:1024],
            t.rb[:, half * 1024:(half + 1) * 1024], OP.mult), sem=t.sV)
        t.lnv_done[dch] = r.cnt[t.sV]


def proj(r, t, half, w_dram, n_in, n_out, out_kind, bias_ap=None, scale=1.0,
         dst=None, dst_dram=None, name=""):
    barrier(r, t)
    nin_ch = n_in // 128
    nout_ch = n_out // 128
    rhs_buf = t.uTh if name in ("cq", "ckv", "krope") else (
        t.cqT if name in ("qup", "qrope") else t.ckvT)
    pj_adone = [0] * nout_ch
    for oc in range(nout_ch):
        pb = oc % 2
        wbf = t.wbf[pb]
        if oc >= 2:
            r.w("gpsimd", t.sT, t.pj_tdone[oc - 2])
        r.dma("gpsimd", wbf[:, 0:nin_ch * 128].rearrange("p (c m) -> p c m", c=nin_ch),
              w_dram[:, oc * 128:(oc + 1) * 128].rearrange("(c p) m -> p c m", p=128),
              t.qG)
        r.w("tensor", t.qG)
        if oc >= 2:
            r.w("tensor", t.sS, t.pj_sdone[oc - 2])
        for qt in range(2):
            ps = t.psQ[pb * 2 + qt]
            for ci in range(nin_ch):
                r.op("tensor", lambda e, ci=ci, qt=qt, ps=ps, wbf=wbf: e.matmul(
                    ps, wbf[:, ci * 128:(ci + 1) * 128],
                    rhs_buf[:, ci * 1024 + qt * 512: ci * 1024 + (qt + 1) * 512],
                    start=(ci == 0), stop=(ci == nin_ch - 1)),
                    sem=(t.sT if (ci == nin_ch - 1 and qt == 1) else None))
        t.pj_tdone[oc] = r.cnt[t.sT]
        r.w("scalar", t.sT)
        if out_kind == "dram" and oc >= 2:
            r.w("scalar", t.qA, pj_adone[oc - 2])
        for qt in range(2):
            ps = t.psQ[pb * 2 + qt]
            if out_kind == "sbuf":
                out_ap = dst[:, oc * 1024 + qt * 512: oc * 1024 + (qt + 1) * 512]
            else:
                out_ap = t.eb[pb][:, qt * 512:(qt + 1) * 512]
            if bias_ap is not None:
                r.op("scalar", lambda e, ps=ps, out_ap=out_ap, oc=oc: e.activation(
                    out_ap, ps, AF.Identity, scale=scale,
                    bias=bias_ap[:, oc:oc + 1]), sem=(t.sS if qt == 1 else None))
            else:
                r.op("scalar", lambda e, ps=ps, out_ap=out_ap: e.activation(
                    out_ap, ps, AF.Identity, scale=scale),
                    sem=(t.sS if qt == 1 else None))
        t.pj_sdone[oc] = r.cnt[t.sS]
        if out_kind == "dram":
            r.dma("scalar", dst_dram.rearrange("(c p) q -> p c q", p=128)
                  [:, oc:oc + 1, half * 1024:(half + 1) * 1024],
                  t.eb[pb][:, 0:1024].unsqueeze(1), t.qA)
            pj_adone[oc] = r.cnt[t.qA]


def p2_krope(r, t, half):
    barrier(r, t)
    r.dma("gpsimd", t.wbf[0][:, 0:2048].rearrange("p (c m) -> p c m", c=32),
          t.wrk_d.rearrange("(c p) m -> p c m", p=128), t.qG)
    r.w("tensor", t.qG)
    for qt in range(2):
        ps = t.psQ[qt][0:64, :]
        for ci in range(32):
            r.op("tensor", lambda e, ci=ci, ps=ps, qt=qt: e.matmul(
                ps, t.wbf[0][:, ci * 64:(ci + 1) * 64],
                t.uTh[:, ci * 1024 + qt * 512: ci * 1024 + (qt + 1) * 512],
                start=(ci == 0), stop=(ci == 31)),
                sem=(t.sT if ci == 31 else None))
        r.w("scalar", t.sT)
        r.op("scalar", lambda e, qt=qt, ps=ps: e.activation(
            t.krs[0:64, qt * 512:(qt + 1) * 512], ps, AF.Identity,
            bias=t.brk[:, 0:1]), sem=t.sS)
        r.w("tensor", t.sS)
    r.w("scalar", t.sS)
    r.dma("scalar", t.kri_d[:, half * 1024:(half + 1) * 1024],
          t.krs[0:64, 0:1024], t.qA)


def p2_vup(r, t, half):
    barrier(r, t)
    va_done = [0] * 16
    vs_done = [0] * 16
    for dvb in range(8):
        pb = dvb % 2
        if dvb >= 2:
            r.w("gpsimd", t.sT, t.pj_tdone[dvb - 2])
        r.dma("gpsimd", t.wbf[pb][:, 0:2048].rearrange("p (c m) -> p c m", c=4),
              t.wuv_d[:, dvb * 512:(dvb + 1) * 512].rearrange("(c p) m -> p c m", p=128),
              t.qG)
        r.w("tensor", t.qG)
        if dvb >= 2:
            r.w("tensor", t.sS, t.pj_sdone[dvb - 2])
        for tk in range(8):
            ps = t.psQ[pb * 2 + tk % 2]
            if tk >= 2:
                r.w("tensor", t.sS, vs_done[tk - 2])
            for ci in range(4):
                r.op("tensor", lambda e, ci=ci, tk=tk, ps=ps, pb=pb: e.matmul(
                    ps, t.ckvT[:, ci * 1024 + tk * 128: ci * 1024 + tk * 128 + 128],
                    t.wbf[pb][:, ci * 512:(ci + 1) * 512],
                    start=(ci == 0), stop=False))
            r.op("tensor", lambda e, tk=tk, ps=ps, dvb=dvb: e.matmul(
                ps, t.ones1[:, :],
                t.buvb[:, dvb * 512:(dvb + 1) * 512],
                start=False, stop=True), sem=t.sT)
            r.w("scalar", t.sT)
            if tk >= 2:
                r.w("scalar", t.qA, va_done[tk - 2])
            r.op("scalar", lambda e, ps=ps, pb=pb, tk=tk: e.activation(
                t.eb[pb][:, (tk % 2) * 512:(tk % 2) * 512 + 512], ps, AF.Identity),
                sem=t.sS)
            vs_done[tk] = r.cnt[t.sS]
            t.pj_sdone[dvb] = r.cnt[t.sS]
            glob_tk = half * 8 + tk
            out = t.vi_d.rearrange("(h tt p) d -> p h tt d", p=128, tt=16)[
                :, dvb * 4:(dvb + 1) * 4, glob_tk, :]
            r.dma("scalar", out,
                  t.eb[pb][:, (tk % 2) * 512:(tk % 2) * 512 + 512]
                  .rearrange("p (h d) -> p h d", h=4), t.qA)
            va_done[tk] = r.cnt[t.qA]
        t.pj_tdone[dvb] = r.cnt[t.sT]


def p2_half(r, t, half):
    ln_apply_half(r, t, half, t.xt1_d)
    barrier(r, t)
    proj(r, t, half, t.wdq_d, 4096, 1536, "sbuf", bias_ap=t.bdq,
         dst=t.cqT, name="cq")
    proj(r, t, half, t.wdkv_d, 4096, 512, "sbuf", bias_ap=t.bdkv,
         dst=t.ckvT, name="ckv")
    p2_krope(r, t, half)
    barrier(r, t)
    proj(r, t, half, t.wuk_d, 512, 4096, "dram", bias_ap=t.buk,
         dst_dram=t.kti_d, name="kup")
    p2_vup(r, t, half)
    barrier(r, t)


def p2c_half(r, t, half):
    proj(r, t, half, t.wuq_d, 1536, 4096, "dram", bias_ap=t.buqs, scale=SCALER,
         dst_dram=t.qtd_d, name="qup")
    proj(r, t, half, t.wrq_d, 1536, 2048, "dram", bias_ap=t.brqs, scale=SCALER,
         dst_dram=t.qrtd_d, name="qrope")
    barrier(r, t)


def p4(r, t):
    r.op("gpsimd", lambda e: e.memset(t.krA[:, :], 0.0))
    r.op("gpsimd", lambda e: e.memset(t.krB[:, :], 0.0), sem=t.sP)
    r.w("sync", t.sP)
    r.w("sync", t.sCC, 3)
    r.dma("sync", t.krA[0:64, 0:2048], t.krg_d[0:64, :], t.qS)
    r.dma("sync", t.krA[0:64, 2048:4096], t.krg_d[64:128, :], t.qS)
    r.dma("sync", t.krB[64:128, 0:2048], t.krg_d[0:64, :], t.qS)
    r.dma("sync", t.krB[64:128, 2048:4096], t.krg_d[64:128, :], t.qS)
    r.op("gpsimd", lambda e: e.memset(t.accL[:, :], 0.0))
    r.op("gpsimd", lambda e: e.memset(t.accP[:, :], 0.0), sem=t.sP)
    head_done_v = [0] * 34
    head_done_t = [0] * 34
    ob_done = []
    t._p4_prev_exp = None
    t._p4_prev_pv = None
    t._p4_prev_acc = None
    t._p4_ovfree = 0
    for h in range(32):
        hb = h % 2
        j = h // 2
        kb, vb, qb = t.kb[hb], t.vb[hb], t.qb[hb]
        if h >= 2:
            r.w("sync", t.sT, head_done_t[h - 2])
        r.dma("sync", kb[:, 0:2048], t.ktg_d[h * 128:(h + 1) * 128, :], t.qS)
        r.dma("sync", kb[:, 2048:4096], t.ktg_d[4096 + h * 128: 4096 + (h + 1) * 128, :], t.qS)
        for hf in range(2):
            src = t.vg_d.rearrange("(r hh tt p) d -> p r hh tt d", p=128, tt=16, hh=32)[
                :, hf, h, :, :]
            r.dma("sync", vb[:, hf * 2048:(hf + 1) * 2048]
                  .rearrange("p (c d) -> p c d", c=16), src, t.qS)
        r.dma("sync", qb[:, :], t.qtd_d[h * 128:(h + 1) * 128, :], t.qS)
        if hb == 0:
            qrb = t.qrb[j % 2]
            r.dma("sync", qrb[:, :], t.qrtd_d[j * 128:(j + 1) * 128, :], t.qS)
        qrb = t.qrb[j % 2]
        krP = t.krA if hb == 0 else t.krB
        opsum = t.psQ[0] if hb == 0 else t.psQ[2]
        lpsum = t.psQ[1] if hb == 0 else t.psQ[3]
        r.w("tensor", t.qS)
        for q in range(4):
            exp_done = [0] * 16
            pv_done = [0] * 16
            acc_done = [("V", 0)] * 16
            ov_free = getattr(t, "_p4_ovfree", 0)

            def emit_S(kk, q=q, kb=kb, qb=qb, krP=krP, qrb=qrb):
                S = t.PB2 if kk % 2 == 0 else t.PB3
                if kk >= 2:
                    r.w("tensor", t.sS, exp_done[kk - 2])
                elif t._p4_prev_exp:
                    r.w("tensor", t.sS, t._p4_prev_exp[kk - 2])
                for cc in range(2):
                    c = 2 * kk + cc
                    r.op("tensor", lambda e, S=S, cc=cc, c=c: e.matmul(
                        S[:, cc * 512:(cc + 1) * 512], kb[:, c * 128:(c + 1) * 128],
                        qb[:, q * 512:(q + 1) * 512], start=True, stop=False))
                    r.op("tensor", lambda e, S=S, cc=cc, c=c: e.matmul(
                        S[:, cc * 512:(cc + 1) * 512], krP[:, c * 128:(c + 1) * 128],
                        qrb[:, q * 512:(q + 1) * 512], start=False, stop=True),
                        sem=(t.sT if cc == 1 else None))
                return r.cnt[t.sT]

            def emit_EPA(kk, s_done, q=q, vb=vb, opsum=opsum, ov_free=ov_free):
                S = t.PB2 if kk % 2 == 0 else t.PB3
                pbf = t.pbuf[kk % 2]
                if kk == 0:
                    r.w("vector", t.sP)
                r.w("scalar", t.sT, s_done)
                def wacc(ad):
                    r.w("scalar", t.sP if ad[0] == "P" else t.sV, ad[1])
                if kk >= 2:
                    r.w("scalar", t.sT, pv_done[kk - 2])
                    wacc(acc_done[kk - 2])
                elif t._p4_prev_pv:
                    r.w("scalar", t.sT, t._p4_prev_pv[kk - 2])
                    wacc(t._p4_prev_acc[kk - 2])
                r.op("scalar", lambda e, S=S, pbf=pbf: e.activation(
                    pbf[:, :], S[:, :], AF.Exp), sem=t.sS)
                exp_done[kk] = r.cnt[t.sS]
                r.w("tensor", t.sS, exp_done[kk])
                if kk == 0 and ov_free:
                    r.w("tensor", t.sV, ov_free)
                for cc in range(2):
                    c = 2 * kk + cc
                    r.op("tensor", lambda e, c=c, cc=cc, pbf=pbf, kk=kk: e.matmul(
                        opsum, vb[:, c * 128:(c + 1) * 128],
                        pbf[:, cc * 512:(cc + 1) * 512],
                        start=(kk == 0 and cc == 0), stop=(kk == 15 and cc == 1)),
                        sem=(t.sT if cc == 1 else None))
                pv_done[kk] = r.cnt[t.sT]
                if kk % 3 == 2:
                    r.w("gpsimd", t.sS, exp_done[kk])
                    r.op("gpsimd", lambda e, pbf=pbf: e.tensor_tensor(
                        t.accP[:, :], t.accP[:, :], pbf[:, :], OP.add), sem=t.sP)
                    acc_done[kk] = ("P", r.cnt[t.sP])
                else:
                    r.w("vector", t.sS, exp_done[kk])
                    r.op("vector", lambda e, pbf=pbf: e.tensor_tensor(
                        t.accL[:, :], t.accL[:, :], pbf[:, :], OP.add), sem=t.sV)
                    acc_done[kk] = ("V", r.cnt[t.sV])

            s_marks = [0] * 16
            PIPE = True
            if PIPE:
                for kk in range(16):
                    s_marks[kk] = emit_S(kk)
                    if kk >= 1:
                        emit_EPA(kk - 1, s_marks[kk - 1])
                emit_EPA(15, s_marks[15])
            else:
                for kk in range(16):
                    s_marks[kk] = emit_S(kk)
                    emit_EPA(kk, s_marks[kk])
            t._p4_prev_exp = exp_done
            t._p4_prev_pv = pv_done
            t._p4_prev_acc = acc_done
            r.w("vector", t.sP)
            r.op("vector", lambda e: e.tensor_copy(t.accLb[:, :], t.accL[:, :]))
            r.op("vector", lambda e: e.tensor_copy(t.accPb[:, :], t.accP[:, :]),
                 sem=t.sV)
            r.w("gpsimd", t.sV)
            r.op("gpsimd", lambda e: e.memset(t.accL[:, :], 0.0))
            r.op("gpsimd", lambda e: e.memset(t.accP[:, :], 0.0), sem=t.sP)
            r.w("tensor", t.sV)
            r.op("tensor", lambda e, lpsum=lpsum: e.matmul(
                lpsum, t.onesK[:, :], t.accLb[:, 0:512], start=True, stop=False))
            r.op("tensor", lambda e, lpsum=lpsum: e.matmul(
                lpsum, t.onesK[:, :], t.accLb[:, 512:1024], start=False, stop=False))
            r.op("tensor", lambda e, lpsum=lpsum: e.matmul(
                lpsum, t.onesK[:, :], t.accPb[:, 0:512], start=False, stop=False))
            r.op("tensor", lambda e, lpsum=lpsum: e.matmul(
                lpsum, t.onesK[:, :], t.accPb[:, 512:1024], start=False, stop=True),
                sem=t.sT)
            r.w("scalar", t.sT)
            r.op("scalar", lambda e, lpsum=lpsum: e.activation(
                t.lnt[:, :], lpsum, AF.Ln))
            r.op("scalar", lambda e: e.activation(
                t.rcp[:, :], t.lnt[:, :], AF.Exp, scale=-1.0), sem=t.sS)
            r.w("vector", t.sS)
            fi = h * 4 + q
            ob = t.ob[fi % 2]
            if fi >= 2:
                r.w("vector", t.qA, ob_done[fi - 2])
            r.op("vector", lambda e, ob=ob, opsum=opsum: e.tensor_tensor(
                ob[:, :], opsum, t.rcp[:, :], OP.mult), sem=t.sV)
            t._p4_ovfree = r.cnt[t.sV]
            r.w("scalar", t.sV)
            r.dma("scalar", t.otd_d[h * 128:(h + 1) * 128, q * 512:(q + 1) * 512],
                  ob[:, :], t.qA)
            ob_done.append(r.cnt[t.qA])
        head_done_v[h] = r.cnt[t.sV]
        head_done_t[h] = r.cnt[t.sT]
    barrier(r, t)


def p5_half(r, t, half):
    r.w("sync", t.sT)
    r.dma("sync", t.uTh[:, :].rearrange("p (c q) -> p c q", c=32),
          t.otd_d[:, half * 1024:(half + 1) * 1024]
          .rearrange("(c p) q -> p c q", p=128), t.qS)
    barrier(r, t)
    v_done, s_done, t_done = [], [], []
    w_done = []
    for o in range(32):
        pb = o % 2
        wbf = t.wbf[pb]
        if o >= 2:
            r.w("gpsimd", t.sT, t_done[o - 2])
        r.dma("gpsimd", wbf[:, :].rearrange("p (c m) -> p c m", c=32),
              t.wout_d[:, o * 128:(o + 1) * 128].rearrange("(c p) m -> p c m", p=128),
              t.qG)
        r.w("tensor", t.qG)
        if o >= 2:
            r.w("tensor", t.sS, s_done[o - 2])
        for qt in range(2):
            ps = t.psQ[pb * 2 + qt]
            for ci in range(32):
                r.op("tensor", lambda e, ci=ci, qt=qt, ps=ps, wbf=wbf: e.matmul(
                    ps, wbf[:, ci * 128:(ci + 1) * 128],
                    t.uTh[:, ci * 1024 + qt * 512: ci * 1024 + (qt + 1) * 512],
                    start=(ci == 0), stop=(ci == 31)),
                    sem=(t.sT if (ci == 31 and qt == 1) else None))
        t_done.append(r.cnt[t.sT])
        r.dma("sync", t.xres[pb][:, 0:1024],
              t.xt1_d.rearrange("(c p) q -> p c q", p=128)
              [:, o:o + 1, half * 1024:(half + 1) * 1024], t.qS)
        r.w("scalar", t.sT)
        if o >= 2:
            r.w("scalar", t.qS, w_done[o - 2])
            r.w("scalar", t.sV, v_done[o - 2])
        for qt in range(2):
            ps = t.psQ[pb * 2 + qt]
            r.op("scalar", lambda e, ps=ps, qt=qt, pb=pb, o=o: e.activation(
                t.t1[pb][:, qt * 512:(qt + 1) * 512], ps, AF.Identity),
                sem=(t.sS if qt == 1 else None))
        r.w("vector", t.sS)
        r.w("vector", t.qS)
        r.op("vector", lambda e, pb=pb, o=o: e.scalar_tensor_tensor(
            t.t1[pb][:, 0:1024], t.t1[pb][:, 0:1024], t.bout[:, o:o + 1],
            t.xres[pb][:, 0:1024], OP.add, OP.add), sem=t.sV)
        r.w("sync", t.sV)
        r.dma("sync", t.xt0_d.rearrange("(c p) q -> p c q", p=128)
              [:, o:o + 1, half * 1024:(half + 1) * 1024],
              t.t1[pb][:, 0:1024].unsqueeze(1), t.qS)
        w_done.append(r.cnt[t.qS])
        r.op("vector", lambda e, pb=pb, half=half: e.tensor_tensor(
            t.accS[:, half * 1024:(half + 1) * 1024],
            t.accS[:, half * 1024:(half + 1) * 1024],
            t.t1[pb][:, 0:1024], OP.add))
        r.w("scalar", t.sV)
        r.op("scalar", lambda e, pb=pb: e.activation(
            t.dump[:, pb * 1024:(pb + 1) * 1024], t.t1[pb][:, 0:1024], AF.Square),
            sem=t.sS)
        s_done.append(r.cnt[t.sS])
        r.w("vector", t.sS)
        r.op("vector", lambda e, pb=pb, half=half: e.tensor_tensor(
            t.accQ[:, half * 1024:(half + 1) * 1024],
            t.accQ[:, half * 1024:(half + 1) * 1024],
            t.dump[:, pb * 1024:(pb + 1) * 1024], OP.add), sem=t.sV)
        v_done.append(r.cnt[t.sV])


def build(stop_after="p6", dbg=()):
    nc = bass.Bass("TRN2", target_bir_lowering=False)
    t = T()
    t.stage = 0
    t.x0_d = nc.dram_tensor("x0", [TC, D], F32, kind="ExternalInput").ap()
    t.wdq_d = nc.dram_tensor("wdq", [4096, 1536], F32, kind="ExternalInput").ap()
    t.wuq_d = nc.dram_tensor("wuq", [1536, 4096], F32, kind="ExternalInput").ap()
    t.wdkv_d = nc.dram_tensor("wdkv", [4096, 512], F32, kind="ExternalInput").ap()
    t.wuk_d = nc.dram_tensor("wuk", [512, 4096], F32, kind="ExternalInput").ap()
    t.wuv_d = nc.dram_tensor("wuv", [512, 4096], F32, kind="ExternalInput").ap()
    t.wrq_d = nc.dram_tensor("wrq", [1536, 2048], F32, kind="ExternalInput").ap()
    t.wrk_d = nc.dram_tensor("wrk", [4096, 64], F32, kind="ExternalInput").ap()
    t.wout_d = nc.dram_tensor("wout", [4096, 4096], F32, kind="ExternalInput").ap()
    t.bdq_d = nc.dram_tensor("bdq", [128, 12], F32, kind="ExternalInput").ap()
    t.buq_d = nc.dram_tensor("buq", [128, 32], F32, kind="ExternalInput").ap()
    t.bdkv_d = nc.dram_tensor("bdkv", [128, 4], F32, kind="ExternalInput").ap()
    t.buk_d = nc.dram_tensor("buk", [128, 32], F32, kind="ExternalInput").ap()
    t.buv_d = nc.dram_tensor("buv", [1, 4096], F32, kind="ExternalInput").ap()
    t.brq_d = nc.dram_tensor("brq", [128, 16], F32, kind="ExternalInput").ap()
    t.brk_d = nc.dram_tensor("brk", [64, 1], F32, kind="ExternalInput").ap()
    t.bout_d = nc.dram_tensor("bout", [128, 32], F32, kind="ExternalInput").ap()
    t.fq_d = nc.dram_tensor("fq", [4096, 4096], I32, kind="ExternalInput").ap()
    t.fsc_d = nc.dram_tensor("fsc", [4096, 32], F32, kind="ExternalInput").ap()
    t.fzp_d = nc.dram_tensor("fzp", [4096, 32], F32, kind="ExternalInput").ap()
    t.fb_d = nc.dram_tensor("fb", [128, 32], F32, kind="ExternalInput").ap()
    t.y_d = nc.dram_tensor("Y", [TC, D], F32, kind="ExternalOutput").ap()
    t.xt0_d = nc.dram_tensor("XT0", [D, TC], F32).ap()
    t.xt1_d = nc.dram_tensor("XT1", [D, TC], F32).ap()
    t.wtb_d = nc.dram_tensor("WTB", [32, 32, 128, 128], BF16).ap()
    t.qtd_d = nc.dram_tensor("QTD", [D, TC], BF16).ap()
    t.qrtd_d = nc.dram_tensor("QRTD", [2048, TC], BF16).ap()
    t.otd_d = nc.dram_tensor("OTD", [D, TC], BF16).ap()
    t.kti_d = nc.dram_tensor("KTI", [D, TC], BF16).ap()
    t.vi_d = nc.dram_tensor("VI", [65536, 128], BF16).ap()
    t.kri_d = nc.dram_tensor("KRI", [64, TC], BF16).ap()
    t.ktg_d = nc.dram_tensor("KTG", [2 * D, TC], BF16).ap()
    t.vg_d = nc.dram_tensor("VG", [131072, 128], BF16).ap()
    t.krg_d = nc.dram_tensor("KRG", [128, TC], BF16).ap()

    r = Rec(nc)
    t.qS = r.sem("qS")
    t.qA = r.sem("qA")
    t.qG = r.sem("qG")
    t.qOut = r.sem("qOut")
    t.sT = r.sem("sT")
    t.sV = r.sem("sV")
    t.sS = r.sem("sS")
    t.sP = r.sem("sP")
    t.sCC = r.sem("sCC")
    t.bar = r.sem("bar")

    import contextlib
    stack = contextlib.ExitStack()

    t.PB0 = stack.enter_context(nc.psum_tensor("PB0", [128, 1024], F32))
    t.PB1 = stack.enter_context(nc.psum_tensor("PB1", [128, 1024], F32))
    t.PB2 = stack.enter_context(nc.psum_tensor("PB2", [128, 1024], F32))
    t.PB3 = stack.enter_context(nc.psum_tensor("PB3", [128, 1024], F32))
    t.psQ = [t.PB0[:, 0:512], t.PB0[:, 512:1024],
             t.PB1[:, 0:512], t.PB1[:, 512:1024]]

    def sb(name, shape, dtype):
        return stack.enter_context(nc.sbuf_tensor(name, shape, dtype))

    t.uTh = sb("uTh", [128, 32 * HALF], BF16)
    t.dump = sb("dump", [128, 2048], BF16)
    t.stats = [sb("statsA", [128, 8], F32), sb("statsB", [128, 8], F32)]
    t.idb = sb("idb", [128, 128], BF16)
    t.idf = sb("idf", [128, 128], F32)
    t.ones1 = sb("ones1", [1, 128], BF16)
    t.onesK = sb("onesK", [128, 128], BF16)
    t.bd = sb("bd", [128, 1024], BF16)
    t.s_all = sb("s_all", [128, 1024], BF16)
    t.zst = sb("zst", [32, 4096], BF16)
    t.usum = sb("usum", [32, 2048], BF16)
    t.krs = sb("krs", [64, 1024], BF16)
    for nm, shp in (("bdq", 12), ("buq", 32), ("bdkv", 4), ("buk", 32),
                    ("brq", 16), ("bout", 32), ("fb", 32)):
        setattr(t, nm, sb(nm + "_s", [128, shp], F32))
    t.brk = sb("brk_s", [64, 1], F32)
    t.buqs = sb("buqs", [128, 32], F32)
    t.brqs = sb("brqs", [128, 16], F32)
    t.fbh = sb("fbh", [128, 32], F32)
    t.buvb = sb("buvb", [1, 4096], BF16)
    t.ceps = sb("ceps", [128, 1], F32)
    nc.const_aps.register(F32, EPS, t.ceps[:, 0:1])

    KB = 1024
    ov0 = (nc.sbuf_base + 63) // 64 * 64

    def at(name, shape, dtype, off):
        return nc.alloc_sbuf_tensor_at(name, shape, dtype, offset=ov0 + off)

    t.wstr = [at("wstrA", [128, 4096], F32, 0 * KB),
              at("wstrB", [128, 4096], F32, 16 * KB)]
    t.cqT = at("cqT", [128, 12 * HALF], BF16, 16 * KB)
    t.wbf = [at("wbfA", [128, 4096], BF16, 40 * KB),
             at("wbfB", [128, 4096], BF16, 48 * KB)]
    t.wtsb = [at("wtsbA", [128, 4096], BF16, 56 * KB),
              at("wtsbB", [128, 4096], BF16, 64 * KB)]
    t.eb = [at("ebA", [128, 1024], BF16, 56 * KB),
            at("ebB", [128, 1024], BF16, 58 * KB)]
    t.ckvT = at("ckvT", [128, 4 * HALF], BF16, 60 * KB)
    t.xres = [at("xresA", [128, 1024], F32, 72 * KB),
              at("xresB", [128, 1024], F32, 76 * KB)]
    t.t1 = [at("t1A", [128, 1024], F32, 80 * KB),
            at("t1B", [128, 1024], F32, 84 * KB)]
    t.accS = at("accS", [128, 2048], F32, 88 * KB)
    t.accQ = at("accQ", [128, 2048], F32, 96 * KB)
    t.mb = at("mb", [128, 2048], F32, 88 * KB)
    t.rb = at("rb", [128, 2048], F32, 96 * KB)
    t.ystg = [at("ystgA", [128, 1024], F32, 0 * KB),
              at("ystgB", [128, 1024], F32, 4 * KB)]
    t.sa_f = at("sa_f", [128, 2048], F32, 0 * KB)
    t.zsb = at("zsb", [128, 1024], BF16, 8 * KB)
    t.kb = [at("kbA", [128, 4096], BF16, 0 * KB),
            at("kbB", [128, 4096], BF16, 8 * KB)]
    t.vb = [at("vbA", [128, 4096], BF16, 16 * KB),
            at("vbB", [128, 4096], BF16, 24 * KB)]
    t.qb = [at("qbA", [128, 2048], BF16, 32 * KB),
            at("qbB", [128, 2048], BF16, 36 * KB)]
    t.qrb = [at("qrbA", [128, 2048], BF16, 40 * KB),
             at("qrbB", [128, 2048], BF16, 44 * KB)]
    t.krA = at("krA", [128, 4096], BF16, 48 * KB)
    t.krB = at("krB", [128, 4096], BF16, 56 * KB)
    t.pbuf = [at("pbufA", [128, 1024], BF16, 64 * KB),
              at("pbufB", [128, 1024], BF16, 66 * KB)]
    t.accL = at("accL", [128, 1024], F32, 68 * KB)
    t.accLb = at("accLb", [128, 1024], BF16, 72 * KB)
    t.lnt = at("lnt", [128, 512], F32, 74 * KB)
    t.rcp = at("rcp", [128, 512], F32, 76 * KB)
    t.ob = [at("obA", [128, 512], BF16, 78 * KB),
            at("obB", [128, 512], BF16, 79 * KB)]
    t.accP = at("accP", [128, 1024], F32, 80 * KB)
    t.accPb = at("accPb", [128, 1024], BF16, 84 * KB)

    t.lnv_done = [0] * 32
    t.pj_vdone = [0] * 32
    t.pj_tdone = [0] * 32
    t.pj_sdone = [0] * 32

    p_init(r, t)
    barrier(r, t)

    def alloc_ACC():
        r.op("gpsimd", lambda e: e.memset(t.accS[:, :], 0.0))
        r.op("gpsimd", lambda e: e.memset(t.accQ[:, :], 0.0), sem=t.sP)
        r.w("vector", t.sP)
        r.w("scalar", t.sP)

    alloc_ACC()

    order = ["p0a", "p1a", "p0b", "p1b", "p2s", "p2h0", "p2h1", "p2c1",
             "p3", "p4", "p5", "p6"]
    upto = order.index(stop_after) if stop_after in order else len(order) - 1

    def phase(name):
        return order.index(name) <= upto

    if phase("p0a"):
        p0_half(r, t, 0)
        usum_half(r, t, 0)
        barrier(r, t)
    if phase("p1a"):
        ffn_half(r, t, 0, 1)
        barrier(r, t)
    if phase("p0b"):
        p0_half(r, t, 1)
        usum_half(r, t, 1)
        barrier(r, t)
    if phase("p1b"):
        ffn_half(r, t, 1, 1)
        barrier(r, t)
    if phase("p2s"):
        ln_stats(r, t)
        barrier(r, t)
    if phase("p2h0"):
        p2_half(r, t, 0)
        p2c_half(r, t, 0)
    if phase("p2h1"):
        p2_half(r, t, 1)
    if phase("p2c1"):
        p2c_half(r, t, 1)
    if phase("p3"):
        for a, b in ((t.kti_d, t.ktg_d), (t.vi_d, t.vg_d), (t.kri_d, t.krg_d)):
            r.op("gpsimd", lambda e, a=a, b=b: e.collective_compute(
                "AllGather", OP.bypass,
                replica_groups=[[0, 1], [2, 3], [4, 5], [6, 7]],
                ins=[a.opt()], outs=[b.opt()]), sem=t.sCC)
    if phase("p4"):
        p4(r, t)
    if phase("p5"):
        alloc_ACC()
        barrier(r, t)
        p5_half(r, t, 0)
        p5_half(r, t, 1)
        barrier(r, t)
    if phase("p6"):
        ln_stats(r, t)
        barrier(r, t)
        for half in range(2):
            ln_apply_half(r, t, half, t.xt0_d)
            barrier(r, t)
            usum_half(r, t, half)
            barrier(r, t)
            ffn_half(r, t, half, 2)
            barrier(r, t)

    dram_dbg = {"x1": t.xt1_d, "x2": t.xt0_d, "qt": t.qtd_d, "qrt": t.qrtd_d,
                "kt": t.kti_d, "kri": t.kri_d, "ot": t.otd_d, "vi": t.vi_d,
                "ktg": t.ktg_d, "krg": t.krg_d, "vg": t.vg_d, "wtb": t.wtb_d,
                "x0t": t.xt0_d}
    sbuf_dbg = {"uTh": t.uTh, "usum": t.usum, "mb": t.mb, "rb": t.rb,
                "stats0": t.stats[0], "stats1": t.stats[1], "accS": t.accS, "accQ": t.accQ,
                "zst": t.zst, "s_all": t.s_all, "cqT": t.cqT, "ckvT": t.ckvT}
    for name in dbg:
        if name in dram_dbg:
            src = dram_dbg[name]
            dbg_out = nc.dram_tensor("DBG_" + name, list(src.shape), src.dtype,
                                     kind="ExternalOutput").ap()
            r.dma("sync", dbg_out, src, t.qOut)
        else:
            h = sbuf_dbg[name]
            ap = h[:, :]
            dbg_out = nc.dram_tensor("DBG_" + name, list(ap.shape), ap.dtype,
                                     kind="ExternalOutput").ap()
            r.dma("sync", dbg_out, ap, t.qOut)
    r.w("sync", t.qOut)
    r.w("sync", t.qS)
    r.w("scalar", t.qA)
    r.w("gpsimd", t.qG)

    with nc.Block() as block:
        @block.sync
        def _(eng):
            for fn in r.steps["sync"]:
                fn(eng)

        @block.gpsimd
        def _(eng):
            for fn in r.steps["gpsimd"]:
                fn(eng)

        @block.vector
        def _(eng):
            for fn in r.steps["vector"]:
                fn(eng)

        @block.scalar
        def _(eng):
            for fn in r.steps["scalar"]:
                fn(eng)

        @block.tensor
        def _(eng):
            for fn in r.steps["tensor"]:
                fn(eng)

    stack.close()
    return nc


def _chunked(b, n):
    return np.ascontiguousarray(b.reshape(n, 128).T.astype(np.float32))


def make_inputs(inputs):
    m = {}
    m["wdq"] = np.ascontiguousarray(inputs["L0_attn_w_dq"], np.float32)
    m["wuq"] = np.ascontiguousarray(inputs["L0_attn_w_uq"], np.float32)
    m["wdkv"] = np.ascontiguousarray(inputs["L0_attn_w_dkv"], np.float32)
    m["wuk"] = np.ascontiguousarray(inputs["L0_attn_w_uk"], np.float32)
    m["wuv"] = np.ascontiguousarray(inputs["L0_attn_w_uv"], np.float32)
    m["wrq"] = np.ascontiguousarray(inputs["L0_attn_w_rq"], np.float32)
    m["wrk"] = np.ascontiguousarray(inputs["L0_attn_w_rk"], np.float32)
    m["wout"] = np.ascontiguousarray(inputs["L0_attn_w_out"], np.float32)
    m["bdq"] = _chunked(np.asarray(inputs["L0_attn_b_dq"]), 12)
    m["buq"] = _chunked(np.asarray(inputs["L0_attn_b_uq"]), 32)
    m["bdkv"] = _chunked(np.asarray(inputs["L0_attn_b_dkv"]), 4)
    m["buk"] = _chunked(np.asarray(inputs["L0_attn_b_uk"]), 32)
    m["buv"] = np.ascontiguousarray(
        np.asarray(inputs["L0_attn_b_uv"], np.float32).reshape(1, 4096))
    m["brq"] = _chunked(np.asarray(inputs["L0_attn_b_rq"]), 16)
    m["brk"] = np.ascontiguousarray(
        np.asarray(inputs["L0_attn_b_rk"], np.float32).reshape(64, 1))
    m["bout"] = _chunked(np.asarray(inputs["L0_attn_b_out"]), 32)
    m["fq"] = np.ascontiguousarray(inputs["L0_ffn_weight_q"], np.int32)
    m["fsc"] = np.ascontiguousarray(inputs["L0_ffn_scale"], np.float32)
    m["fzp"] = np.ascontiguousarray(inputs["L0_ffn_zero_point"], np.float32)
    m["fb"] = _chunked(np.asarray(inputs["L0_ffn_bias"]), 32)
    x = np.asarray(inputs["L0_attn_sequence"], np.float32)
    in_maps = []
    for c in range(8):
        b, hf = c // 2, c % 2
        mm = dict(m)
        mm["x0"] = np.ascontiguousarray(x[b, hf * TC:(hf + 1) * TC, :])
        in_maps.append(mm)
    return in_maps


def kernel(**inputs):
    in_maps = make_inputs(inputs)
    nc = build()
    res = run_bass_kernel_spmd(nc, in_maps, core_ids=list(range(8)))
    out = np.empty((4, 4096, 4096), np.float32)
    for c in range(8):
        out[c // 2, (c % 2) * TC:(c % 2 + 1) * TC, :] = res.results[c]["Y"]
    return out


# version 06
# speedup vs baseline: 1.3179x; 1.0647x over previous
import math
import numpy as np
import ml_dtypes
import concourse.bass as bass
import concourse.mybir as mybir
from concourse.bass_utils import run_bass_kernel_spmd

F32 = mybir.dt.float32
BF16 = mybir.dt.bfloat16
I32 = mybir.dt.int32
AF = mybir.ActivationFunctionType
OP = mybir.AluOpType

ENGINES = ("sync", "gpsimd", "vector", "scalar", "tensor")
SCALER = 1.0 / math.sqrt(192.0)
EPS = 1e-5
D = 4096
TC = 2048
HALF = 1024
NCH = 32


class Rec:

    def __init__(self, nc):
        self.nc = nc
        self.steps = {e: [] for e in ENGINES}
        self.cnt = {}
        self._guards = []

    def sem(self, name):
        g = self.nc.semaphore(name)
        h = g.__enter__()
        self._guards.append(g)
        self.cnt[h] = 0
        return h

    def em(self, eng, fn):
        self.steps[eng].append(fn)

    def w(self, eng, sem, val=None):
        v = self.cnt[sem] if val is None else val
        self.em(eng, lambda e, s=sem, v=v: e.wait_ge(s, v))

    def dma(self, queue, out, in_, sem, transpose=False, slow=False):
        self.cnt[sem] += 16

        def fn(e, out=out, in_=in_, t=transpose, s=sem, sl=slow):
            kw = {}
            if sl:
                kw["allow_slow_non_contiguous"] = True
            if t:
                e.dma_start(out=out, in_=in_, transpose=True, **kw).then_inc(s, 16)
            else:
                e.dma_start(out=out, in_=in_, **kw).then_inc(s, 16)

        self.em(queue, fn)
        return self.cnt[sem]

    def op(self, eng, builder, sem=None, inc=1):
        if sem is not None:
            self.cnt[sem] += inc

        def fn(e, b=builder, s=sem, i=inc):
            ins = b(e)
            if s is not None:
                ins.then_inc(s, i)

        self.em(eng, fn)
        return self.cnt[sem] if sem is not None else None


class T:
    pass


def _alloc_group(nc, specs):
    out = {}
    guards = []
    for attr, name, shape, dtype in specs:
        g = nc.sbuf_tensor(name, shape, dtype)
        h = g.__enter__()
        guards.append(g)
        out[attr] = h
    return out, guards


def _free_group(guards):
    for g in reversed(guards):
        g.__exit__(None, None, None)


def barrier(r, t):
    t.stage += 1
    r.w("sync", t.qS)
    r.w("scalar", t.qA)
    r.w("gpsimd", t.qG)
    for e in ENGINES:
        r.op(e, lambda en: en.nop(), sem=t.bar, inc=1)
    v = 5 * t.stage
    assert r.cnt[t.bar] == v
    for e in ENGINES:
        r.w(e, t.bar, v)


def p_init(r, t):
    nc = r.nc
    r.op("gpsimd", lambda e: e.memset(t.ceps[:, :], EPS))
    r.op("gpsimd", lambda e: e.memset(t.idb[:, :], 0.0))
    r.op("gpsimd", lambda e: e.affine_select(
        out=t.idb[:, :], in_=t.idb[:, :], compare_op=OP.not_equal, fill=1.0,
        base=0, pattern=[[-1, 128]], channel_multiplier=1))
    r.op("gpsimd", lambda e: e.memset(t.idf[:, :], 0.0))
    r.op("gpsimd", lambda e: e.affine_select(
        out=t.idf[:, :], in_=t.idf[:, :], compare_op=OP.not_equal, fill=1.0,
        base=0, pattern=[[-1, 128]], channel_multiplier=1))
    r.op("gpsimd", lambda e: e.memset(t.ones1[:, :], 1.0))
    r.op("gpsimd", lambda e: e.memset(t.onesK[:, :], 1.0))
    r.op("gpsimd", lambda e: e.memset(t.bd[:, :], 0.0))
    for g in range(32):
        r.op("gpsimd", lambda e, g=g: e.memset(t.bd[:, g * 32 + g: g * 32 + g + 1], 1.0))
    r.op("gpsimd", lambda e: e.engine_nop(), sem=t.sP)
    for nm in ("bdq", "buq", "bdkv", "buk", "brq", "bout", "fb"):
        r.dma("gpsimd", getattr(t, nm)[:, :], getattr(t, nm + "_d"), t.qG)
    r.dma("gpsimd", t.buvb[:, :], t.buv_d, t.qG)
    r.dma("gpsimd", t.brk[:, :], t.brk_d, t.qG)
    r.dma("gpsimd", t.sa_f[:, 0:1024].rearrange("p (c g) -> p c g", c=32),
          t.fsc_d.rearrange("(c p) g -> p c g", p=128), t.qG)
    r.dma("gpsimd", t.sa_f[:, 1024:2048].rearrange("p (c g) -> p c g", c=32),
          t.fzp_d.rearrange("(c p) g -> p c g", p=128), t.qG)
    r.w("vector", t.qG)
    r.w("gpsimd", t.qG)
    r.op("vector", lambda e: e.tensor_copy(t.s_all[:, :], t.sa_f[:, 0:1024]))
    r.op("vector", lambda e: e.tensor_tensor(
        t.zsb[:, :], t.sa_f[:, 0:1024], t.sa_f[:, 1024:2048], OP.mult), sem=t.sV)
    r.op("vector", lambda e: e.tensor_scalar(
        t.buqs[:, :], t.buq[:, :], SCALER, None, OP.mult))
    r.op("vector", lambda e: e.tensor_scalar(
        t.brqs[:, :], t.brq[:, :], SCALER, None, OP.mult))
    r.op("vector", lambda e: e.tensor_scalar(
        t.fbh[:, :], t.fb[:, :], 0.5, None, OP.mult))
    r.w("tensor", t.sV)
    r.w("tensor", t.sP)
    for o in range(32):
        pb = t.PB2 if (o % 2 == 0) else t.PB3
        dst = pb[0:32, 0:64].bitcast(BF16)
        r.op("tensor", lambda e, o=o, dst=dst: e.transpose(
            dst, t.zsb[:, o * 32:(o + 1) * 32], t.idb[:, :]), sem=t.sT)
        r.w("vector", t.sT)
        r.op("vector", lambda e, o=o, src=dst: e.tensor_copy(
            t.zst[0:32, o * 128:(o + 1) * 128], src), sem=t.sV)
        r.w("tensor", t.sV)


def p0_half(r, t, half):
    v_done, s_done, t_done = [], [], []
    for tc in range(8):
        g = half * 8 + tc
        pb = tc % 2
        xnat = t.wstr[pb]
        u1n = t.wbf[pb]
        st = t.stats[pb]
        if tc >= 2:
            r.w("sync", t.sV, v_done[tc - 2])
            r.w("sync", t.sS, s_done[tc - 2])
            r.w("sync", t.sT, t_done[tc - 2])
        r.dma("sync", xnat[:, :], t.x0_d[g * 128:(g + 1) * 128, :], t.qS)
        r.w("vector", t.qS)
        r.op("vector", lambda e, x=xnat, st=st: e.reduce_sum(
            st[:, 0:1], x[:, :], axis=mybir.AxisListType.X), sem=t.sV)
        r.w("vector", t.sV)
        r.op("vector", lambda e, st=st: e.tensor_scalar(
            st[:, 1:2], st[:, 0:1], 1.0 / D, None, OP.mult), sem=t.sV)
        r.w("vector", t.sV)
        r.w("scalar", t.qS)
        r.op("scalar", lambda e, x=xnat, st=st: e.activation(
            t.dump[:, 0:2048], x[:, 0:2048], AF.Square,
            accum_out=st[:, 2:3]))
        r.op("scalar", lambda e, x=xnat, st=st: e.activation(
            t.dump[:, 0:2048], x[:, 2048:4096], AF.Square,
            accum_out=st[:, 3:4]), sem=t.sS)
        r.w("vector", t.sS)
        r.op("vector", lambda e, st=st: e.tensor_tensor(
            st[:, 4:5], st[:, 2:3], st[:, 3:4], OP.add), sem=t.sV)
        r.w("vector", t.sV)
        r.op("vector", lambda e, st=st: e.tensor_scalar(
            st[:, 4:5], st[:, 4:5], 1.0 / D, None, OP.mult), sem=t.sV)
        r.w("vector", t.sV)
        r.op("vector", lambda e, st=st: e.tensor_tensor(
            st[:, 5:6], st[:, 1:2], st[:, 1:2], OP.mult), sem=t.sV)
        r.w("vector", t.sV)
        r.op("vector", lambda e, st=st: e.tensor_tensor(
            st[:, 4:5], st[:, 4:5], st[:, 5:6], OP.subtract), sem=t.sV)
        r.w("scalar", t.sV)
        r.op("scalar", lambda e, st=st: e.activation(
            st[:, 6:7], st[:, 4:5], AF.Ln, bias=EPS), sem=t.sS)
        r.w("scalar", t.sS)
        r.op("scalar", lambda e, st=st: e.activation(
            st[:, 7:8], st[:, 6:7], AF.Exp, scale=-0.5), sem=t.sS)
        r.w("vector", t.sS)
        r.op("vector", lambda e, x=xnat, u=u1n, st=st: e.tensor_scalar(
            u[:, :], x[:, :], st[:, 1:2], st[:, 7:8], OP.subtract, OP.mult), sem=t.sV)
        uv = r.cnt[t.sV]
        r.w("tensor", t.qS)
        for grp in range(4):
            pbk = t.PB2 if grp % 2 == 0 else t.PB3
            for j in range(8):
                d = grp * 8 + j
                r.op("tensor", lambda e, x=xnat, d=d, pbk=pbk, j=j: e.transpose(
                    pbk[:, j * 128:(j + 1) * 128], x[:, d * 128:(d + 1) * 128],
                    t.idf[:, :]), sem=(t.sT if j == 7 else None))
            r.w("scalar", t.sT)
            r.w("scalar", t.qS)
            r.op("scalar", lambda e, pbk=pbk, pb=pb: e.activation(
                t.t1[pb][:, 0:1024], pbk[:, :], AF.Identity), sem=t.sS)
            r.w("sync", t.sS)
            r.dma("sync", t.xt0_d.rearrange("(c p) q -> p c q", p=128)
                  [:, grp * 8:(grp + 1) * 8, g * 128:(g + 1) * 128],
                  t.t1[pb][:, 0:1024].rearrange("p (c q) -> p c q", c=8), t.qS)
            r.w("tensor", t.sS)
        r.w("tensor", t.sV, uv)
        for grp in range(2):
            pbk = t.PB2 if grp == 0 else t.PB3
            dstall = pbk[:, 0:1024].bitcast(BF16)
            for j in range(16):
                d = grp * 16 + j
                dst = pbk[:, j * 64:(j + 1) * 64].bitcast(BF16)
                r.op("tensor", lambda e, u=u1n, d=d, dst=dst: e.transpose(
                    dst, u[:, d * 128:(d + 1) * 128], t.idb[:, :]),
                    sem=(t.sT if j == 15 else None))
            r.w("vector", t.sT)
            dd = grp * 16
            out = t.uTh[:, :].rearrange("p (c q) -> p c q", c=32)[
                :, dd:dd + 16, tc * 128:(tc + 1) * 128]
            r.op("vector", lambda e, out=out, src=dstall: e.tensor_copy(
                out, src.rearrange("p (c q) -> p c q", c=16)), sem=t.sV)
            r.w("tensor", t.sV)
        v_done.append(r.cnt[t.sV])
        s_done.append(r.cnt[t.sS])
        t_done.append(r.cnt[t.sT])


def usum_half(r, t, half):
    for qt in range(2):
        ps = t.psQ[qt][0:32, :]
        for i in range(32):
            r.op("tensor", lambda e, i=i, qt=qt, ps=ps: e.matmul(
                ps, t.bd[:, i * 32:(i + 1) * 32],
                t.uTh[:, i * 1024 + qt * 512: i * 1024 + (qt + 1) * 512],
                start=(i == 0), stop=(i == 31)),
                sem=(t.sT if i == 31 else None))
        r.w("scalar", t.sT)
        r.op("scalar", lambda e, qt=qt, ps=ps, half=half: e.activation(
            t.usum[0:32, half * 1024 + qt * 512: half * 1024 + (qt + 1) * 512],
            ps[0:32, :], AF.Identity, scale=-1.0), sem=t.sS)
        r.w("tensor", t.sS)


def ffn_half(r, t, half, app):
    build_w = (app == 1 and half == 0)
    v_done, s_done, t_done = [], [], []
    w_done = []
    for o in range(32):
        pb = o % 2
        wt = t.wtsb[pb] if build_w else t.wtsb[o % 3]
        if o >= 2:
            r.w("sync", t.sT, t_done[o - 2])
            r.w("sync", t.sS, s_done[o - 2])
            r.w("sync", t.sV, v_done[o - 2])
            r.w("scalar", t.sT, t_done[o - 2])
            r.w("gpsimd", t.sV, v_done[o - 2])
        if build_w:
            wq = t.wbf[pb]
            r.dma("gpsimd", wq[:, :], t.fq_d[o * 128:(o + 1) * 128, :], t.qG)
            r.w("vector", t.qG)
            r.op("vector", lambda e, wq=wq, o=o: e.tensor_tensor(
                wq[:, :].rearrange("p (g m) -> p g m", g=32), wq[:, :].rearrange("p (g m) -> p g m", g=32),
                t.s_all[:, o * 32:(o + 1) * 32].unsqueeze(2).broadcast_to((128, 32, 128)),
                OP.mult), sem=t.sV)
            r.w("tensor", t.sV)
            for grp in range(2):
                pbk = t.PB2 if grp == 0 else t.PB3
                for j in range(16):
                    i = grp * 16 + j
                    dst = pbk[:, j * 64:(j + 1) * 64].bitcast(BF16)
                    r.op("tensor", lambda e, wq=wq, i=i, dst=dst: e.transpose(
                        dst, wq[:, i * 128:(i + 1) * 128], t.idb[:, :]),
                        sem=(t.sT if j == 15 else None))
                r.w("vector", t.sT)
                r.op("vector", lambda e, wt=wt, grp=grp, pbk=pbk: e.tensor_copy(
                    wt[:, grp * 2048:(grp + 1) * 2048], pbk[:, 0:1024].bitcast(BF16)),
                    sem=t.sV)
                r.w("tensor", t.sV)
            r.w("scalar", t.sV)
            r.dma("scalar", t.wtb_d[o].rearrange("i p m -> p i m"),
                  wt[:, :].rearrange("p (i m) -> p i m", i=32), t.qA)
        else:
            if o >= 3:
                r.w("gpsimd", t.sT, t_done[o - 3])
            r.dma("gpsimd", wt[:, :].rearrange("p (i m) -> p i m", i=32),
                  t.wtb_d[o].rearrange("i p m -> p i m"), t.qG)
            r.w("tensor", t.qG)
        for qt in range(2):
            ps = t.psQ[pb * 2 + qt]
            for i in range(32):
                r.op("tensor", lambda e, i=i, qt=qt, ps=ps, wt=wt: e.matmul(
                    ps, wt[:, i * 128:(i + 1) * 128],
                    t.uTh[:, i * 1024 + qt * 512: i * 1024 + (qt + 1) * 512],
                    start=(i == 0), stop=False))
            r.op("tensor", lambda e, qt=qt, ps=ps, o=o, half=half: e.matmul(
                ps, t.zst[0:32, o * 128:(o + 1) * 128],
                t.usum[0:32, half * 1024 + qt * 512: half * 1024 + (qt + 1) * 512],
                start=False, stop=True), sem=(t.sT if qt == 1 else None))
        t_done.append(r.cnt[t.sT])
        res_src = t.xt0_d
        r.dma("sync", t.xres[pb][:, 0:1024],
              res_src.rearrange("(c p) q -> p c q", p=128)
              [:, o:o + 1, half * 1024:(half + 1) * 1024], t.qS)
        r.w("scalar", t.sT)
        if o >= 2:
            r.w("scalar", t.qS if app == 1 else t.qOut, w_done[o - 2])
            r.w("scalar", t.sV, v_done[o - 2])
        for qt in range(2):
            ps = t.psQ[pb * 2 + qt]
            r.op("scalar", lambda e, qt=qt, ps=ps, o=o, pb=pb: e.activation(
                t.t1[pb][:, qt * 512:(qt + 1) * 512], ps, AF.Identity,
                scale=0.5), sem=(t.sS if qt == 1 else None))
        r.w("vector", t.sS)
        r.w("vector", t.qS)
        r.op("vector", lambda e, pb=pb, o=o: e.scalar_tensor_tensor(
            t.t1[pb][:, 0:1024], t.t1[pb][:, 0:1024], t.fbh[:, o:o + 1],
            t.xres[pb][:, 0:1024], OP.add, OP.add), sem=t.sV)
        r.w("sync", t.sV)
        if app == 1:
            r.dma("sync", t.xt1_d.rearrange("(c p) q -> p c q", p=128)
                  [:, o:o + 1, half * 1024:(half + 1) * 1024],
                  t.t1[pb][:, 0:1024].unsqueeze(1), t.qS)
            w_done.append(r.cnt[t.qS])
            r.op("vector", lambda e, pb=pb, half=half: e.tensor_tensor(
                t.accS[:, half * 1024:(half + 1) * 1024],
                t.accS[:, half * 1024:(half + 1) * 1024],
                t.t1[pb][:, 0:1024], OP.add))
            r.w("scalar", t.sV)
            r.op("scalar", lambda e, pb=pb: e.activation(
                t.dump[:, pb * 1024:(pb + 1) * 1024], t.t1[pb][:, 0:1024],
                AF.Square), sem=t.sS)
            r.w("vector", t.sS)
            r.op("vector", lambda e, pb=pb, half=half: e.tensor_tensor(
                t.accQ[:, half * 1024:(half + 1) * 1024],
                t.accQ[:, half * 1024:(half + 1) * 1024],
                t.dump[:, pb * 1024:(pb + 1) * 1024], OP.add), sem=t.sV)
        else:
            r.w("tensor", t.sV)
            pbk = t.PB2 if pb == 0 else t.PB3
            for j in range(8):
                r.op("tensor", lambda e, pb=pb, j=j, pbk=pbk: e.transpose(
                    pbk[:, j * 128:(j + 1) * 128],
                    t.t1[pb][:, j * 128:(j + 1) * 128], t.idf[:, :]),
                    sem=(t.sT if j == 7 else None))
            t_done[-1] = r.cnt[t.sT]
            r.w("vector", t.sT)
            if o >= 2:
                r.w("vector", t.qOut, w_done[o - 2])
            r.op("vector", lambda e, pb=pb, pbk=pbk: e.tensor_copy(
                t.ystg[pb][:, 0:1024], pbk[:, :]), sem=t.sV)
            r.w("sync", t.sV)
            r.dma("sync", t.y_d.rearrange("(c p) q -> p c q", p=128)
                  [:, half * 8:(half + 1) * 8, o * 128:(o + 1) * 128],
                  t.ystg[pb][:, 0:1024].rearrange("p (c q) -> p c q", c=8),
                  t.qOut)
            w_done.append(r.cnt[t.qOut])
        v_done.append(r.cnt[t.sV])
        s_done.append(r.cnt[t.sS])


def ln_stats(r, t):
    r.op("vector", lambda e: e.tensor_copy(t.dump[:, 0:2048], t.accS[:, :]),
         sem=t.sV)
    r.w("tensor", t.sV)
    for qt in range(4):
        psm = t.psQ[qt % 2]
        r.op("tensor", lambda e, qt=qt, psm=psm: e.matmul(
            psm, t.onesK[:, :], t.dump[:, qt * 512:(qt + 1) * 512],
            start=True, stop=True), sem=t.sT)
        r.w("vector", t.sT)
        r.op("vector", lambda e, qt=qt, psm=psm: e.tensor_scalar(
            t.mb[:, qt * 512:(qt + 1) * 512], psm, 1.0 / D, None, OP.mult),
            sem=t.sV)
        r.w("tensor", t.sV)
    r.op("vector", lambda e: e.tensor_copy(t.dump[:, 0:2048], t.accQ[:, :]),
         sem=t.sV)
    r.w("tensor", t.sV)
    for qt in range(4):
        psv = t.psQ[2 + qt % 2]
        r.op("tensor", lambda e, qt=qt, psv=psv: e.matmul(
            psv, t.onesK[:, :], t.dump[:, qt * 512:(qt + 1) * 512],
            start=True, stop=True), sem=t.sT)
        r.w("vector", t.sT)
        r.op("vector", lambda e, qt=qt, psv=psv: e.tensor_scalar(
            t.rb[:, qt * 512:(qt + 1) * 512], psv, 1.0 / D, None, OP.mult))
        r.op("vector", lambda e, qt=qt: e.tensor_tensor(
            t.t1[0][:, 0:512], t.mb[:, qt * 512:(qt + 1) * 512],
            t.mb[:, qt * 512:(qt + 1) * 512], OP.mult))
        r.op("vector", lambda e, qt=qt: e.tensor_tensor(
            t.rb[:, qt * 512:(qt + 1) * 512], t.rb[:, qt * 512:(qt + 1) * 512],
            t.t1[0][:, 0:512], OP.subtract), sem=t.sV)
        r.w("scalar", t.sV)
        r.op("scalar", lambda e, qt=qt: e.activation(
            t.rb[:, qt * 512:(qt + 1) * 512], t.rb[:, qt * 512:(qt + 1) * 512],
            AF.Ln, bias=EPS))
        r.op("scalar", lambda e, qt=qt: e.activation(
            t.rb[:, qt * 512:(qt + 1) * 512], t.rb[:, qt * 512:(qt + 1) * 512],
            AF.Exp, scale=-0.5), sem=t.sS)
        r.w("tensor", t.sS)


def ln_apply_half(r, t, half, src_dram):
    for dch in range(32):
        pb = dch % 2
        if dch >= 2:
            r.w("sync", t.sV, t.lnv_done[dch - 2])
        r.dma("sync", t.xres[pb][:, 0:1024],
              src_dram.rearrange("(c p) q -> p c q", p=128)
              [:, dch:dch + 1, half * 1024:(half + 1) * 1024], t.qS)
        r.w("vector", t.qS)
        r.op("vector", lambda e, pb=pb, half=half: e.tensor_tensor(
            t.t1[pb][:, 0:1024], t.xres[pb][:, 0:1024],
            t.mb[:, half * 1024:(half + 1) * 1024], OP.subtract))
        r.op("vector", lambda e, pb=pb, dch=dch, half=half: e.tensor_tensor(
            t.uTh[:, dch * 1024:(dch + 1) * 1024], t.t1[pb][:, 0:1024],
            t.rb[:, half * 1024:(half + 1) * 1024], OP.mult), sem=t.sV)
        t.lnv_done[dch] = r.cnt[t.sV]


def proj(r, t, half, w_dram, n_in, n_out, out_kind, bias_ap=None, scale=1.0,
         dst=None, dst_dram=None, name=""):
    barrier(r, t)
    nin_ch = n_in // 128
    nout_ch = n_out // 128
    rhs_buf = t.uTh if name in ("cq", "ckv", "krope") else (
        t.cqT if name in ("qup", "qrope") else t.ckvT)
    pj_adone = [0] * nout_ch
    for oc in range(nout_ch):
        pb = oc % 2
        wbf = t.wbf3[oc % 3]
        if oc >= 3:
            r.w("gpsimd", t.sT, t.pj_tdone[oc - 3])
        r.dma("gpsimd", wbf[:, 0:nin_ch * 128].rearrange("p (c m) -> p c m", c=nin_ch),
              w_dram[:, oc * 128:(oc + 1) * 128].rearrange("(c p) m -> p c m", p=128),
              t.qG)
        r.w("tensor", t.qG)
        if oc >= 2:
            r.w("tensor", t.sS, t.pj_sdone[oc - 2])
        for qt in range(2):
            ps = t.psQ[pb * 2 + qt]
            for ci in range(nin_ch):
                r.op("tensor", lambda e, ci=ci, qt=qt, ps=ps, wbf=wbf: e.matmul(
                    ps, wbf[:, ci * 128:(ci + 1) * 128],
                    rhs_buf[:, ci * 1024 + qt * 512: ci * 1024 + (qt + 1) * 512],
                    start=(ci == 0), stop=(ci == nin_ch - 1)),
                    sem=(t.sT if (ci == nin_ch - 1 and qt == 1) else None))
        t.pj_tdone[oc] = r.cnt[t.sT]
        r.w("scalar", t.sT)
        if out_kind == "dram" and oc >= 2:
            r.w("scalar", t.qA, pj_adone[oc - 2])
        for qt in range(2):
            ps = t.psQ[pb * 2 + qt]
            if out_kind == "sbuf":
                out_ap = dst[:, oc * 1024 + qt * 512: oc * 1024 + (qt + 1) * 512]
            else:
                out_ap = t.eb[pb][:, qt * 512:(qt + 1) * 512]
            if bias_ap is not None:
                r.op("scalar", lambda e, ps=ps, out_ap=out_ap, oc=oc: e.activation(
                    out_ap, ps, AF.Identity, scale=scale,
                    bias=bias_ap[:, oc:oc + 1]), sem=(t.sS if qt == 1 else None))
            else:
                r.op("scalar", lambda e, ps=ps, out_ap=out_ap: e.activation(
                    out_ap, ps, AF.Identity, scale=scale),
                    sem=(t.sS if qt == 1 else None))
        t.pj_sdone[oc] = r.cnt[t.sS]
        if out_kind == "dram":
            r.dma("scalar", dst_dram.rearrange("(c p) q -> p c q", p=128)
                  [:, oc:oc + 1, half * 1024:(half + 1) * 1024],
                  t.eb[pb][:, 0:1024].unsqueeze(1), t.qA)
            pj_adone[oc] = r.cnt[t.qA]


def p2_krope(r, t, half):
    barrier(r, t)
    r.dma("gpsimd", t.wbf[0][:, 0:2048].rearrange("p (c m) -> p c m", c=32),
          t.wrk_d.rearrange("(c p) m -> p c m", p=128), t.qG)
    r.w("tensor", t.qG)
    for qt in range(2):
        ps = t.psQ[qt][0:64, :]
        for ci in range(32):
            r.op("tensor", lambda e, ci=ci, ps=ps, qt=qt: e.matmul(
                ps, t.wbf[0][:, ci * 64:(ci + 1) * 64],
                t.uTh[:, ci * 1024 + qt * 512: ci * 1024 + (qt + 1) * 512],
                start=(ci == 0), stop=(ci == 31)),
                sem=(t.sT if ci == 31 else None))
        r.w("scalar", t.sT)
        r.op("scalar", lambda e, qt=qt, ps=ps: e.activation(
            t.krs[0:64, qt * 512:(qt + 1) * 512], ps, AF.Identity,
            bias=t.brk[:, 0:1]), sem=t.sS)
        r.w("tensor", t.sS)
    r.w("scalar", t.sS)
    r.dma("scalar", t.kri_d[:, half * 1024:(half + 1) * 1024],
          t.krs[0:64, 0:1024], t.qA)


def p2_vup(r, t, half):
    barrier(r, t)
    va_done = [0] * 16
    vs_done = [0] * 16
    for dvb in range(8):
        pb = dvb % 2
        if dvb >= 2:
            r.w("gpsimd", t.sT, t.pj_tdone[dvb - 2])
        r.dma("gpsimd", t.wbf[pb][:, 0:2048].rearrange("p (c m) -> p c m", c=4),
              t.wuv_d[:, dvb * 512:(dvb + 1) * 512].rearrange("(c p) m -> p c m", p=128),
              t.qG)
        r.w("tensor", t.qG)
        if dvb >= 2:
            r.w("tensor", t.sS, t.pj_sdone[dvb - 2])
        for tk in range(8):
            ps = t.psQ[pb * 2 + tk % 2]
            if tk >= 2:
                r.w("tensor", t.sS, vs_done[tk - 2])
            for ci in range(4):
                r.op("tensor", lambda e, ci=ci, tk=tk, ps=ps, pb=pb: e.matmul(
                    ps, t.ckvT[:, ci * 1024 + tk * 128: ci * 1024 + tk * 128 + 128],
                    t.wbf[pb][:, ci * 512:(ci + 1) * 512],
                    start=(ci == 0), stop=False))
            r.op("tensor", lambda e, tk=tk, ps=ps, dvb=dvb: e.matmul(
                ps, t.ones1[:, :],
                t.buvb[:, dvb * 512:(dvb + 1) * 512],
                start=False, stop=True), sem=t.sT)
            r.w("scalar", t.sT)
            if tk >= 2:
                r.w("scalar", t.qA, va_done[tk - 2])
            r.op("scalar", lambda e, ps=ps, pb=pb, tk=tk: e.activation(
                t.eb[pb][:, (tk % 2) * 512:(tk % 2) * 512 + 512], ps, AF.Identity),
                sem=t.sS)
            vs_done[tk] = r.cnt[t.sS]
            t.pj_sdone[dvb] = r.cnt[t.sS]
            glob_tk = half * 8 + tk
            out = t.vi_d.rearrange("(h tt p) d -> p h tt d", p=128, tt=16)[
                :, dvb * 4:(dvb + 1) * 4, glob_tk, :]
            r.dma("scalar", out,
                  t.eb[pb][:, (tk % 2) * 512:(tk % 2) * 512 + 512]
                  .rearrange("p (h d) -> p h d", h=4), t.qA)
            va_done[tk] = r.cnt[t.qA]
        t.pj_tdone[dvb] = r.cnt[t.sT]


def p2_half(r, t, half):
    ln_apply_half(r, t, half, t.xt1_d)
    barrier(r, t)
    proj(r, t, half, t.wdq_d, 4096, 1536, "sbuf", bias_ap=t.bdq,
         dst=t.cqT, name="cq")
    proj(r, t, half, t.wdkv_d, 4096, 512, "sbuf", bias_ap=t.bdkv,
         dst=t.ckvT, name="ckv")
    p2_krope(r, t, half)
    barrier(r, t)
    proj(r, t, half, t.wuk_d, 512, 4096, "dram", bias_ap=t.buk,
         dst_dram=t.kti_d, name="kup")
    p2_vup(r, t, half)
    barrier(r, t)


def p2c_half(r, t, half):
    proj(r, t, half, t.wuq_d, 1536, 4096, "dram", bias_ap=t.buqs, scale=SCALER,
         dst_dram=t.qtd_d, name="qup")
    proj(r, t, half, t.wrq_d, 1536, 2048, "dram", bias_ap=t.brqs, scale=SCALER,
         dst_dram=t.qrtd_d, name="qrope")
    barrier(r, t)


V_KK = [k for k in range(16) if k % 3 != 2]
G_KK = [k for k in range(16) if k % 3 == 2]


def p4(r, t):
    r.op("gpsimd", lambda e: e.memset(t.krA[:, :], 0.0))
    r.op("gpsimd", lambda e: e.memset(t.krB[:, :], 0.0), sem=t.sP)
    r.w("sync", t.sP)
    r.w("sync", t.sCC, 3)
    r.dma("sync", t.krA[0:64, 0:2048], t.krg_d[0:64, :], t.qS)
    r.dma("sync", t.krA[0:64, 2048:4096], t.krg_d[64:128, :], t.qS)
    r.dma("sync", t.krB[64:128, 0:2048], t.krg_d[0:64, :], t.qS)
    r.dma("sync", t.krB[64:128, 2048:4096], t.krg_d[64:128, :], t.qS)
    head_done_t = [0] * 34
    rcp_done = {-2: 0, -1: 0}
    obmult_v = {-2: 0, -1: 0}
    ob_dma = {-2: 0, -1: 0}
    lmm_done = {-1: 0}
    t._p4_prev_exp = None
    t._p4_prev_pv = None
    t._p4_prev_acc = None
    t._p4_fin = None

    def emit_fin():
        if t._p4_fin is None:
            return
        st = t._p4_fin
        t._p4_fin = None
        fi = st["fi"]
        lpsum, opsum = st["lpsum"], st["opsum"]
        r.w("tensor", t.sV, st["accLb_v"])
        r.w("tensor", t.sP, st["accPb_p"])
        r.w("tensor", t.sS, rcp_done[fi - 2])
        r.op("tensor", lambda e: e.matmul(
            lpsum, t.onesK[:, :], t.accLb[:, 0:512], start=True, stop=False))
        r.op("tensor", lambda e: e.matmul(
            lpsum, t.onesK[:, :], t.accLb[:, 512:1024], start=False, stop=False))
        r.op("tensor", lambda e: e.matmul(
            lpsum, t.onesK[:, :], t.accPb[:, 0:512], start=False, stop=False))
        r.op("tensor", lambda e: e.matmul(
            lpsum, t.onesK[:, :], t.accPb[:, 512:1024], start=False, stop=True),
            sem=t.sT)
        lmm_done[fi] = r.cnt[t.sT]
        lnt, rcp = t.lnt[fi % 2], t.rcp[fi % 2]
        r.w("scalar", t.sT, lmm_done[fi])
        r.op("scalar", lambda e: e.activation(lnt[:, :], lpsum, AF.Ln))
        r.op("scalar", lambda e: e.activation(
            rcp[:, :], lnt[:, :], AF.Exp, scale=-1.0), sem=t.sS)
        rcp_done[fi] = r.cnt[t.sS]
        ob = t.ob[fi % 2]
        r.w("vector", t.sS, rcp_done[fi])
        r.w("vector", t.sT, st["pv_stop"])
        r.w("vector", t.qS, ob_dma[fi - 2])
        r.op("vector", lambda e: e.tensor_tensor(
            ob[:, :], opsum, rcp[:, :], OP.mult), sem=t.sV)
        obmult_v[fi] = r.cnt[t.sV]
        r.w("sync", t.sV, obmult_v[fi])
        h, q = st["h"], st["q"]
        r.dma("sync", t.otd_d[h * 128:(h + 1) * 128, q * 512:(q + 1) * 512],
              ob[:, :], t.qS)
        ob_dma[fi] = r.cnt[t.qS]

    for h in range(32):
        hb = h % 2
        j = h // 2
        kb, vb, qb = t.kb[hb], t.vb[hb], t.qb[hb]
        if h >= 2:
            r.w("sync", t.sT, head_done_t[h - 2])
        r.dma("sync", kb[:, 0:2048], t.ktg_d[h * 128:(h + 1) * 128, :], t.qS)
        r.dma("sync", kb[:, 2048:4096], t.ktg_d[4096 + h * 128: 4096 + (h + 1) * 128, :], t.qS)
        for hf in range(2):
            src = t.vg_d.rearrange("(r hh tt p) d -> p r hh tt d", p=128, tt=16, hh=32)[
                :, hf, h, :, :]
            r.dma("sync", vb[:, hf * 2048:(hf + 1) * 2048]
                  .rearrange("p (c d) -> p c d", c=16), src, t.qS)
        r.dma("sync", qb[:, :], t.qtd_d[h * 128:(h + 1) * 128, :], t.qS)
        if hb == 0:
            qrb = t.qrb[j % 2]
            r.dma("sync", qrb[:, :], t.qrtd_d[j * 128:(j + 1) * 128, :], t.qS)
        qrb = t.qrb[j % 2]
        krP = t.krA if hb == 0 else t.krB
        kv_load = r.cnt[t.qS]
        r.w("tensor", t.qS, kv_load)
        for q in range(4):
            fi = h * 4 + q
            opsum = t.psQ[fi % 2]
            lpsum = t.psQ[2 + fi % 2]
            exp_done = [0] * 16
            pv_done = [0] * 16
            acc_done = [("V", 0)] * 16

            def emit_S(kk, q=q, kb=kb, qb=qb, krP=krP, qrb=qrb):
                S = t.PB2 if kk % 2 == 0 else t.PB3
                if kk >= 2:
                    r.w("tensor", t.sS, exp_done[kk - 2])
                elif t._p4_prev_exp:
                    r.w("tensor", t.sS, t._p4_prev_exp[kk - 2])
                for cc in range(2):
                    c = 2 * kk + cc
                    r.op("tensor", lambda e, S=S, cc=cc, c=c: e.matmul(
                        S[:, cc * 512:(cc + 1) * 512], kb[:, c * 128:(c + 1) * 128],
                        qb[:, q * 512:(q + 1) * 512], start=True, stop=False))
                    r.op("tensor", lambda e, S=S, cc=cc, c=c: e.matmul(
                        S[:, cc * 512:(cc + 1) * 512], krP[:, c * 128:(c + 1) * 128],
                        qrb[:, q * 512:(q + 1) * 512], start=False, stop=True),
                        sem=(t.sT if cc == 1 else None))
                return r.cnt[t.sT]

            def emit_EPA(kk, s_done, q=q, fi=fi, vb=vb, opsum=opsum,
                         exp_done=exp_done, pv_done=pv_done, acc_done=acc_done):
                S = t.PB2 if kk % 2 == 0 else t.PB3
                pbf = t.pbuf[kk % 4]
                r.w("scalar", t.sT, s_done)

                def wacc(ad):
                    r.w("scalar", t.sP if ad[0] == "P" else t.sV, ad[1])
                if kk >= 4:
                    r.w("scalar", t.sT, pv_done[kk - 4])
                    wacc(acc_done[kk - 4])
                elif t._p4_prev_pv:
                    r.w("scalar", t.sT, t._p4_prev_pv[kk - 4])
                    wacc(t._p4_prev_acc[kk - 4])
                exp_half = [0, 0]
                for cc in range(2):
                    r.op("scalar", lambda e, S=S, pbf=pbf, cc=cc: e.activation(
                        pbf[:, cc * 512:(cc + 1) * 512],
                        S[:, cc * 512:(cc + 1) * 512], AF.Exp), sem=t.sS)
                    exp_half[cc] = r.cnt[t.sS]
                exp_done[kk] = exp_half[1]
                if kk == 0:
                    r.w("tensor", t.sV, obmult_v[fi - 2])
                for cc in range(2):
                    c = 2 * kk + cc
                    r.w("tensor", t.sS, exp_half[cc])
                    r.op("tensor", lambda e, c=c, cc=cc, pbf=pbf, kk=kk: e.matmul(
                        opsum, vb[:, c * 128:(c + 1) * 128],
                        pbf[:, cc * 512:(cc + 1) * 512],
                        start=(kk == 0 and cc == 0), stop=(kk == 15 and cc == 1)),
                        sem=(t.sT if cc == 1 else None))
                pv_done[kk] = r.cnt[t.sT]
                if kk in G_KK:
                    r.w("gpsimd", t.sS, exp_done[kk])
                    if kk == G_KK[0]:
                        r.op("gpsimd", lambda e, pbf=pbf: e.tensor_copy(
                            t.accP[:, :], pbf[:, :]), sem=t.sP)
                    elif kk == G_KK[-1]:
                        r.w("gpsimd", t.sT, lmm_done[fi - 1])
                        r.op("gpsimd", lambda e, pbf=pbf: e.tensor_tensor(
                            t.accPb[:, :], t.accP[:, :], pbf[:, :], OP.add),
                            sem=t.sP)
                    else:
                        r.op("gpsimd", lambda e, pbf=pbf: e.tensor_tensor(
                            t.accP[:, :], t.accP[:, :], pbf[:, :], OP.add),
                            sem=t.sP)
                    acc_done[kk] = ("P", r.cnt[t.sP])
                else:
                    r.w("vector", t.sS, exp_done[kk])
                    if kk == V_KK[0]:
                        r.op("vector", lambda e, pbf=pbf: e.tensor_copy(
                            t.accL[:, :], pbf[:, :]), sem=t.sV)
                    elif kk == V_KK[-1]:
                        r.w("vector", t.sT, lmm_done[fi - 1])
                        r.op("vector", lambda e, pbf=pbf: e.tensor_tensor(
                            t.accLb[:, :], t.accL[:, :], pbf[:, :], OP.add),
                            sem=t.sV)
                    else:
                        r.op("vector", lambda e, pbf=pbf: e.tensor_tensor(
                            t.accL[:, :], t.accL[:, :], pbf[:, :], OP.add),
                            sem=t.sV)
                    acc_done[kk] = ("V", r.cnt[t.sV])

            s_marks = [0] * 16
            for kk in range(16):
                s_marks[kk] = emit_S(kk)
                if kk == 2:
                    emit_fin()
                if kk >= 1:
                    emit_EPA(kk - 1, s_marks[kk - 1])
            emit_EPA(15, s_marks[15])
            t._p4_prev_exp = exp_done
            t._p4_prev_pv = pv_done
            t._p4_prev_acc = acc_done
            t._p4_fin = {
                "fi": fi, "h": h, "q": q, "lpsum": lpsum, "opsum": opsum,
                "accLb_v": acc_done[V_KK[-1]][1],
                "accPb_p": acc_done[G_KK[-1]][1],
                "pv_stop": pv_done[15],
            }
        head_done_t[h] = r.cnt[t.sT]
    emit_fin()
    barrier(r, t)


def p5_half(r, t, half):
    r.w("sync", t.sT)
    r.dma("sync", t.uTh[:, :].rearrange("p (c q) -> p c q", c=32),
          t.otd_d[:, half * 1024:(half + 1) * 1024]
          .rearrange("(c p) q -> p c q", p=128), t.qS)
    barrier(r, t)
    v_done, s_done, t_done = [], [], []
    w_done = []
    for o in range(32):
        pb = o % 2
        wbf = t.wbf3[o % 3]
        if o >= 3:
            r.w("gpsimd", t.sT, t_done[o - 3])
        r.dma("gpsimd", wbf[:, :].rearrange("p (c m) -> p c m", c=32),
              t.wout_d[:, o * 128:(o + 1) * 128].rearrange("(c p) m -> p c m", p=128),
              t.qG)
        r.w("tensor", t.qG)
        if o >= 2:
            r.w("tensor", t.sS, s_done[o - 2])
        for qt in range(2):
            ps = t.psQ[pb * 2 + qt]
            for ci in range(32):
                r.op("tensor", lambda e, ci=ci, qt=qt, ps=ps, wbf=wbf: e.matmul(
                    ps, wbf[:, ci * 128:(ci + 1) * 128],
                    t.uTh[:, ci * 1024 + qt * 512: ci * 1024 + (qt + 1) * 512],
                    start=(ci == 0), stop=(ci == 31)),
                    sem=(t.sT if (ci == 31 and qt == 1) else None))
        t_done.append(r.cnt[t.sT])
        r.dma("sync", t.xres[pb][:, 0:1024],
              t.xt1_d.rearrange("(c p) q -> p c q", p=128)
              [:, o:o + 1, half * 1024:(half + 1) * 1024], t.qS)
        r.w("scalar", t.sT)
        if o >= 2:
            r.w("scalar", t.qS, w_done[o - 2])
            r.w("scalar", t.sV, v_done[o - 2])
        for qt in range(2):
            ps = t.psQ[pb * 2 + qt]
            r.op("scalar", lambda e, ps=ps, qt=qt, pb=pb, o=o: e.activation(
                t.t1[pb][:, qt * 512:(qt + 1) * 512], ps, AF.Identity),
                sem=(t.sS if qt == 1 else None))
        r.w("vector", t.sS)
        r.w("vector", t.qS)
        r.op("vector", lambda e, pb=pb, o=o: e.scalar_tensor_tensor(
            t.t1[pb][:, 0:1024], t.t1[pb][:, 0:1024], t.bout[:, o:o + 1],
            t.xres[pb][:, 0:1024], OP.add, OP.add), sem=t.sV)
        r.w("sync", t.sV)
        r.dma("sync", t.xt0_d.rearrange("(c p) q -> p c q", p=128)
              [:, o:o + 1, half * 1024:(half + 1) * 1024],
              t.t1[pb][:, 0:1024].unsqueeze(1), t.qS)
        w_done.append(r.cnt[t.qS])
        r.op("vector", lambda e, pb=pb, half=half: e.tensor_tensor(
            t.accS[:, half * 1024:(half + 1) * 1024],
            t.accS[:, half * 1024:(half + 1) * 1024],
            t.t1[pb][:, 0:1024], OP.add))
        r.w("scalar", t.sV)
        r.op("scalar", lambda e, pb=pb: e.activation(
            t.dump[:, pb * 1024:(pb + 1) * 1024], t.t1[pb][:, 0:1024], AF.Square),
            sem=t.sS)
        s_done.append(r.cnt[t.sS])
        r.w("vector", t.sS)
        r.op("vector", lambda e, pb=pb, half=half: e.tensor_tensor(
            t.accQ[:, half * 1024:(half + 1) * 1024],
            t.accQ[:, half * 1024:(half + 1) * 1024],
            t.dump[:, pb * 1024:(pb + 1) * 1024], OP.add), sem=t.sV)
        v_done.append(r.cnt[t.sV])


def build(stop_after="p6", dbg=()):
    nc = bass.Bass("TRN2", target_bir_lowering=False)
    t = T()
    t.stage = 0
    t.x0_d = nc.dram_tensor("x0", [TC, D], F32, kind="ExternalInput").ap()
    t.wdq_d = nc.dram_tensor("wdq", [4096, 1536], F32, kind="ExternalInput").ap()
    t.wuq_d = nc.dram_tensor("wuq", [1536, 4096], F32, kind="ExternalInput").ap()
    t.wdkv_d = nc.dram_tensor("wdkv", [4096, 512], F32, kind="ExternalInput").ap()
    t.wuk_d = nc.dram_tensor("wuk", [512, 4096], F32, kind="ExternalInput").ap()
    t.wuv_d = nc.dram_tensor("wuv", [512, 4096], F32, kind="ExternalInput").ap()
    t.wrq_d = nc.dram_tensor("wrq", [1536, 2048], F32, kind="ExternalInput").ap()
    t.wrk_d = nc.dram_tensor("wrk", [4096, 64], F32, kind="ExternalInput").ap()
    t.wout_d = nc.dram_tensor("wout", [4096, 4096], F32, kind="ExternalInput").ap()
    t.bdq_d = nc.dram_tensor("bdq", [128, 12], F32, kind="ExternalInput").ap()
    t.buq_d = nc.dram_tensor("buq", [128, 32], F32, kind="ExternalInput").ap()
    t.bdkv_d = nc.dram_tensor("bdkv", [128, 4], F32, kind="ExternalInput").ap()
    t.buk_d = nc.dram_tensor("buk", [128, 32], F32, kind="ExternalInput").ap()
    t.buv_d = nc.dram_tensor("buv", [1, 4096], F32, kind="ExternalInput").ap()
    t.brq_d = nc.dram_tensor("brq", [128, 16], F32, kind="ExternalInput").ap()
    t.brk_d = nc.dram_tensor("brk", [64, 1], F32, kind="ExternalInput").ap()
    t.bout_d = nc.dram_tensor("bout", [128, 32], F32, kind="ExternalInput").ap()
    t.fq_d = nc.dram_tensor("fq", [4096, 4096], I32, kind="ExternalInput").ap()
    t.fsc_d = nc.dram_tensor("fsc", [4096, 32], F32, kind="ExternalInput").ap()
    t.fzp_d = nc.dram_tensor("fzp", [4096, 32], F32, kind="ExternalInput").ap()
    t.fb_d = nc.dram_tensor("fb", [128, 32], F32, kind="ExternalInput").ap()
    t.y_d = nc.dram_tensor("Y", [TC, D], F32, kind="ExternalOutput").ap()
    t.xt0_d = nc.dram_tensor("XT0", [D, TC], F32).ap()
    t.xt1_d = nc.dram_tensor("XT1", [D, TC], F32).ap()
    t.wtb_d = nc.dram_tensor("WTB", [32, 32, 128, 128], BF16).ap()
    t.qtd_d = nc.dram_tensor("QTD", [D, TC], BF16).ap()
    t.qrtd_d = nc.dram_tensor("QRTD", [2048, TC], BF16).ap()
    t.otd_d = nc.dram_tensor("OTD", [D, TC], BF16).ap()
    t.kti_d = nc.dram_tensor("KTI", [D, TC], BF16).ap()
    t.vi_d = nc.dram_tensor("VI", [65536, 128], BF16).ap()
    t.kri_d = nc.dram_tensor("KRI", [64, TC], BF16).ap()
    t.ktg_d = nc.dram_tensor("KTG", [2 * D, TC], BF16).ap()
    t.vg_d = nc.dram_tensor("VG", [131072, 128], BF16).ap()
    t.krg_d = nc.dram_tensor("KRG", [128, TC], BF16).ap()

    r = Rec(nc)
    t.qS = r.sem("qS")
    t.qA = r.sem("qA")
    t.qG = r.sem("qG")
    t.qOut = r.sem("qOut")
    t.sT = r.sem("sT")
    t.sV = r.sem("sV")
    t.sS = r.sem("sS")
    t.sP = r.sem("sP")
    t.sCC = r.sem("sCC")
    t.bar = r.sem("bar")

    import contextlib
    stack = contextlib.ExitStack()

    t.PB0 = stack.enter_context(nc.psum_tensor("PB0", [128, 1024], F32))
    t.PB1 = stack.enter_context(nc.psum_tensor("PB1", [128, 1024], F32))
    t.PB2 = stack.enter_context(nc.psum_tensor("PB2", [128, 1024], F32))
    t.PB3 = stack.enter_context(nc.psum_tensor("PB3", [128, 1024], F32))
    t.psQ = [t.PB0[:, 0:512], t.PB0[:, 512:1024],
             t.PB1[:, 0:512], t.PB1[:, 512:1024]]

    def sb(name, shape, dtype):
        return stack.enter_context(nc.sbuf_tensor(name, shape, dtype))

    t.uTh = sb("uTh", [128, 32 * HALF], BF16)
    t.dump = sb("dump", [128, 2048], BF16)
    t.stats = [sb("statsA", [128, 8], F32), sb("statsB", [128, 8], F32)]
    t.idb = sb("idb", [128, 128], BF16)
    t.idf = sb("idf", [128, 128], F32)
    t.ones1 = sb("ones1", [1, 128], BF16)
    t.onesK = sb("onesK", [128, 128], BF16)
    t.bd = sb("bd", [128, 1024], BF16)
    t.s_all = sb("s_all", [128, 1024], BF16)
    t.zst = sb("zst", [32, 4096], BF16)
    t.usum = sb("usum", [32, 2048], BF16)
    t.krs = sb("krs", [64, 1024], BF16)
    for nm, shp in (("bdq", 12), ("buq", 32), ("bdkv", 4), ("buk", 32),
                    ("brq", 16), ("bout", 32), ("fb", 32)):
        setattr(t, nm, sb(nm + "_s", [128, shp], F32))
    t.brk = sb("brk_s", [64, 1], F32)
    t.buqs = sb("buqs", [128, 32], F32)
    t.brqs = sb("brqs", [128, 16], F32)
    t.fbh = sb("fbh", [128, 32], F32)
    t.buvb = sb("buvb", [1, 4096], BF16)
    t.ceps = sb("ceps", [128, 1], F32)
    nc.const_aps.register(F32, EPS, t.ceps[:, 0:1])

    KB = 1024
    ov0 = (nc.sbuf_base + 63) // 64 * 64

    def at(name, shape, dtype, off):
        return nc.alloc_sbuf_tensor_at(name, shape, dtype, offset=ov0 + off)

    t.wstr = [at("wstrA", [128, 4096], F32, 0 * KB),
              at("wstrB", [128, 4096], F32, 16 * KB)]
    t.cqT = at("cqT", [128, 12 * HALF], BF16, 16 * KB)
    t.wbf = [at("wbfA", [128, 4096], BF16, 40 * KB),
             at("wbfB", [128, 4096], BF16, 48 * KB)]
    t.wbf3 = t.wbf + [at("wbfC", [128, 4096], BF16, 0 * KB)]
    t.wtsb = [at("wtsbA", [128, 4096], BF16, 56 * KB),
              at("wtsbB", [128, 4096], BF16, 64 * KB),
              at("wtsbC", [128, 4096], BF16, 16 * KB)]
    t.eb = [at("ebA", [128, 1024], BF16, 56 * KB),
            at("ebB", [128, 1024], BF16, 58 * KB)]
    t.ckvT = at("ckvT", [128, 4 * HALF], BF16, 60 * KB)
    t.xres = [at("xresA", [128, 1024], F32, 72 * KB),
              at("xresB", [128, 1024], F32, 76 * KB)]
    t.t1 = [at("t1A", [128, 1024], F32, 80 * KB),
            at("t1B", [128, 1024], F32, 84 * KB)]
    t.accS = at("accS", [128, 2048], F32, 88 * KB)
    t.accQ = at("accQ", [128, 2048], F32, 96 * KB)
    t.mb = at("mb", [128, 2048], F32, 88 * KB)
    t.rb = at("rb", [128, 2048], F32, 96 * KB)
    t.ystg = [at("ystgA", [128, 1024], F32, 0 * KB),
              at("ystgB", [128, 1024], F32, 4 * KB)]
    t.sa_f = at("sa_f", [128, 2048], F32, 0 * KB)
    t.zsb = at("zsb", [128, 1024], BF16, 8 * KB)
    t.kb = [at("kbA", [128, 4096], BF16, 0 * KB),
            at("kbB", [128, 4096], BF16, 8 * KB)]
    t.vb = [at("vbA", [128, 4096], BF16, 16 * KB),
            at("vbB", [128, 4096], BF16, 24 * KB)]
    t.qb = [at("qbA", [128, 2048], BF16, 32 * KB),
            at("qbB", [128, 2048], BF16, 36 * KB)]
    t.qrb = [at("qrbA", [128, 2048], BF16, 40 * KB),
             at("qrbB", [128, 2048], BF16, 44 * KB)]
    t.krA = at("krA", [128, 4096], BF16, 48 * KB)
    t.krB = at("krB", [128, 4096], BF16, 56 * KB)
    t.pbuf = [at("pbufA", [128, 1024], BF16, 64 * KB),
              at("pbufB", [128, 1024], BF16, 66 * KB),
              at("pbufC", [128, 1024], BF16, 68 * KB),
              at("pbufD", [128, 1024], BF16, 70 * KB)]
    t.accL = at("accL", [128, 1024], F32, 72 * KB)
    t.accP = at("accP", [128, 1024], F32, 76 * KB)
    t.accLb = at("accLb", [128, 1024], BF16, 80 * KB)
    t.accPb = at("accPb", [128, 1024], BF16, 82 * KB)
    t.lnt = [at("lntA", [128, 512], F32, 84 * KB),
             at("lntB", [128, 512], F32, 86 * KB)]
    t.rcp = [at("rcpA", [128, 512], F32, 88 * KB),
             at("rcpB", [128, 512], F32, 90 * KB)]
    t.ob = [at("obA", [128, 512], BF16, 92 * KB),
            at("obB", [128, 512], BF16, 93 * KB)]

    t.lnv_done = [0] * 32
    t.pj_vdone = [0] * 32
    t.pj_tdone = [0] * 32
    t.pj_sdone = [0] * 32

    p_init(r, t)
    barrier(r, t)

    def alloc_ACC():
        r.op("gpsimd", lambda e: e.memset(t.accS[:, :], 0.0))
        r.op("gpsimd", lambda e: e.memset(t.accQ[:, :], 0.0), sem=t.sP)
        r.w("vector", t.sP)
        r.w("scalar", t.sP)

    alloc_ACC()

    order = ["p0a", "p1a", "p0b", "p1b", "p2s", "p2h0", "p2h1", "p2c1",
             "p3", "p4", "p5", "p6"]
    upto = order.index(stop_after) if stop_after in order else len(order) - 1

    def phase(name):
        return order.index(name) <= upto

    if phase("p0a"):
        p0_half(r, t, 0)
        usum_half(r, t, 0)
        barrier(r, t)
    if phase("p1a"):
        ffn_half(r, t, 0, 1)
        barrier(r, t)
    if phase("p0b"):
        p0_half(r, t, 1)
        usum_half(r, t, 1)
        barrier(r, t)
    if phase("p1b"):
        ffn_half(r, t, 1, 1)
        barrier(r, t)
    if phase("p2s"):
        ln_stats(r, t)
        barrier(r, t)
    if phase("p2h0"):
        p2_half(r, t, 0)
        p2c_half(r, t, 0)
    if phase("p2h1"):
        p2_half(r, t, 1)
    if phase("p2c1"):
        p2c_half(r, t, 1)
    if phase("p3"):
        for a, b in ((t.kti_d, t.ktg_d), (t.vi_d, t.vg_d), (t.kri_d, t.krg_d)):
            r.op("gpsimd", lambda e, a=a, b=b: e.collective_compute(
                "AllGather", OP.bypass,
                replica_groups=[[0, 1], [2, 3], [4, 5], [6, 7]],
                ins=[a.opt()], outs=[b.opt()]), sem=t.sCC)
    if phase("p4"):
        p4(r, t)
    if phase("p5"):
        alloc_ACC()
        barrier(r, t)
        p5_half(r, t, 0)
        p5_half(r, t, 1)
        barrier(r, t)
    if phase("p6"):
        ln_stats(r, t)
        barrier(r, t)
        for half in range(2):
            ln_apply_half(r, t, half, t.xt0_d)
            barrier(r, t)
            usum_half(r, t, half)
            barrier(r, t)
            ffn_half(r, t, half, 2)
            barrier(r, t)

    dram_dbg = {"x1": t.xt1_d, "x2": t.xt0_d, "qt": t.qtd_d, "qrt": t.qrtd_d,
                "kt": t.kti_d, "kri": t.kri_d, "ot": t.otd_d, "vi": t.vi_d,
                "ktg": t.ktg_d, "krg": t.krg_d, "vg": t.vg_d, "wtb": t.wtb_d,
                "x0t": t.xt0_d}
    sbuf_dbg = {"uTh": t.uTh, "usum": t.usum, "mb": t.mb, "rb": t.rb,
                "stats0": t.stats[0], "stats1": t.stats[1], "accS": t.accS, "accQ": t.accQ,
                "zst": t.zst, "s_all": t.s_all, "cqT": t.cqT, "ckvT": t.ckvT}
    for name in dbg:
        if name in dram_dbg:
            src = dram_dbg[name]
            dbg_out = nc.dram_tensor("DBG_" + name, list(src.shape), src.dtype,
                                     kind="ExternalOutput").ap()
            r.dma("sync", dbg_out, src, t.qOut)
        else:
            h = sbuf_dbg[name]
            ap = h[:, :]
            dbg_out = nc.dram_tensor("DBG_" + name, list(ap.shape), ap.dtype,
                                     kind="ExternalOutput").ap()
            r.dma("sync", dbg_out, ap, t.qOut)
    r.w("sync", t.qOut)
    r.w("sync", t.qS)
    r.w("scalar", t.qA)
    r.w("gpsimd", t.qG)

    with nc.Block() as block:
        @block.sync
        def _(eng):
            for fn in r.steps["sync"]:
                fn(eng)

        @block.gpsimd
        def _(eng):
            for fn in r.steps["gpsimd"]:
                fn(eng)

        @block.vector
        def _(eng):
            for fn in r.steps["vector"]:
                fn(eng)

        @block.scalar
        def _(eng):
            for fn in r.steps["scalar"]:
                fn(eng)

        @block.tensor
        def _(eng):
            for fn in r.steps["tensor"]:
                fn(eng)

    stack.close()
    return nc


def _chunked(b, n):
    return np.ascontiguousarray(b.reshape(n, 128).T.astype(np.float32))


def make_inputs(inputs):
    m = {}
    m["wdq"] = np.ascontiguousarray(inputs["L0_attn_w_dq"], np.float32)
    m["wuq"] = np.ascontiguousarray(inputs["L0_attn_w_uq"], np.float32)
    m["wdkv"] = np.ascontiguousarray(inputs["L0_attn_w_dkv"], np.float32)
    m["wuk"] = np.ascontiguousarray(inputs["L0_attn_w_uk"], np.float32)
    m["wuv"] = np.ascontiguousarray(inputs["L0_attn_w_uv"], np.float32)
    m["wrq"] = np.ascontiguousarray(inputs["L0_attn_w_rq"], np.float32)
    m["wrk"] = np.ascontiguousarray(inputs["L0_attn_w_rk"], np.float32)
    m["wout"] = np.ascontiguousarray(inputs["L0_attn_w_out"], np.float32)
    m["bdq"] = _chunked(np.asarray(inputs["L0_attn_b_dq"]), 12)
    m["buq"] = _chunked(np.asarray(inputs["L0_attn_b_uq"]), 32)
    m["bdkv"] = _chunked(np.asarray(inputs["L0_attn_b_dkv"]), 4)
    m["buk"] = _chunked(np.asarray(inputs["L0_attn_b_uk"]), 32)
    m["buv"] = np.ascontiguousarray(
        np.asarray(inputs["L0_attn_b_uv"], np.float32).reshape(1, 4096))
    m["brq"] = _chunked(np.asarray(inputs["L0_attn_b_rq"]), 16)
    m["brk"] = np.ascontiguousarray(
        np.asarray(inputs["L0_attn_b_rk"], np.float32).reshape(64, 1))
    m["bout"] = _chunked(np.asarray(inputs["L0_attn_b_out"]), 32)
    m["fq"] = np.ascontiguousarray(inputs["L0_ffn_weight_q"], np.int32)
    m["fsc"] = np.ascontiguousarray(inputs["L0_ffn_scale"], np.float32)
    m["fzp"] = np.ascontiguousarray(inputs["L0_ffn_zero_point"], np.float32)
    m["fb"] = _chunked(np.asarray(inputs["L0_ffn_bias"]), 32)
    x = np.asarray(inputs["L0_attn_sequence"], np.float32)
    in_maps = []
    for c in range(8):
        b, hf = c // 2, c % 2
        mm = dict(m)
        mm["x0"] = np.ascontiguousarray(x[b, hf * TC:(hf + 1) * TC, :])
        in_maps.append(mm)
    return in_maps


def kernel(**inputs):
    in_maps = make_inputs(inputs)
    nc = build()
    res = run_bass_kernel_spmd(nc, in_maps, core_ids=list(range(8)))
    out = np.empty((4, 4096, 4096), np.float32)
    for c in range(8):
        out[c // 2, (c % 2) * TC:(c % 2 + 1) * TC, :] = res.results[c]["Y"]
    return out


# version 07
# speedup vs baseline: 1.3687x; 1.0385x over previous
import math
import numpy as np
import ml_dtypes
import concourse.bass as bass
import concourse.mybir as mybir
from concourse.bass_utils import run_bass_kernel_spmd

F32 = mybir.dt.float32
BF16 = mybir.dt.bfloat16
I32 = mybir.dt.int32
AF = mybir.ActivationFunctionType
OP = mybir.AluOpType

ENGINES = ("sync", "gpsimd", "vector", "scalar", "tensor")
SCALER = 1.0 / math.sqrt(192.0)
EPS = 1e-5
D = 4096
TC = 2048
HALF = 1024
NCH = 32


class Rec:

    def __init__(self, nc):
        self.nc = nc
        self.steps = {e: [] for e in ENGINES}
        self.cnt = {}
        self._guards = []

    def sem(self, name):
        g = self.nc.semaphore(name)
        h = g.__enter__()
        self._guards.append(g)
        self.cnt[h] = 0
        return h

    def em(self, eng, fn):
        self.steps[eng].append(fn)

    def w(self, eng, sem, val=None):
        v = self.cnt[sem] if val is None else val
        self.em(eng, lambda e, s=sem, v=v: e.wait_ge(s, v))

    def dma(self, queue, out, in_, sem, transpose=False, slow=False):
        self.cnt[sem] += 16

        def fn(e, out=out, in_=in_, t=transpose, s=sem, sl=slow):
            kw = {}
            if sl:
                kw["allow_slow_non_contiguous"] = True
            if t:
                e.dma_start(out=out, in_=in_, transpose=True, **kw).then_inc(s, 16)
            else:
                e.dma_start(out=out, in_=in_, **kw).then_inc(s, 16)

        self.em(queue, fn)
        return self.cnt[sem]

    def op(self, eng, builder, sem=None, inc=1):
        if sem is not None:
            self.cnt[sem] += inc

        def fn(e, b=builder, s=sem, i=inc):
            ins = b(e)
            if s is not None:
                ins.then_inc(s, i)

        self.em(eng, fn)
        return self.cnt[sem] if sem is not None else None


class T:
    pass


def _alloc_group(nc, specs):
    out = {}
    guards = []
    for attr, name, shape, dtype in specs:
        g = nc.sbuf_tensor(name, shape, dtype)
        h = g.__enter__()
        guards.append(g)
        out[attr] = h
    return out, guards


def _free_group(guards):
    for g in reversed(guards):
        g.__exit__(None, None, None)


def barrier(r, t):
    t.stage += 1
    r.w("sync", t.qS)
    r.w("scalar", t.qA)
    r.w("gpsimd", t.qG)
    r.w("vector", t.qV)
    for e in ENGINES:
        r.op(e, lambda en: en.nop(), sem=t.bar, inc=1)
    v = 5 * t.stage
    assert r.cnt[t.bar] == v
    for e in ENGINES:
        r.w(e, t.bar, v)


def p_init(r, t):
    nc = r.nc
    r.op("gpsimd", lambda e: e.memset(t.ceps[:, :], EPS))
    r.op("gpsimd", lambda e: e.memset(t.idb[:, :], 0.0))
    r.op("gpsimd", lambda e: e.affine_select(
        out=t.idb[:, :], in_=t.idb[:, :], compare_op=OP.not_equal, fill=1.0,
        base=0, pattern=[[-1, 128]], channel_multiplier=1))
    r.op("gpsimd", lambda e: e.memset(t.idf[:, :], 0.0))
    r.op("gpsimd", lambda e: e.affine_select(
        out=t.idf[:, :], in_=t.idf[:, :], compare_op=OP.not_equal, fill=1.0,
        base=0, pattern=[[-1, 128]], channel_multiplier=1))
    r.op("gpsimd", lambda e: e.memset(t.ones1[:, :], 1.0))
    r.op("gpsimd", lambda e: e.memset(t.onesK[:, :], 1.0))
    r.op("gpsimd", lambda e: e.memset(t.bd[:, :], 0.0))
    for g in range(32):
        r.op("gpsimd", lambda e, g=g: e.memset(t.bd[:, g * 32 + g: g * 32 + g + 1], 1.0))
    r.op("gpsimd", lambda e: e.engine_nop(), sem=t.sP)
    for nm in ("bdq", "buq", "bdkv", "buk", "brq", "bout", "fb"):
        r.dma("gpsimd", getattr(t, nm)[:, :], getattr(t, nm + "_d"), t.qG)
    r.dma("gpsimd", t.buvb[:, :], t.buv_d, t.qG)
    r.dma("gpsimd", t.brk[:, :], t.brk_d, t.qG)
    r.dma("gpsimd", t.sa_f[:, 0:1024].rearrange("p (c g) -> p c g", c=32),
          t.fsc_d.rearrange("(c p) g -> p c g", p=128), t.qG)
    r.dma("gpsimd", t.sa_f[:, 1024:2048].rearrange("p (c g) -> p c g", c=32),
          t.fzp_d.rearrange("(c p) g -> p c g", p=128), t.qG)
    r.w("vector", t.qG)
    r.w("gpsimd", t.qG)
    r.op("vector", lambda e: e.tensor_copy(t.s_all[:, :], t.sa_f[:, 0:1024]))
    r.op("vector", lambda e: e.tensor_tensor(
        t.zsb[:, :], t.sa_f[:, 0:1024], t.sa_f[:, 1024:2048], OP.mult), sem=t.sV)
    r.op("vector", lambda e: e.tensor_scalar(
        t.buqs[:, :], t.buq[:, :], SCALER, None, OP.mult))
    r.op("vector", lambda e: e.tensor_scalar(
        t.brqs[:, :], t.brq[:, :], SCALER, None, OP.mult))
    r.op("vector", lambda e: e.tensor_scalar(
        t.fbh[:, :], t.fb[:, :], 0.5, None, OP.mult))
    r.w("tensor", t.sV)
    r.w("tensor", t.sP)
    for o in range(32):
        pb = t.PB2 if (o % 2 == 0) else t.PB3
        dst = pb[0:32, 0:64].bitcast(BF16)
        r.op("tensor", lambda e, o=o, dst=dst: e.transpose(
            dst, t.zsb[:, o * 32:(o + 1) * 32], t.idb[:, :]), sem=t.sT)
        r.w("vector", t.sT)
        r.op("vector", lambda e, o=o, src=dst: e.tensor_copy(
            t.zst[0:32, o * 128:(o + 1) * 128], src), sem=t.sV)
        r.w("tensor", t.sV)


def p0_half(r, t, half):
    v_done, s_done, t_done = [], [], []
    for tc in range(8):
        g = half * 8 + tc
        pb = tc % 2
        xnat = t.wstr[pb]
        u1n = t.wbf[pb]
        st = t.stats[pb]
        if tc >= 2:
            r.w("sync", t.sV, v_done[tc - 2])
            r.w("sync", t.sS, s_done[tc - 2])
            r.w("sync", t.sT, t_done[tc - 2])
        r.dma("sync", xnat[:, :], t.x0_d[g * 128:(g + 1) * 128, :], t.qS)
        r.w("vector", t.qS)
        r.op("vector", lambda e, x=xnat, st=st: e.reduce_sum(
            st[:, 0:1], x[:, :], axis=mybir.AxisListType.X), sem=t.sV)
        r.w("vector", t.sV)
        r.op("vector", lambda e, st=st: e.tensor_scalar(
            st[:, 1:2], st[:, 0:1], 1.0 / D, None, OP.mult), sem=t.sV)
        r.w("vector", t.sV)
        r.w("scalar", t.qS)
        r.op("scalar", lambda e, x=xnat, st=st: e.activation(
            t.dump[:, 0:2048], x[:, 0:2048], AF.Square,
            accum_out=st[:, 2:3]))
        r.op("scalar", lambda e, x=xnat, st=st: e.activation(
            t.dump[:, 0:2048], x[:, 2048:4096], AF.Square,
            accum_out=st[:, 3:4]), sem=t.sS)
        r.w("vector", t.sS)
        r.op("vector", lambda e, st=st: e.tensor_tensor(
            st[:, 4:5], st[:, 2:3], st[:, 3:4], OP.add), sem=t.sV)
        r.w("vector", t.sV)
        r.op("vector", lambda e, st=st: e.tensor_scalar(
            st[:, 4:5], st[:, 4:5], 1.0 / D, None, OP.mult), sem=t.sV)
        r.w("vector", t.sV)
        r.op("vector", lambda e, st=st: e.tensor_tensor(
            st[:, 5:6], st[:, 1:2], st[:, 1:2], OP.mult), sem=t.sV)
        r.w("vector", t.sV)
        r.op("vector", lambda e, st=st: e.tensor_tensor(
            st[:, 4:5], st[:, 4:5], st[:, 5:6], OP.subtract), sem=t.sV)
        r.w("scalar", t.sV)
        r.op("scalar", lambda e, st=st: e.activation(
            st[:, 6:7], st[:, 4:5], AF.Ln, bias=EPS), sem=t.sS)
        r.w("scalar", t.sS)
        r.op("scalar", lambda e, st=st: e.activation(
            st[:, 7:8], st[:, 6:7], AF.Exp, scale=-0.5), sem=t.sS)
        r.w("vector", t.sS)
        r.op("vector", lambda e, x=xnat, u=u1n, st=st: e.tensor_scalar(
            u[:, :], x[:, :], st[:, 1:2], st[:, 7:8], OP.subtract, OP.mult), sem=t.sV)
        uv = r.cnt[t.sV]
        r.w("tensor", t.qS)
        for grp in range(4):
            pbk = t.PB2 if grp % 2 == 0 else t.PB3
            for j in range(8):
                d = grp * 8 + j
                r.op("tensor", lambda e, x=xnat, d=d, pbk=pbk, j=j: e.transpose(
                    pbk[:, j * 128:(j + 1) * 128], x[:, d * 128:(d + 1) * 128],
                    t.idf[:, :]), sem=(t.sT if j == 7 else None))
            r.w("scalar", t.sT)
            r.w("scalar", t.qS)
            r.op("scalar", lambda e, pbk=pbk, pb=pb: e.activation(
                t.t1[pb][:, 0:1024], pbk[:, :], AF.Identity), sem=t.sS)
            r.w("sync", t.sS)
            r.dma("sync", t.xt0_d.rearrange("(c p) q -> p c q", p=128)
                  [:, grp * 8:(grp + 1) * 8, g * 128:(g + 1) * 128],
                  t.t1[pb][:, 0:1024].rearrange("p (c q) -> p c q", c=8), t.qS)
            r.w("tensor", t.sS)
        r.w("tensor", t.sV, uv)
        for grp in range(2):
            pbk = t.PB2 if grp == 0 else t.PB3
            dstall = pbk[:, 0:1024].bitcast(BF16)
            for j in range(16):
                d = grp * 16 + j
                dst = pbk[:, j * 64:(j + 1) * 64].bitcast(BF16)
                r.op("tensor", lambda e, u=u1n, d=d, dst=dst: e.transpose(
                    dst, u[:, d * 128:(d + 1) * 128], t.idb[:, :]),
                    sem=(t.sT if j == 15 else None))
            r.w("vector", t.sT)
            dd = grp * 16
            out = t.uTh[:, :].rearrange("p (c q) -> p c q", c=32)[
                :, dd:dd + 16, tc * 128:(tc + 1) * 128]
            r.op("vector", lambda e, out=out, src=dstall: e.tensor_copy(
                out, src.rearrange("p (c q) -> p c q", c=16)), sem=t.sV)
            r.w("tensor", t.sV)
        v_done.append(r.cnt[t.sV])
        s_done.append(r.cnt[t.sS])
        t_done.append(r.cnt[t.sT])


def usum_half(r, t, half):
    for qt in range(2):
        ps = t.psQ[qt][0:32, :]
        for i in range(32):
            r.op("tensor", lambda e, i=i, qt=qt, ps=ps: e.matmul(
                ps, t.bd[:, i * 32:(i + 1) * 32],
                t.uTh[:, i * 1024 + qt * 512: i * 1024 + (qt + 1) * 512],
                start=(i == 0), stop=(i == 31)),
                sem=(t.sT if i == 31 else None))
        r.w("scalar", t.sT)
        r.op("scalar", lambda e, qt=qt, ps=ps, half=half: e.activation(
            t.usum[0:32, half * 1024 + qt * 512: half * 1024 + (qt + 1) * 512],
            ps[0:32, :], AF.Identity, scale=-1.0), sem=t.sS)
        r.w("tensor", t.sS)


def ffn_half(r, t, half, app):
    build_w = (app == 1 and half == 0)
    v_done, s_done, t_done = [], [], []
    w_done = []
    for o in range(32):
        pb = o % 2
        wt = t.wtsb[pb] if build_w else t.wtsb[o % 3]
        if o >= 2:
            r.w("sync", t.sT, t_done[o - 2])
            r.w("sync", t.sS, s_done[o - 2])
            r.w("sync", t.sV, v_done[o - 2])
            r.w("scalar", t.sT, t_done[o - 2])
            r.w("gpsimd", t.sV, v_done[o - 2])
        if build_w:
            wq = t.wbf[pb]
            r.dma("gpsimd", wq[:, :], t.fq_d[o * 128:(o + 1) * 128, :], t.qG)
            r.w("vector", t.qG)
            r.op("vector", lambda e, wq=wq, o=o: e.tensor_tensor(
                wq[:, :].rearrange("p (g m) -> p g m", g=32), wq[:, :].rearrange("p (g m) -> p g m", g=32),
                t.s_all[:, o * 32:(o + 1) * 32].unsqueeze(2).broadcast_to((128, 32, 128)),
                OP.mult), sem=t.sV)
            r.w("tensor", t.sV)
            for grp in range(2):
                pbk = t.PB2 if grp == 0 else t.PB3
                for j in range(16):
                    i = grp * 16 + j
                    dst = pbk[:, j * 64:(j + 1) * 64].bitcast(BF16)
                    r.op("tensor", lambda e, wq=wq, i=i, dst=dst: e.transpose(
                        dst, wq[:, i * 128:(i + 1) * 128], t.idb[:, :]),
                        sem=(t.sT if j == 15 else None))
                r.w("vector", t.sT)
                r.op("vector", lambda e, wt=wt, grp=grp, pbk=pbk: e.tensor_copy(
                    wt[:, grp * 2048:(grp + 1) * 2048], pbk[:, 0:1024].bitcast(BF16)),
                    sem=t.sV)
                r.w("tensor", t.sV)
            r.w("scalar", t.sV)
            r.dma("scalar", t.wtb_d[o].rearrange("i p m -> p i m"),
                  wt[:, :].rearrange("p (i m) -> p i m", i=32), t.qA)
        else:
            if o >= 3:
                r.w("gpsimd", t.sT, t_done[o - 3])
            r.dma("gpsimd", wt[:, :].rearrange("p (i m) -> p i m", i=32),
                  t.wtb_d[o].rearrange("i p m -> p i m"), t.qG)
            r.w("tensor", t.qG)
        for qt in range(2):
            ps = t.psQ[pb * 2 + qt]
            for i in range(32):
                r.op("tensor", lambda e, i=i, qt=qt, ps=ps, wt=wt: e.matmul(
                    ps, wt[:, i * 128:(i + 1) * 128],
                    t.uTh[:, i * 1024 + qt * 512: i * 1024 + (qt + 1) * 512],
                    start=(i == 0), stop=False))
            r.op("tensor", lambda e, qt=qt, ps=ps, o=o, half=half: e.matmul(
                ps, t.zst[0:32, o * 128:(o + 1) * 128],
                t.usum[0:32, half * 1024 + qt * 512: half * 1024 + (qt + 1) * 512],
                start=False, stop=True), sem=(t.sT if qt == 1 else None))
        t_done.append(r.cnt[t.sT])
        res_src = t.xt0_d
        r.dma("sync", t.xres[pb][:, 0:1024],
              res_src.rearrange("(c p) q -> p c q", p=128)
              [:, o:o + 1, half * 1024:(half + 1) * 1024], t.qS)
        r.w("scalar", t.sT)
        if o >= 2:
            r.w("scalar", t.qS if app == 1 else t.qOut, w_done[o - 2])
            r.w("scalar", t.sV, v_done[o - 2])
        for qt in range(2):
            ps = t.psQ[pb * 2 + qt]
            r.op("scalar", lambda e, qt=qt, ps=ps, o=o, pb=pb: e.activation(
                t.t1[pb][:, qt * 512:(qt + 1) * 512], ps, AF.Identity,
                scale=0.5), sem=(t.sS if qt == 1 else None))
        r.w("vector", t.sS)
        r.w("vector", t.qS)
        r.op("vector", lambda e, pb=pb, o=o: e.scalar_tensor_tensor(
            t.t1[pb][:, 0:1024], t.t1[pb][:, 0:1024], t.fbh[:, o:o + 1],
            t.xres[pb][:, 0:1024], OP.add, OP.add), sem=t.sV)
        r.w("sync", t.sV)
        if app == 1:
            r.dma("sync", t.xt1_d.rearrange("(c p) q -> p c q", p=128)
                  [:, o:o + 1, half * 1024:(half + 1) * 1024],
                  t.t1[pb][:, 0:1024].unsqueeze(1), t.qS)
            w_done.append(r.cnt[t.qS])
            r.op("vector", lambda e, pb=pb, half=half: e.tensor_tensor(
                t.accS[:, half * 1024:(half + 1) * 1024],
                t.accS[:, half * 1024:(half + 1) * 1024],
                t.t1[pb][:, 0:1024], OP.add))
            r.w("scalar", t.sV)
            r.op("scalar", lambda e, pb=pb: e.activation(
                t.dump[:, pb * 1024:(pb + 1) * 1024], t.t1[pb][:, 0:1024],
                AF.Square), sem=t.sS)
            r.w("vector", t.sS)
            r.op("vector", lambda e, pb=pb, half=half: e.tensor_tensor(
                t.accQ[:, half * 1024:(half + 1) * 1024],
                t.accQ[:, half * 1024:(half + 1) * 1024],
                t.dump[:, pb * 1024:(pb + 1) * 1024], OP.add), sem=t.sV)
        else:
            r.w("tensor", t.sV)
            pbk = t.PB2 if pb == 0 else t.PB3
            for j in range(8):
                r.op("tensor", lambda e, pb=pb, j=j, pbk=pbk: e.transpose(
                    pbk[:, j * 128:(j + 1) * 128],
                    t.t1[pb][:, j * 128:(j + 1) * 128], t.idf[:, :]),
                    sem=(t.sT if j == 7 else None))
            t_done[-1] = r.cnt[t.sT]
            r.w("vector", t.sT)
            if o >= 2:
                r.w("vector", t.qOut, w_done[o - 2])
            r.op("vector", lambda e, pb=pb, pbk=pbk: e.tensor_copy(
                t.ystg[pb][:, 0:1024], pbk[:, :]), sem=t.sV)
            r.w("sync", t.sV)
            r.dma("sync", t.y_d.rearrange("(c p) q -> p c q", p=128)
                  [:, half * 8:(half + 1) * 8, o * 128:(o + 1) * 128],
                  t.ystg[pb][:, 0:1024].rearrange("p (c q) -> p c q", c=8),
                  t.qOut)
            w_done.append(r.cnt[t.qOut])
        v_done.append(r.cnt[t.sV])
        s_done.append(r.cnt[t.sS])


def ln_stats(r, t):
    r.op("vector", lambda e: e.tensor_copy(t.dump[:, 0:2048], t.accS[:, :]),
         sem=t.sV)
    r.w("tensor", t.sV)
    for qt in range(4):
        psm = t.psQ[qt % 2]
        r.op("tensor", lambda e, qt=qt, psm=psm: e.matmul(
            psm, t.onesK[:, :], t.dump[:, qt * 512:(qt + 1) * 512],
            start=True, stop=True), sem=t.sT)
        r.w("vector", t.sT)
        r.op("vector", lambda e, qt=qt, psm=psm: e.tensor_scalar(
            t.mb[:, qt * 512:(qt + 1) * 512], psm, 1.0 / D, None, OP.mult),
            sem=t.sV)
        r.w("tensor", t.sV)
    r.op("vector", lambda e: e.tensor_copy(t.dump[:, 0:2048], t.accQ[:, :]),
         sem=t.sV)
    r.w("tensor", t.sV)
    for qt in range(4):
        psv = t.psQ[2 + qt % 2]
        r.op("tensor", lambda e, qt=qt, psv=psv: e.matmul(
            psv, t.onesK[:, :], t.dump[:, qt * 512:(qt + 1) * 512],
            start=True, stop=True), sem=t.sT)
        r.w("vector", t.sT)
        r.op("vector", lambda e, qt=qt, psv=psv: e.tensor_scalar(
            t.rb[:, qt * 512:(qt + 1) * 512], psv, 1.0 / D, None, OP.mult))
        r.op("vector", lambda e, qt=qt: e.tensor_tensor(
            t.t1[0][:, 0:512], t.mb[:, qt * 512:(qt + 1) * 512],
            t.mb[:, qt * 512:(qt + 1) * 512], OP.mult))
        r.op("vector", lambda e, qt=qt: e.tensor_tensor(
            t.rb[:, qt * 512:(qt + 1) * 512], t.rb[:, qt * 512:(qt + 1) * 512],
            t.t1[0][:, 0:512], OP.subtract), sem=t.sV)
        r.w("scalar", t.sV)
        r.op("scalar", lambda e, qt=qt: e.activation(
            t.rb[:, qt * 512:(qt + 1) * 512], t.rb[:, qt * 512:(qt + 1) * 512],
            AF.Ln, bias=EPS))
        r.op("scalar", lambda e, qt=qt: e.activation(
            t.rb[:, qt * 512:(qt + 1) * 512], t.rb[:, qt * 512:(qt + 1) * 512],
            AF.Exp, scale=-0.5), sem=t.sS)
        r.w("tensor", t.sS)


def ln_apply_half(r, t, half, src_dram):
    for dch in range(32):
        pb = dch % 2
        if dch >= 2:
            r.w("sync", t.sV, t.lnv_done[dch - 2])
        r.dma("sync", t.xres[pb][:, 0:1024],
              src_dram.rearrange("(c p) q -> p c q", p=128)
              [:, dch:dch + 1, half * 1024:(half + 1) * 1024], t.qS)
        r.w("vector", t.qS)
        r.op("vector", lambda e, pb=pb, half=half: e.tensor_tensor(
            t.t1[pb][:, 0:1024], t.xres[pb][:, 0:1024],
            t.mb[:, half * 1024:(half + 1) * 1024], OP.subtract))
        r.op("vector", lambda e, pb=pb, dch=dch, half=half: e.tensor_tensor(
            t.uTh[:, dch * 1024:(dch + 1) * 1024], t.t1[pb][:, 0:1024],
            t.rb[:, half * 1024:(half + 1) * 1024], OP.mult), sem=t.sV)
        t.lnv_done[dch] = r.cnt[t.sV]


def proj(r, t, half, w_dram, n_in, n_out, out_kind, bias_ap=None, scale=1.0,
         dst=None, dst_dram=None, name="", dst_half=False):
    barrier(r, t)
    nin_ch = n_in // 128
    nout_ch = n_out // 128
    rhs_buf = t.uTh if name in ("cq", "ckv", "krope") else (
        t.cqT_h[half] if name in ("qup", "qrope") else t.ckvT)
    pj_adone = [0] * nout_ch
    for oc in range(nout_ch):
        pb = oc % 2
        wbf = t.wbf3[oc % 3]
        if oc >= 3:
            r.w("sync", t.sT, t.pj_tdone[oc - 3])
        r.dma("sync", wbf[:, 0:nin_ch * 128].rearrange("p (c m) -> p c m", c=nin_ch),
              w_dram[:, oc * 128:(oc + 1) * 128].rearrange("(c p) m -> p c m", p=128),
              t.qS)
        r.w("tensor", t.qS)
        if oc >= 2:
            r.w("tensor", t.sS, t.pj_sdone[oc - 2])
        for qt in range(2):
            ps = t.psQ[pb * 2 + qt]
            for ci in range(nin_ch):
                r.op("tensor", lambda e, ci=ci, qt=qt, ps=ps, wbf=wbf: e.matmul(
                    ps, wbf[:, ci * 128:(ci + 1) * 128],
                    rhs_buf[:, ci * 1024 + qt * 512: ci * 1024 + (qt + 1) * 512],
                    start=(ci == 0), stop=(ci == nin_ch - 1)),
                    sem=(t.sT if (ci == nin_ch - 1 and qt == 1) else None))
        t.pj_tdone[oc] = r.cnt[t.sT]
        r.w("scalar", t.sT)
        if out_kind == "dram" and oc >= 2:
            r.w("scalar", t.qA, pj_adone[oc - 2])
        for qt in range(2):
            ps = t.psQ[pb * 2 + qt]
            if out_kind == "sbuf":
                out_ap = dst[:, oc * 1024 + qt * 512: oc * 1024 + (qt + 1) * 512]
            else:
                out_ap = t.eb[pb][:, qt * 512:(qt + 1) * 512]
            if bias_ap is not None:
                r.op("scalar", lambda e, ps=ps, out_ap=out_ap, oc=oc: e.activation(
                    out_ap, ps, AF.Identity, scale=scale,
                    bias=bias_ap[:, oc:oc + 1]), sem=(t.sS if qt == 1 else None))
            else:
                r.op("scalar", lambda e, ps=ps, out_ap=out_ap: e.activation(
                    out_ap, ps, AF.Identity, scale=scale),
                    sem=(t.sS if qt == 1 else None))
        t.pj_sdone[oc] = r.cnt[t.sS]
        if out_kind == "dram":
            c0 = 0 if dst_half else half * 1024
            r.dma("scalar", dst_dram.rearrange("(c p) q -> p c q", p=128)
                  [:, oc:oc + 1, c0:c0 + 1024],
                  t.eb[pb][:, 0:1024].unsqueeze(1), t.qA)
            pj_adone[oc] = r.cnt[t.qA]


def p2_krope(r, t, half):
    barrier(r, t)
    r.dma("sync", t.wbf[0][:, 0:2048].rearrange("p (c m) -> p c m", c=32),
          t.wrkB.rearrange("(c p) m -> p c m", p=128), t.qS)
    r.w("tensor", t.qS)
    for qt in range(2):
        ps = t.psQ[qt][0:64, :]
        for ci in range(32):
            r.op("tensor", lambda e, ci=ci, ps=ps, qt=qt: e.matmul(
                ps, t.wbf[0][:, ci * 64:(ci + 1) * 64],
                t.uTh[:, ci * 1024 + qt * 512: ci * 1024 + (qt + 1) * 512],
                start=(ci == 0), stop=(ci == 31)),
                sem=(t.sT if ci == 31 else None))
        r.w("scalar", t.sT)
        r.op("scalar", lambda e, qt=qt, ps=ps: e.activation(
            t.krs[0:64, qt * 512:(qt + 1) * 512], ps, AF.Identity,
            bias=t.brk[:, 0:1]), sem=t.sS)
        r.w("tensor", t.sS)
    r.w("scalar", t.sS)
    r.dma("scalar", t.kri_h[half][:, 0:1024], t.krs[0:64, 0:1024], t.qA)


def p2_vup(r, t, half):
    barrier(r, t)
    va_done = [0] * 16
    vs_done = [0] * 16
    for dvb in range(8):
        pb = dvb % 2
        if dvb >= 2:
            r.w("sync", t.sT, t.pj_tdone[dvb - 2])
        r.dma("sync", t.wbf[pb][:, 0:2048].rearrange("p (c m) -> p c m", c=4),
              t.wuvB[:, dvb * 512:(dvb + 1) * 512].rearrange("(c p) m -> p c m", p=128),
              t.qS)
        r.w("tensor", t.qS)
        if dvb >= 2:
            r.w("tensor", t.sS, t.pj_sdone[dvb - 2])
        for tk in range(8):
            ps = t.psQ[pb * 2 + tk % 2]
            if tk >= 2:
                r.w("tensor", t.sS, vs_done[tk - 2])
            for ci in range(4):
                r.op("tensor", lambda e, ci=ci, tk=tk, ps=ps, pb=pb: e.matmul(
                    ps, t.ckvT[:, ci * 1024 + tk * 128: ci * 1024 + tk * 128 + 128],
                    t.wbf[pb][:, ci * 512:(ci + 1) * 512],
                    start=(ci == 0), stop=False))
            r.op("tensor", lambda e, tk=tk, ps=ps, dvb=dvb: e.matmul(
                ps, t.ones1[:, :],
                t.buvb[:, dvb * 512:(dvb + 1) * 512],
                start=False, stop=True), sem=t.sT)
            r.w("scalar", t.sT)
            if tk >= 2:
                r.w("scalar", t.qA, va_done[tk - 2])
            r.op("scalar", lambda e, ps=ps, pb=pb, tk=tk: e.activation(
                t.eb[pb][:, (tk % 2) * 512:(tk % 2) * 512 + 512], ps, AF.Identity),
                sem=t.sS)
            vs_done[tk] = r.cnt[t.sS]
            t.pj_sdone[dvb] = r.cnt[t.sS]
            out = t.vi_h[half].rearrange("(h tt p) d -> p h tt d", p=128, tt=8)[
                :, dvb * 4:(dvb + 1) * 4, tk, :]
            r.dma("scalar", out,
                  t.eb[pb][:, (tk % 2) * 512:(tk % 2) * 512 + 512]
                  .rearrange("p (h d) -> p h d", h=4), t.qA)
            va_done[tk] = r.cnt[t.qA]
        t.pj_tdone[dvb] = r.cnt[t.sT]


def p2_half(r, t, half):
    ln_apply_half(r, t, half, t.xt1_d)
    barrier(r, t)
    proj(r, t, half, t.wdqB, 4096, 1536, "sbuf", bias_ap=t.bdq,
         dst=t.cqT_h[half], name="cq")
    proj(r, t, half, t.wdkvB, 4096, 512, "sbuf", bias_ap=t.bdkv,
         dst=t.ckvT, name="ckv")
    p2_krope(r, t, half)
    barrier(r, t)
    proj(r, t, half, t.wukB, 512, 4096, "dram", bias_ap=t.buk,
         dst_dram=t.kti_h[half], name="kup", dst_half=True)
    p2_vup(r, t, half)
    barrier(r, t)


def p_precast(r, t):
    chunks = []
    for wf, wb in t.precast_pairs:
        R, C = wf.shape
        n = (R * C) // 128
        fin = wf.rearrange("(p a) c -> p (a c)", p=128)
        fout = wb.rearrange("(p a) c -> p (a c)", p=128)
        for j in range(n // 2048):
            chunks.append((fin[:, j * 2048:(j + 1) * 2048],
                           fout[:, j * 2048:(j + 1) * 2048]))
    marks_out = []
    for i, (src, dst) in enumerate(chunks):
        b = i % 2
        if i >= 2:
            r.w("gpsimd", t.qG, marks_out[i - 2])
        mark_in = r.dma("gpsimd", t.pcst[b][:, :], src, t.qG)
        r.w("gpsimd", t.qG, mark_in)
        r.dma("gpsimd", dst, t.pcst[b][:, :], t.qG)
        marks_out.append(r.cnt[t.qG])


def p3_issue(r, t, half):
    r.w("gpsimd", t.qA)
    for a, b in ((t.kti_h[half], t.ktg_h[half]),
                 (t.vi_h[half], t.vg_h[half]),
                 (t.kri_h[half], t.krg_h[half])):
        r.op("gpsimd", lambda e, a=a, b=b: e.collective_compute(
            "AllGather", OP.bypass,
            replica_groups=[[0, 1], [2, 3], [4, 5], [6, 7]],
            ins=[a.opt()], outs=[b.opt()]), sem=t.sCC)


def p2c_half(r, t, half):
    proj(r, t, half, t.wuqB, 1536, 4096, "dram", bias_ap=t.buqs, scale=SCALER,
         dst_dram=t.qtd_d, name="qup")
    proj(r, t, half, t.wrqB, 1536, 2048, "dram", bias_ap=t.brqs, scale=SCALER,
         dst_dram=t.qrtd_d, name="qrope")
    barrier(r, t)


V_KK = [k for k in range(16) if k % 3 != 2]
G_KK = [k for k in range(16) if k % 3 == 2]


def p4(r, t):
    r.op("gpsimd", lambda e: e.memset(t.krA[:, :], 0.0))
    r.op("gpsimd", lambda e: e.memset(t.krB[:, :], 0.0), sem=t.sP)
    r.w("sync", t.sP)
    r.w("sync", t.sCC, 6)
    for m in range(2):
        for hf in range(2):
            cols = slice(m * 2048 + hf * 1024, m * 2048 + (hf + 1) * 1024)
            src = t.krg_h[hf][64 * m:64 * (m + 1), :]
            r.dma("sync", t.krA[0:64, cols], src, t.qS)
            r.dma("sync", t.krB[64:128, cols], src, t.qS)
    head_done_t = [0] * 34
    rcp_done = {-2: 0, -1: 0}
    obmult_v = {-2: 0, -1: 0}
    ob_dma = {-2: 0, -1: 0}
    lmm_done = {-1: 0}
    t._p4_prev_exp = None
    t._p4_prev_pv = None
    t._p4_prev_acc = None
    t._p4_fin = None

    def emit_fin():
        if t._p4_fin is None:
            return
        st = t._p4_fin
        t._p4_fin = None
        fi = st["fi"]
        lpsum, opsum = st["lpsum"], st["opsum"]
        r.w("tensor", t.sV, st["accLb_v"])
        r.w("tensor", t.sP, st["accPb_p"])
        r.w("tensor", t.sS, rcp_done[fi - 2])
        r.op("tensor", lambda e: e.matmul(
            lpsum, t.onesK[:, :], t.accLb[:, 0:512], start=True, stop=False))
        r.op("tensor", lambda e: e.matmul(
            lpsum, t.onesK[:, :], t.accLb[:, 512:1024], start=False, stop=False))
        r.op("tensor", lambda e: e.matmul(
            lpsum, t.onesK[:, :], t.accPb[:, 0:512], start=False, stop=False))
        r.op("tensor", lambda e: e.matmul(
            lpsum, t.onesK[:, :], t.accPb[:, 512:1024], start=False, stop=True),
            sem=t.sT)
        lmm_done[fi] = r.cnt[t.sT]
        lnt, rcp = t.lnt[fi % 2], t.rcp[fi % 2]
        r.w("scalar", t.sT, lmm_done[fi])
        r.op("scalar", lambda e: e.activation(lnt[:, :], lpsum, AF.Ln))
        r.op("scalar", lambda e: e.activation(
            rcp[:, :], lnt[:, :], AF.Exp, scale=-1.0), sem=t.sS)
        rcp_done[fi] = r.cnt[t.sS]
        ob = t.ob[fi % 2]
        r.w("vector", t.sS, rcp_done[fi])
        r.w("vector", t.sT, st["pv_stop"])
        r.w("vector", t.qS, ob_dma[fi - 2])
        r.op("vector", lambda e: e.tensor_tensor(
            ob[:, :], opsum, rcp[:, :], OP.mult), sem=t.sV)
        obmult_v[fi] = r.cnt[t.sV]
        r.w("sync", t.sV, obmult_v[fi])
        h, q = st["h"], st["q"]
        r.dma("sync", t.otd_d[h * 128:(h + 1) * 128, q * 512:(q + 1) * 512],
              ob[:, :], t.qS)
        ob_dma[fi] = r.cnt[t.qS]

    for h in range(32):
        hb = h % 2
        j = h // 2
        kb, vb, qb = t.kb[hb], t.vb[hb], t.qb[hb]
        if h >= 2:
            r.w("sync", t.sT, head_done_t[h - 2])
        for m in range(2):
            for hf in range(2):
                cols = slice(m * 2048 + hf * 1024, m * 2048 + (hf + 1) * 1024)
                r.dma("sync", kb[:, cols],
                      t.ktg_h[hf][m * 4096 + h * 128: m * 4096 + (h + 1) * 128, :],
                      t.qS)
                vsrc = t.vg_h[hf].rearrange(
                    "(r hh tt p) d -> p r hh tt d", p=128, tt=8, hh=32)[:, m, h, :, :]
                r.dma("sync", vb[:, cols].rearrange("p (c d) -> p c d", c=8),
                      vsrc, t.qS)
        r.dma("sync", qb[:, :], t.qtd_d[h * 128:(h + 1) * 128, :], t.qS)
        if hb == 0:
            qrb = t.qrb[j % 2]
            r.dma("sync", qrb[:, :], t.qrtd_d[j * 128:(j + 1) * 128, :], t.qS)
        qrb = t.qrb[j % 2]
        krP = t.krA if hb == 0 else t.krB
        kv_load = r.cnt[t.qS]
        r.w("tensor", t.qS, kv_load)
        for q in range(4):
            fi = h * 4 + q
            opsum = t.psQ[fi % 2]
            lpsum = t.psQ[2 + fi % 2]
            exp_done = [0] * 16
            pv_done = [0] * 16
            acc_done = [("V", 0)] * 16

            def emit_S(kk, q=q, kb=kb, qb=qb, krP=krP, qrb=qrb):
                S = t.PB2 if kk % 2 == 0 else t.PB3
                if kk >= 2:
                    r.w("tensor", t.sS, exp_done[kk - 2])
                elif t._p4_prev_exp:
                    r.w("tensor", t.sS, t._p4_prev_exp[kk - 2])
                for cc in range(2):
                    c = 2 * kk + cc
                    r.op("tensor", lambda e, S=S, cc=cc, c=c: e.matmul(
                        S[:, cc * 512:(cc + 1) * 512], kb[:, c * 128:(c + 1) * 128],
                        qb[:, q * 512:(q + 1) * 512], start=True, stop=False))
                    r.op("tensor", lambda e, S=S, cc=cc, c=c: e.matmul(
                        S[:, cc * 512:(cc + 1) * 512], krP[:, c * 128:(c + 1) * 128],
                        qrb[:, q * 512:(q + 1) * 512], start=False, stop=True),
                        sem=(t.sT if cc == 1 else None))
                return r.cnt[t.sT]

            def emit_EPA(kk, s_done, q=q, fi=fi, vb=vb, opsum=opsum,
                         exp_done=exp_done, pv_done=pv_done, acc_done=acc_done):
                S = t.PB2 if kk % 2 == 0 else t.PB3
                pbf = t.pbuf[kk % 4]
                r.w("scalar", t.sT, s_done)

                def wacc(ad):
                    r.w("scalar", t.sP if ad[0] == "P" else t.sV, ad[1])
                if kk >= 4:
                    r.w("scalar", t.sT, pv_done[kk - 4])
                    wacc(acc_done[kk - 4])
                elif t._p4_prev_pv:
                    r.w("scalar", t.sT, t._p4_prev_pv[kk - 4])
                    wacc(t._p4_prev_acc[kk - 4])
                exp_half = [0, 0]
                for cc in range(2):
                    r.op("scalar", lambda e, S=S, pbf=pbf, cc=cc: e.activation(
                        pbf[:, cc * 512:(cc + 1) * 512],
                        S[:, cc * 512:(cc + 1) * 512], AF.Exp), sem=t.sS)
                    exp_half[cc] = r.cnt[t.sS]
                exp_done[kk] = exp_half[1]
                if kk == 0:
                    r.w("tensor", t.sV, obmult_v[fi - 2])
                for cc in range(2):
                    c = 2 * kk + cc
                    r.w("tensor", t.sS, exp_half[cc])
                    r.op("tensor", lambda e, c=c, cc=cc, pbf=pbf, kk=kk: e.matmul(
                        opsum, vb[:, c * 128:(c + 1) * 128],
                        pbf[:, cc * 512:(cc + 1) * 512],
                        start=(kk == 0 and cc == 0), stop=(kk == 15 and cc == 1)),
                        sem=(t.sT if cc == 1 else None))
                pv_done[kk] = r.cnt[t.sT]
                if kk in G_KK:
                    r.w("gpsimd", t.sS, exp_done[kk])
                    if kk == G_KK[0]:
                        r.op("gpsimd", lambda e, pbf=pbf: e.tensor_copy(
                            t.accP[:, :], pbf[:, :]), sem=t.sP)
                    elif kk == G_KK[-1]:
                        r.w("gpsimd", t.sT, lmm_done[fi - 1])
                        r.op("gpsimd", lambda e, pbf=pbf: e.tensor_tensor(
                            t.accPb[:, :], t.accP[:, :], pbf[:, :], OP.add),
                            sem=t.sP)
                    else:
                        r.op("gpsimd", lambda e, pbf=pbf: e.tensor_tensor(
                            t.accP[:, :], t.accP[:, :], pbf[:, :], OP.add),
                            sem=t.sP)
                    acc_done[kk] = ("P", r.cnt[t.sP])
                else:
                    r.w("vector", t.sS, exp_done[kk])
                    if kk == V_KK[0]:
                        r.op("vector", lambda e, pbf=pbf: e.tensor_copy(
                            t.accL[:, :], pbf[:, :]), sem=t.sV)
                    elif kk == V_KK[-1]:
                        r.w("vector", t.sT, lmm_done[fi - 1])
                        r.op("vector", lambda e, pbf=pbf: e.tensor_tensor(
                            t.accLb[:, :], t.accL[:, :], pbf[:, :], OP.add),
                            sem=t.sV)
                    else:
                        r.op("vector", lambda e, pbf=pbf: e.tensor_tensor(
                            t.accL[:, :], t.accL[:, :], pbf[:, :], OP.add),
                            sem=t.sV)
                    acc_done[kk] = ("V", r.cnt[t.sV])

            s_marks = [0] * 16
            for kk in range(16):
                s_marks[kk] = emit_S(kk)
                if kk == 2:
                    emit_fin()
                if kk >= 1:
                    emit_EPA(kk - 1, s_marks[kk - 1])
            emit_EPA(15, s_marks[15])
            t._p4_prev_exp = exp_done
            t._p4_prev_pv = pv_done
            t._p4_prev_acc = acc_done
            t._p4_fin = {
                "fi": fi, "h": h, "q": q, "lpsum": lpsum, "opsum": opsum,
                "accLb_v": acc_done[V_KK[-1]][1],
                "accPb_p": acc_done[G_KK[-1]][1],
                "pv_stop": pv_done[15],
            }
        head_done_t[h] = r.cnt[t.sT]
    emit_fin()
    barrier(r, t)


def p5_half(r, t, half):
    r.w("sync", t.sT)
    r.dma("sync", t.uTh[:, :].rearrange("p (c q) -> p c q", c=32),
          t.otd_d[:, half * 1024:(half + 1) * 1024]
          .rearrange("(c p) q -> p c q", p=128), t.qS)
    barrier(r, t)
    v_done, s_done, t_done = [], [], []
    w_done = []
    for o in range(32):
        pb = o % 2
        wbf = t.wbf3[o % 3]
        if o >= 3:
            r.w("gpsimd", t.sT, t_done[o - 3])
        r.dma("gpsimd", wbf[:, :].rearrange("p (c m) -> p c m", c=32),
              t.wout_d[:, o * 128:(o + 1) * 128].rearrange("(c p) m -> p c m", p=128),
              t.qG)
        r.w("tensor", t.qG)
        if o >= 2:
            r.w("tensor", t.sS, s_done[o - 2])
        for qt in range(2):
            ps = t.psQ[pb * 2 + qt]
            for ci in range(32):
                r.op("tensor", lambda e, ci=ci, qt=qt, ps=ps, wbf=wbf: e.matmul(
                    ps, wbf[:, ci * 128:(ci + 1) * 128],
                    t.uTh[:, ci * 1024 + qt * 512: ci * 1024 + (qt + 1) * 512],
                    start=(ci == 0), stop=(ci == 31)),
                    sem=(t.sT if (ci == 31 and qt == 1) else None))
        t_done.append(r.cnt[t.sT])
        r.dma("sync", t.xres[pb][:, 0:1024],
              t.xt1_d.rearrange("(c p) q -> p c q", p=128)
              [:, o:o + 1, half * 1024:(half + 1) * 1024], t.qS)
        r.w("scalar", t.sT)
        if o >= 2:
            r.w("scalar", t.qS, w_done[o - 2])
            r.w("scalar", t.sV, v_done[o - 2])
        for qt in range(2):
            ps = t.psQ[pb * 2 + qt]
            r.op("scalar", lambda e, ps=ps, qt=qt, pb=pb, o=o: e.activation(
                t.t1[pb][:, qt * 512:(qt + 1) * 512], ps, AF.Identity),
                sem=(t.sS if qt == 1 else None))
        r.w("vector", t.sS)
        r.w("vector", t.qS)
        r.op("vector", lambda e, pb=pb, o=o: e.scalar_tensor_tensor(
            t.t1[pb][:, 0:1024], t.t1[pb][:, 0:1024], t.bout[:, o:o + 1],
            t.xres[pb][:, 0:1024], OP.add, OP.add), sem=t.sV)
        r.w("sync", t.sV)
        r.dma("sync", t.xt0_d.rearrange("(c p) q -> p c q", p=128)
              [:, o:o + 1, half * 1024:(half + 1) * 1024],
              t.t1[pb][:, 0:1024].unsqueeze(1), t.qS)
        w_done.append(r.cnt[t.qS])
        r.op("vector", lambda e, pb=pb, half=half: e.tensor_tensor(
            t.accS[:, half * 1024:(half + 1) * 1024],
            t.accS[:, half * 1024:(half + 1) * 1024],
            t.t1[pb][:, 0:1024], OP.add))
        r.w("scalar", t.sV)
        r.op("scalar", lambda e, pb=pb: e.activation(
            t.dump[:, pb * 1024:(pb + 1) * 1024], t.t1[pb][:, 0:1024], AF.Square),
            sem=t.sS)
        s_done.append(r.cnt[t.sS])
        r.w("vector", t.sS)
        r.op("vector", lambda e, pb=pb, half=half: e.tensor_tensor(
            t.accQ[:, half * 1024:(half + 1) * 1024],
            t.accQ[:, half * 1024:(half + 1) * 1024],
            t.dump[:, pb * 1024:(pb + 1) * 1024], OP.add), sem=t.sV)
        v_done.append(r.cnt[t.sV])


def build(stop_after="p6", dbg=()):
    nc = bass.Bass("TRN2", target_bir_lowering=False)
    t = T()
    t.stage = 0
    t.x0_d = nc.dram_tensor("x0", [TC, D], F32, kind="ExternalInput").ap()
    t.wdq_d = nc.dram_tensor("wdq", [4096, 1536], F32, kind="ExternalInput").ap()
    t.wuq_d = nc.dram_tensor("wuq", [1536, 4096], F32, kind="ExternalInput").ap()
    t.wdkv_d = nc.dram_tensor("wdkv", [4096, 512], F32, kind="ExternalInput").ap()
    t.wuk_d = nc.dram_tensor("wuk", [512, 4096], F32, kind="ExternalInput").ap()
    t.wuv_d = nc.dram_tensor("wuv", [512, 4096], F32, kind="ExternalInput").ap()
    t.wrq_d = nc.dram_tensor("wrq", [1536, 2048], F32, kind="ExternalInput").ap()
    t.wrk_d = nc.dram_tensor("wrk", [4096, 64], F32, kind="ExternalInput").ap()
    t.wout_d = nc.dram_tensor("wout", [4096, 4096], F32, kind="ExternalInput").ap()
    t.bdq_d = nc.dram_tensor("bdq", [128, 12], F32, kind="ExternalInput").ap()
    t.buq_d = nc.dram_tensor("buq", [128, 32], F32, kind="ExternalInput").ap()
    t.bdkv_d = nc.dram_tensor("bdkv", [128, 4], F32, kind="ExternalInput").ap()
    t.buk_d = nc.dram_tensor("buk", [128, 32], F32, kind="ExternalInput").ap()
    t.buv_d = nc.dram_tensor("buv", [1, 4096], F32, kind="ExternalInput").ap()
    t.brq_d = nc.dram_tensor("brq", [128, 16], F32, kind="ExternalInput").ap()
    t.brk_d = nc.dram_tensor("brk", [64, 1], F32, kind="ExternalInput").ap()
    t.bout_d = nc.dram_tensor("bout", [128, 32], F32, kind="ExternalInput").ap()
    t.fq_d = nc.dram_tensor("fq", [4096, 4096], I32, kind="ExternalInput").ap()
    t.fsc_d = nc.dram_tensor("fsc", [4096, 32], F32, kind="ExternalInput").ap()
    t.fzp_d = nc.dram_tensor("fzp", [4096, 32], F32, kind="ExternalInput").ap()
    t.fb_d = nc.dram_tensor("fb", [128, 32], F32, kind="ExternalInput").ap()
    t.y_d = nc.dram_tensor("Y", [TC, D], F32, kind="ExternalOutput").ap()
    t.xt0_d = nc.dram_tensor("XT0", [D, TC], F32).ap()
    t.xt1_d = nc.dram_tensor("XT1", [D, TC], F32).ap()
    t.wdqB = nc.dram_tensor("WDQB", [4096, 1536], BF16).ap()
    t.wuqB = nc.dram_tensor("WUQB", [1536, 4096], BF16).ap()
    t.wdkvB = nc.dram_tensor("WDKVB", [4096, 512], BF16).ap()
    t.wukB = nc.dram_tensor("WUKB", [512, 4096], BF16).ap()
    t.wuvB = nc.dram_tensor("WUVB", [512, 4096], BF16).ap()
    t.wrqB = nc.dram_tensor("WRQB", [1536, 2048], BF16).ap()
    t.wrkB = nc.dram_tensor("WRKB", [4096, 64], BF16).ap()
    t.wtb_d = nc.dram_tensor("WTB", [32, 32, 128, 128], BF16).ap()
    t.qtd_d = nc.dram_tensor("QTD", [D, TC], BF16).ap()
    t.qrtd_d = nc.dram_tensor("QRTD", [2048, TC], BF16).ap()
    t.otd_d = nc.dram_tensor("OTD", [D, TC], BF16).ap()
    t.kti_h = [nc.dram_tensor(f"KTI{i}", [D, HALF], BF16).ap() for i in range(2)]
    t.vi_h = [nc.dram_tensor(f"VI{i}", [32768, 128], BF16).ap() for i in range(2)]
    t.kri_h = [nc.dram_tensor(f"KRI{i}", [64, HALF], BF16).ap() for i in range(2)]
    t.ktg_h = [nc.dram_tensor(f"KTG{i}", [2 * D, HALF], BF16).ap() for i in range(2)]
    t.vg_h = [nc.dram_tensor(f"VG{i}", [65536, 128], BF16).ap() for i in range(2)]
    t.krg_h = [nc.dram_tensor(f"KRG{i}", [128, HALF], BF16).ap() for i in range(2)]

    r = Rec(nc)
    t.qS = r.sem("qS")
    t.qA = r.sem("qA")
    t.qG = r.sem("qG")
    t.qV = r.sem("qV")
    t.qOut = r.sem("qOut")
    t.sT = r.sem("sT")
    t.sV = r.sem("sV")
    t.sS = r.sem("sS")
    t.sP = r.sem("sP")
    t.sCC = r.sem("sCC")
    t.bar = r.sem("bar")

    import contextlib
    stack = contextlib.ExitStack()

    t.PB0 = stack.enter_context(nc.psum_tensor("PB0", [128, 1024], F32))
    t.PB1 = stack.enter_context(nc.psum_tensor("PB1", [128, 1024], F32))
    t.PB2 = stack.enter_context(nc.psum_tensor("PB2", [128, 1024], F32))
    t.PB3 = stack.enter_context(nc.psum_tensor("PB3", [128, 1024], F32))
    t.psQ = [t.PB0[:, 0:512], t.PB0[:, 512:1024],
             t.PB1[:, 0:512], t.PB1[:, 512:1024]]

    def sb(name, shape, dtype):
        return stack.enter_context(nc.sbuf_tensor(name, shape, dtype))

    t.uTh = sb("uTh", [128, 32 * HALF], BF16)
    t.dump = sb("dump", [128, 2048], BF16)
    t.stats = [sb("statsA", [128, 8], F32), sb("statsB", [128, 8], F32)]
    t.idb = sb("idb", [128, 128], BF16)
    t.idf = sb("idf", [128, 128], F32)
    t.ones1 = sb("ones1", [1, 128], BF16)
    t.onesK = sb("onesK", [128, 128], BF16)
    t.bd = sb("bd", [128, 1024], BF16)
    t.s_all = sb("s_all", [128, 1024], BF16)
    t.zst = sb("zst", [32, 4096], BF16)
    t.usum = sb("usum", [32, 2048], BF16)
    t.krs = sb("krs", [64, 1024], BF16)
    for nm, shp in (("bdq", 12), ("buq", 32), ("bdkv", 4), ("buk", 32),
                    ("brq", 16), ("bout", 32), ("fb", 32)):
        setattr(t, nm, sb(nm + "_s", [128, shp], F32))
    t.brk = sb("brk_s", [64, 1], F32)
    t.buqs = sb("buqs", [128, 32], F32)
    t.brqs = sb("brqs", [128, 16], F32)
    t.fbh = sb("fbh", [128, 32], F32)
    t.buvb = sb("buvb", [1, 4096], BF16)
    t.ceps = sb("ceps", [128, 1], F32)
    nc.const_aps.register(F32, EPS, t.ceps[:, 0:1])

    KB = 1024
    ov0 = (nc.sbuf_base + 63) // 64 * 64

    def at(name, shape, dtype, off):
        return nc.alloc_sbuf_tensor_at(name, shape, dtype, offset=ov0 + off)

    t.wstr = [at("wstrA", [128, 4096], F32, 0 * KB),
              at("wstrB", [128, 4096], F32, 16 * KB)]
    t.cqT_h = [at("cqT0", [128, 12 * HALF], BF16, 16 * KB),
               at("cqT1", [128, 12 * HALF], BF16, 64 * KB)]
    t.pcst = [at("pcstA", [128, 2048], BF16, 32 * KB),
              at("pcstB", [128, 2048], BF16, 36 * KB)]
    t.wbf = [at("wbfA", [128, 4096], BF16, 40 * KB),
             at("wbfB", [128, 4096], BF16, 48 * KB)]
    t.wbf3 = t.wbf + [at("wbfC", [128, 4096], BF16, 0 * KB)]
    t.wtsb = [at("wtsbA", [128, 4096], BF16, 56 * KB),
              at("wtsbB", [128, 4096], BF16, 64 * KB),
              at("wtsbC", [128, 4096], BF16, 16 * KB)]
    t.eb = [at("ebA", [128, 1024], BF16, 56 * KB),
            at("ebB", [128, 1024], BF16, 58 * KB)]
    t.ckvT = at("ckvT", [128, 4 * HALF], BF16, 60 * KB)
    t.xres = [at("xresA", [128, 1024], F32, 72 * KB),
              at("xresB", [128, 1024], F32, 76 * KB)]
    t.t1 = [at("t1A", [128, 1024], F32, 80 * KB),
            at("t1B", [128, 1024], F32, 84 * KB)]
    t.accS = at("accS", [128, 2048], F32, 88 * KB)
    t.accQ = at("accQ", [128, 2048], F32, 96 * KB)
    t.mb = at("mb", [128, 2048], F32, 88 * KB)
    t.rb = at("rb", [128, 2048], F32, 96 * KB)
    t.ystg = [at("ystgA", [128, 1024], F32, 0 * KB),
              at("ystgB", [128, 1024], F32, 4 * KB)]
    t.sa_f = at("sa_f", [128, 2048], F32, 0 * KB)
    t.zsb = at("zsb", [128, 1024], BF16, 8 * KB)
    t.kb = [at("kbA", [128, 4096], BF16, 0 * KB),
            at("kbB", [128, 4096], BF16, 8 * KB)]
    t.vb = [at("vbA", [128, 4096], BF16, 16 * KB),
            at("vbB", [128, 4096], BF16, 24 * KB)]
    t.qb = [at("qbA", [128, 2048], BF16, 32 * KB),
            at("qbB", [128, 2048], BF16, 36 * KB)]
    t.qrb = [at("qrbA", [128, 2048], BF16, 40 * KB),
             at("qrbB", [128, 2048], BF16, 44 * KB)]
    t.krA = at("krA", [128, 4096], BF16, 48 * KB)
    t.krB = at("krB", [128, 4096], BF16, 56 * KB)
    t.pbuf = [at("pbufA", [128, 1024], BF16, 64 * KB),
              at("pbufB", [128, 1024], BF16, 66 * KB),
              at("pbufC", [128, 1024], BF16, 68 * KB),
              at("pbufD", [128, 1024], BF16, 70 * KB)]
    t.accL = at("accL", [128, 1024], F32, 72 * KB)
    t.accP = at("accP", [128, 1024], F32, 76 * KB)
    t.accLb = at("accLb", [128, 1024], BF16, 80 * KB)
    t.accPb = at("accPb", [128, 1024], BF16, 82 * KB)
    t.lnt = [at("lntA", [128, 512], F32, 84 * KB),
             at("lntB", [128, 512], F32, 86 * KB)]
    t.rcp = [at("rcpA", [128, 512], F32, 88 * KB),
             at("rcpB", [128, 512], F32, 90 * KB)]
    t.ob = [at("obA", [128, 512], BF16, 92 * KB),
            at("obB", [128, 512], BF16, 93 * KB)]

    t.lnv_done = [0] * 32
    t.pj_vdone = [0] * 32
    t.pj_tdone = [0] * 32
    t.pj_sdone = [0] * 32

    p_init(r, t)
    barrier(r, t)

    def alloc_ACC():
        r.op("gpsimd", lambda e: e.memset(t.accS[:, :], 0.0))
        r.op("gpsimd", lambda e: e.memset(t.accQ[:, :], 0.0), sem=t.sP)
        r.w("vector", t.sP)
        r.w("scalar", t.sP)

    alloc_ACC()

    t.precast_pairs = [(t.wdq_d, t.wdqB), (t.wuq_d, t.wuqB),
                       (t.wdkv_d, t.wdkvB), (t.wuk_d, t.wukB),
                       (t.wuv_d, t.wuvB), (t.wrq_d, t.wrqB),
                       (t.wrk_d, t.wrkB)]
    order = ["p0a", "p1a", "p0b", "p1b", "p2s", "p2h0", "p2h1", "p2c1",
             "p4", "p5", "p6"]
    upto = order.index(stop_after) if stop_after in order else len(order) - 1

    def phase(name):
        return order.index(name) <= upto

    if phase("p0a"):
        p0_half(r, t, 0)
        usum_half(r, t, 0)
        barrier(r, t)
    if phase("p1a"):
        ffn_half(r, t, 0, 1)
        barrier(r, t)
        p_precast(r, t)
    if phase("p0b"):
        p0_half(r, t, 1)
        usum_half(r, t, 1)
        barrier(r, t)
    if phase("p1b"):
        ffn_half(r, t, 1, 1)
        barrier(r, t)
    if phase("p2s"):
        ln_stats(r, t)
        barrier(r, t)
    if phase("p2h0"):
        p2_half(r, t, 0)
        p3_issue(r, t, 0)
    if phase("p2h1"):
        p2_half(r, t, 1)
        p3_issue(r, t, 1)
    if phase("p2c1"):
        p2c_half(r, t, 0)
        p2c_half(r, t, 1)
    if phase("p4"):
        p4(r, t)
    if phase("p5"):
        alloc_ACC()
        barrier(r, t)
        p5_half(r, t, 0)
        p5_half(r, t, 1)
        barrier(r, t)
    if phase("p6"):
        ln_stats(r, t)
        barrier(r, t)
        for half in range(2):
            ln_apply_half(r, t, half, t.xt0_d)
            barrier(r, t)
            usum_half(r, t, half)
            barrier(r, t)
            ffn_half(r, t, half, 2)
            barrier(r, t)

    dram_dbg = {"x1": t.xt1_d, "x2": t.xt0_d, "qt": t.qtd_d, "qrt": t.qrtd_d,
                "kt0": t.kti_h[0], "kri0": t.kri_h[0], "ot": t.otd_d,
                "vi0": t.vi_h[0], "ktg0": t.ktg_h[0], "krg0": t.krg_h[0],
                "vg0": t.vg_h[0], "wtb": t.wtb_d, "x0t": t.xt0_d}
    sbuf_dbg = {"uTh": t.uTh, "usum": t.usum, "mb": t.mb, "rb": t.rb,
                "stats0": t.stats[0], "stats1": t.stats[1], "accS": t.accS, "accQ": t.accQ,
                "zst": t.zst, "s_all": t.s_all, "cqT": t.cqT_h[0], "ckvT": t.ckvT}
    for name in dbg:
        if name in dram_dbg:
            src = dram_dbg[name]
            dbg_out = nc.dram_tensor("DBG_" + name, list(src.shape), src.dtype,
                                     kind="ExternalOutput").ap()
            r.dma("sync", dbg_out, src, t.qOut)
        else:
            h = sbuf_dbg[name]
            ap = h[:, :]
            dbg_out = nc.dram_tensor("DBG_" + name, list(ap.shape), ap.dtype,
                                     kind="ExternalOutput").ap()
            r.dma("sync", dbg_out, ap, t.qOut)
    r.w("sync", t.qOut)
    r.w("sync", t.qS)
    r.w("scalar", t.qA)
    r.w("gpsimd", t.qG)

    with nc.Block() as block:
        @block.sync
        def _(eng):
            for fn in r.steps["sync"]:
                fn(eng)

        @block.gpsimd
        def _(eng):
            for fn in r.steps["gpsimd"]:
                fn(eng)

        @block.vector
        def _(eng):
            for fn in r.steps["vector"]:
                fn(eng)

        @block.scalar
        def _(eng):
            for fn in r.steps["scalar"]:
                fn(eng)

        @block.tensor
        def _(eng):
            for fn in r.steps["tensor"]:
                fn(eng)

    stack.close()
    return nc


def _chunked(b, n):
    return np.ascontiguousarray(b.reshape(n, 128).T.astype(np.float32))


def make_inputs(inputs):
    m = {}
    m["wdq"] = np.ascontiguousarray(inputs["L0_attn_w_dq"], np.float32)
    m["wuq"] = np.ascontiguousarray(inputs["L0_attn_w_uq"], np.float32)
    m["wdkv"] = np.ascontiguousarray(inputs["L0_attn_w_dkv"], np.float32)
    m["wuk"] = np.ascontiguousarray(inputs["L0_attn_w_uk"], np.float32)
    m["wuv"] = np.ascontiguousarray(inputs["L0_attn_w_uv"], np.float32)
    m["wrq"] = np.ascontiguousarray(inputs["L0_attn_w_rq"], np.float32)
    m["wrk"] = np.ascontiguousarray(inputs["L0_attn_w_rk"], np.float32)
    m["wout"] = np.ascontiguousarray(inputs["L0_attn_w_out"], np.float32)
    m["bdq"] = _chunked(np.asarray(inputs["L0_attn_b_dq"]), 12)
    m["buq"] = _chunked(np.asarray(inputs["L0_attn_b_uq"]), 32)
    m["bdkv"] = _chunked(np.asarray(inputs["L0_attn_b_dkv"]), 4)
    m["buk"] = _chunked(np.asarray(inputs["L0_attn_b_uk"]), 32)
    m["buv"] = np.ascontiguousarray(
        np.asarray(inputs["L0_attn_b_uv"], np.float32).reshape(1, 4096))
    m["brq"] = _chunked(np.asarray(inputs["L0_attn_b_rq"]), 16)
    m["brk"] = np.ascontiguousarray(
        np.asarray(inputs["L0_attn_b_rk"], np.float32).reshape(64, 1))
    m["bout"] = _chunked(np.asarray(inputs["L0_attn_b_out"]), 32)
    m["fq"] = np.ascontiguousarray(inputs["L0_ffn_weight_q"], np.int32)
    m["fsc"] = np.ascontiguousarray(inputs["L0_ffn_scale"], np.float32)
    m["fzp"] = np.ascontiguousarray(inputs["L0_ffn_zero_point"], np.float32)
    m["fb"] = _chunked(np.asarray(inputs["L0_ffn_bias"]), 32)
    x = np.asarray(inputs["L0_attn_sequence"], np.float32)
    in_maps = []
    for c in range(8):
        b, hf = c // 2, c % 2
        mm = dict(m)
        mm["x0"] = np.ascontiguousarray(x[b, hf * TC:(hf + 1) * TC, :])
        in_maps.append(mm)
    return in_maps


def kernel(**inputs):
    in_maps = make_inputs(inputs)
    nc = build()
    res = run_bass_kernel_spmd(nc, in_maps, core_ids=list(range(8)))
    out = np.empty((4, 4096, 4096), np.float32)
    for c in range(8):
        out[c // 2, (c % 2) * TC:(c % 2 + 1) * TC, :] = res.results[c]["Y"]
    return out
